# version 01
# baseline (speedup 1.0000x reference)
import base64
import math
import zlib
from contextlib import ExitStack

import numpy as np
import ml_dtypes
import concourse.bass as bass
import concourse.mybir as mybir
from concourse.bass_utils import run_bass_kernel_spmd

F32 = mybir.dt.float32
BF16 = mybir.dt.bfloat16
AF = mybir.ActivationFunctionType
ALU = mybir.AluOpType

B, T, D = 2, 4096, 4096
NTT, TTW = 8, 512
NFC = 32
NH_L, NKV_L = 8, 2
NQT = 32
ISQ = 1.0 / math.sqrt(128.0)
EPS = 1e-5
LORA = 4.0
PHASES = 3
DBG = False

_COS_NCORR = 2168
_SIN_NCORR = 3087
_COS_CORR_B64 = "eNoV13VcVE0XB/DZXbpRwqY7VWw5Y7cIFirYYNdrt4iiYhAqiqAigoqB3Y+cMbALFbsRsQMTUPT9+cf3c3dvzMw5c2burp5GiPZaIRbAYn0hrhgIccZQiBXGQhiYCGH6jxnOmQthbSHEEuhpKURbK5yDKtZCGFYRQq+qEPq2QqyGO6C1E8IYRDUhQqA3LATD6kKcgqSaQvSpJUQq3IF7EFpbiAKwdxAiDtZCL0ch1sBJsHISIsxZiBQXIZJdhejqJsRVdyFiPTA2T4wfkuE2LPMSoq+3EBfgCiT7CHEO+vgK8QDa+GFckA/3IClAiBaBQrSCu/VwD1gGYRwN0D8sbIg2oADaNEZfTYTo0UyI+aDXHO3CArgLXYKFqEpCtITWUohVsA7athBiKRi2FOINVGslRA2o2hp5hbA2QvSDZEiDQqjZFufgEjyAJe0QMxS0R447CpEAV6EQbDsJEQlpnXE/1OoiRHc4B926CrEcbsK6EFzrhiPk/xOKZ7sjVlgPF6BdT+QXroQjt30wt32RfwiFRXDy3/d+mFeIg86ROA7EGMFoEO4bjJzCejgDVkOQL9gQJcRFqBqNPmENXIMewzFOqDMCeYL+sGEk4huF9kYLEQ82YzBuWADdxgqRCOnjUGPj8RksJwhh9j8hbkCXiWhjkhDhkzFWOAN6U5BLMJ6KGoMEyAD9aVgD04U4D2EzhDgLF2DjTMQPS2bjM/ScI8Stucg16Obh+RghnkDV+fgei1qAmgsRC0TAI7CNwzPwfDFysQTtgH48agJugMlS5AYcwGMZavef5aj7FYgTVsENqJaA5yEiETHDPaiSjHqFARC+EmNYJYQFPIFrq4UwXyPEIIiDLmtxHhxSsf7AcJ0QA2ETPP33OQ1rEZ6BQzraWo/PUGcD8r4R92UIcRmsNgnRAQZAAURlotY2IyewIluI2luwniF2K9bgNtRODsYJN+HeDszVTuQLzkLLXcgz3IHHYJiLmoOb4LAbNQT9IQHWwcA9uB9C92Le4DI8g777UOf7hXA8gJqBxxB3EGM/JIQzmB5GjPAceh/B/dDiKPJ7TIjq/2E9n0C/kABr8oDRPlRX+AydT2LdQsYpjOs04oYBZ9AX2OUjJ2fRJ5idw3hhAyScR03CSbgChhewb0E+3IfrF1GPlxAXLL2MsUGNK6gPsLqKmoOlYH8N8wb9ClDDMPgG1v1NnIPMW9iHCoVwuo2YIBau3cUaBLP7qFe4Df0fYj4fobYhGWweYxxQ4ylyBuFQ/RnaAWdYAmehCIye4xy0LhLi9AshXsO1l5jTEuwRUATPXqHOXiPON2jvLdoD7Xs8A5mw7APeJR/xDOT8O35C3mDRZ8QNTqXIJzwBwy/I91fMJyyENfAU6nwTIgZOQuJ3jAHif+BzGcZYjrhgE1yGFhXI1y+MCVKhEB5And+IoRLX/mAdwRBYIDRiEWzUaISJViPWwFMYqNOIc3APauppREdYrK8R8bDYQCMWGGpEZyONOAU3oLuxRhRCWxON6A0psAbOwDlIM9WITDONyIIiOGmuETYWGtEPrkCaJfoFGyuNaA294AW8AwdrfIeB4FdVI8Kgo41GPIQ+thphYIf74EU1jdCrrhEuMBAsaqAtWACb4RQMrakR6bUwBlhcWyNO10EcThgnRDhrRE8XjXgOL8DMFe1DBNwCezeNcIaBkAnV3HGES9DKU4N3L77DJWjjpRE9IBlugpm3RnSATXAaLsENSPDRCA9f3Afr4B6E+2lEFX+NuAuaAMQPd/4dAzViFFSph8+wHDT10Sbcga5BGhEJCZACvRtoxHpY1hB9gnEj3NMY8w02TTAvcB3sm2qEYzONaAG9YR1cB+vmyDv0BodgjbgMV8Ga0C7cADOJHEM6GLdArUBcS424D8/AsBWO4NoaR9C2Rb4hG6q20wir9pgvSIdr8ALMOuAzPIQiaN0ROYeWnTB/ULWzRkRBNth21Yhi0A9BH9ADLLvhOlQNwz1wB7K64zM8g4IeiLcnnoP2vTQiBrJ7I65wjejbB/mEzZAFhn1xBLMIjBWKwToSdQBD/x37o5ah5QDU00DU3CCNqAVLBmtEKpyDIggZgnmARUMxb7ARcuE5rIhCDqKxhmAF6A3Ds1BtuEa0gwVwD+6PQIwj0e8ojciHK6Px43WMRpjCKTCcoBE1/odaglvgOFEjFsIpWDoJccGQycjZVHyG82A/DfM7Hdehw0yNGATzYQM8m4W2ZyOfUANqQ0voNwc1BongOBf1Cv1h8DzUbIxGPIJa87EWwCAW9Qr9YA3YLUDOYT3sAMOFGhEK3WEtZMch3sUacQFOx6PPpcgRvIBBy3AdbkCP5ZhjqLoCOU9AXcNCWASX4Do8A12iRnSCHtAfBkEm6JLQD5yEi9A/WSNWwnUwX4n+YMMq5GI17ocsOAO2KdgDoeca9AMb1mJdgE0q5gbaQ2+4BXfAZR0+w6g0jUiCNCiEmunIG8SC2XrMA1zfgPqCHhvRH2gykEvYskkjzoI2EzUEyZuRryw8l4374BwUQ+0tyDMUw0vQbsUYYUUOjnAN/LbjOujvQMw7UQNQdxfqCdrkIrbdmO892APgJrwA3V7EBtfh3H48DxcPoN2D6B+iDyEPUHgYcwXtjiA/cBGeQ/WjiAmqH8MaOoG48lDDjOfhBdxXiOUk5hLs4A50PIV5g25wBgxOYy7gJrwAlzOoQeiQj7yC9VmMBV6cw/E89qILOMIZaHUR8wI1L2FM8BAGX9YItyuIHwyuog7BBiyuacQAaFegEY9B/wZyeRPr7BbqolAj6kAmXADNbdQ0nAG7O8g/dIIn0OUe1jCsg8f/Pj/AmniIeX6EdySkQovHqEFYC+nQ54lGDAPjp8gFRMANuAfXnmEMz1ELEFOEdiEbzsMTMHyBNqBFsUaseom5gmJILEEMr7CfQRG0eK0Rq99oxG0weIccQDhchdsg3qNvCIWVsO6fD5h3iPqIvHxGv1C7FLmA/tDuC/IJ4V+xB0L8N9QrPPmO3IHeD+wvcAPMfiJOWAvZcA8sy3AdEuA2iArMwy/MGSyvRFxQAif/oJ6g5l/kCaKEVmyEl2Cu0YoIiNdqxRZw1WlFD6ipj88GWrHQUCu6GOEcLIeaxlpxE3qb4LqpVuRDFXM8D6ugxFIrzKy0oh0MgGGwELbAbahhjf7gBmyposX/f4wBYmArDLPB2GCJrVb0tNMKK3ut8ICOEFpNK0yra8VqOAvRNdAvuNVEv7C5NtqAc/+OdbQiygHXHLXC2QnPQxG4O2tFe1gIK2ADVHHViiTIhhewyE0rlsEDsHLXijTIgtPg5qEVa6HIE+16oV3oDtbeWmEDQ2ABnIQ4H62w90XMfogdevmjTzgbgDFCv0CtaFEX7cNZeAmG9bRifZBWXIB2DbRCNNQKHVyGR/80Qr4aa8UtiGuqFV2boS14Cl2ba0UfKICnUC0YcwfrwJm0YiAkQCpYSVyD+2DfQivCYT2chqyWiKc18gIOkAUXIaoNYmyLeYXVkAoloNcOuYL77bXiCcR30IqVsB6yIbyjVsyHOGjVCfmB9V20QtsV44Fb8ApqhaAveAG3uuF7qFbYheEILSAMtsJFCO2OnEGtHloxApbDObgF2p5a0bY3xgd34GQ46hQM+qIN2Ai2/VC/MByuQxFsjkCeInEO+vXHXEI0JMAquAMvBiAvA7WiOkQPQp6h+2Ct6AvaIah9yIL70Hko6hmewoAotAsF0aj7YTg/XCtOwcsRqMmRuAZboADugOUorXCBTjBktFacgXMwfIxWxEL1sZg/WAnR45Bb2ALa8RgjRE9AzYHt/xDTRK2IhPWwDVwm4TuMhRXQYhrWBGRNxzyA9Qz0MxP3zEK9gd5s5B305mIs8Ar05qH2YQVExmhFBuRD0nz0EYvYQH+BVgyGGJgPq+AaVF2IuoIWccgNDFyEfMAjKAHtYtQMZID5Eq24Ci/BJh59wXzQLUX8MB9WwtZleB5EAmobouEk2CahLxgGi5O14iFYrkSfYLsKuYBQiIYM2AILV2tFOvinYE3BBchYi7zC6VSs43XYw6BuGvKUjnGsR92B5QbUI0TDMLgGazdqxfkMxLEJMYF5Ju6DFtAaOsFZsNqMWgcXGAqJYJ2FeQVdNvYdWA3hW1D/0Hkr+oZ40G3XChOoCn1gG9wAtx2IBV6C2U7ULURAf1gFV+EaGO/CuGBhLuKD4r1aEbIP++9+zCPYHMA4D2rFhMPYq+Ay3AG3I6jDYxgvGBzHu+E/XD+BOoK3eagR1gpH6AZ9YSAYKMQG6yALrsKWk9gTTqEWoNNp5OcM2gIX0J3Duwe2QOvzOH8B44NX4HYR7UIMXIEH4H4JMUExrLyMc+B2BXmBxxB1FX3DKbgFfa7hCJrr2A+gG9yH6gVoC7pBxg3MEbyGbjeRP3C8hbwVIjeQBta3sd9CDFyHqDvYfyDuLt4ncBHO3EPuIOS+VmyCAQ+04hLcfIh98RHqChZBMcQ/xvifILdPteIuGD3DZyiBFc8xT9CzSCtyQPcCezisBZdirUiBTVDvJdqEYZAIr8C4BDmEEbAI7kPMK+zTcA5MX6NO3mA80A8GgftbrDNYCIthJWTALXgKz+DVv2vvMG44D7Hv0SYM/Ih8//MJcw23wPoz1iR4lGIcsBiywOEL3nkQAnehGMK/4h7IgbRvqG249h3tgN0PzBP0gzVQAot+4lwZYipH7ivwfoeBv7Cv/kab0K9SK5bCmj+YQ6ETj+EFDNXoxErIhsswVKsTW2EbaHU6kQDxejqxAlLgJhRDf32dSISBBrgXHAx1whWGw3wwN9IJE2OdOAXn4TZsMEHb0MYURxhophOF8BS2meMeGG6hEx5WuBe2wS14Da7WOA/bIAdqV0Gf0K0qxgUjbHC0xb1gY6cTzvY60Rp6gXE1HKvrxDXoVgNjq6kT7Wuhvdo60QWWgXsdjAkeg5ED2oMFkACRjrj2j5NOvHNGPlx04h6UwGVXnbBzw5gg2wPj8sR1OO+F89464Qlh0BsSfHTila9O6PvphD0kwybYCmYB6AcugEGgTjyC6Lo6sRb06iHu+jpRAIsa6IRVQ4wdDBvhOyyBRKjWWCdCoRjcm+hEHzBoinn4p7lO1IJuEAWrIRtu/TsXjHGBIelEV4iDK6CROtECjFro8DtJJ7bALYhuqRPDoBBattKJ6q11oge4t8F44QZca6sTd+Ah2LXTiUtwC+5D1fboBxLhKnTqoBN9oRBqdUQ7cA22dtKJc2DeGfmE0C7IEZzuirGE6MRdMOyGeYJsuBiqE09AF6YTFrAQsqFdD53Ih+o9Mb5eOAdx4ahNsO2rEzUhGuZDJizqpxMxEegTXCN1YiOIAVgP0BFyoM5AnXCETlAySCd6DsHzkA3thurEGXgATlGIHZ7Bo2idGDBMJ1bBuuGYT2g5Au2MRFywAlxHoS3YCsNG4xxshUewagzqBS7CA9AbqxM1YCIkwksoAevxaAe6QDcYOQHj/Z9OuMEaWA8b4RIUwkOwnqgTgyABtsAjaDsJeYFNICbrRBV4AGbT0BakwSuoPh11Dl3gAoTOwJ4CKXAT7sEraDsT/cMrGDAbawxs5yBOSIEnUGsu5h6uzsN3MI1B29ALrsJteAVd5mPeIQWew71YzM9C1DYMj0P/YLBIJzrDsiWICbbCLbgDHvHYh2A71FyGOCEHRi5HXYH1CpyHfrAIlkBxAuY7EbUOq2BbEmoEDJJRhxAOw2EtmKxE7qEQPFcht2C6GrkDN9gOTikYB4yE02tQN6nIP+RA7DqdSIdNUAzV0vAcbEvHM+t1IgmiN+rEcnDNwJ6wCfFAOmyD7XAB2mbq8DsL/W9GvmAlvIKHWajVbNwLBRCyBc/Atm3oH6JysI7hLThuRxuwfQfW7S7ED8NzUZvwEsJ3I5Y9GBd47sW8QjYM34f63q8TlgexFx/CcxANSTDyMNoDsyN4Dl5CylH0DZ2OYx8Hvf+QZ9gEp+AyjDiB2oVnYJ2HdQODYBNkgh3rhBdcgCewS2HPgTdQ5yTWHeSA4SmMFTrAFdA/jTqGWFgMg85g/HAdCqFNPnIJXWEZGJzHewZGQPULaOcS9jAYAdrL2O+voC1YCpnwGqyu4rlryBUMh5zrqEXQL8CeByEwAKrcQIxwFa6B6U3smWB0C20W4hosuoPcQJ27eP4e9rv7iBdeQ4sHyC9sgnwQD9HWI+yjj7FHPUHeoepTvMNhFYQ8w/ciHCEWnF7g/Qv3YXAx3pUvdWIaLC/B+xGqv0aO/3mDfQJM3+JeCH+Hvt7DB+zB0AtSYSO8AttPeBaiPqPeQVOKe2DEF9wHl+Al1PmKevmG9QB9wf47ngMP6AbLIB+G/0AeYAucgctg8BPjgFZleA/DdjArRwzQ+xf6/Y2YK3E/tPiDeoAM2Aav/+pEnlUrHhVrovI3ucinrYWs2OajGp/4w61fDlIntrrIiQZ/OLhgE5l0epWXsK8q57SdR1trG8nJj6ZxXuix4DEPBR8/bc5WX99zp8XJFL3aloqyqsp7zZezpnIcnTndmNs/CZLGm36S8WQ/aXtrKj1ZX1NtHWSq0ov85I5YX2lxPI0GOvrImGEbuP2+Cdwm8wWVmu6jL4Mf0paDvtzPxpoOJdjKpZ/m0ufBv/mSTS5HeU7iTk/t1OSwPrzoSD6HxrUgh09u8kPeHDox3k7Nz/wV3HmKsRzXty6bfnzJb3rayMBGrvLanBW0+pqbvDrkIod381V9Ct+S847xtDfbXfYd0YMTjpKq93wGe9lZyNq/Z6m/dbvQ94ZByn5PY/7RdowsPsHc80V58KRvUZxddDe4n5eDPNLkER2sSZzr/YjKF1orp94/ODx+C7mM6c427o+C77Z8So/95gTf/PSB19n14yVj7eWbhHny+aF9vK9sFRne8pHfdv3lPenlvKZiNi1cq1NLfKeTNsqJ2+38QAnHRtKlv0N5odUgetdoFY/v4SM19iGsf9ZI1g2xUaePXc972CCQmmzzVP3r2EiTIU+DG/VcTm2GWsqk8inU4kkDbtpMy5OKRtKb53nBhTmruWMtQ9mh1Sa+Geioyg65Ka96ydx0lIl8PayCNuzpS6Ye7/nA8tK8sb91NHJvOGdNfsPWUf5ye2MPafrVVu3v4SHte1vJA2v9lPXC33z9vrksvPcfHbuZwCtzqsnRu0o5oblG9ZKb6Jmei3o6w1od8TfhKbo/ZLzMV+Y9PcC7An1lzYo5XLPrN+5iqqfMMzdzQYGTXLHnMB0d9J5OjSqhLW5mslunGMq8q+hvsaucdLUpvckzUs6GM+nFp86U+cWfW6/TqkumvvwscCFHWjjIYcUdKfrmVJq+9TDdO6Kvukx/wxdaP6b+xzPo1QFXdea6jzyQ/In3VTjyypr+8vrDFzR2vAc1r76Bz7/xVKrLLvJvM0s23SZkpMMHrhwToAyOldOEQQ+5/QFn9dxeT9ltWcrXGnqqeQPe0FCzMv6fQ1Kedcfn3KttBW9/Npk3GP7mwZ2NedFQR+ns9S3vdlw3mfO3Cfc5d4mCLMdx1SvN6Wz5OU4ZsIfjH2/neL+qssXiWXLNyg203HoBjek9mt4YbOdurSrorsvx4CI/E7V+zgKOTnFWWZEuapjmD7fssJ0de7mrAL9tNCdYyEOX+qjfvXJ42q31VGTio/QiHVTMT0+ZeThA1evoT3PqLKeJXX/T2Eu9aa3NWdIfW0bVhrvJg09+0KZu36m4PJT76LnKtT+/ULNT53jwBSf5tng0hbTXqE65deT9oYZqQZ0yijQNp1H1Q8l2lo2clTFaTn74l7X1+nFM/iBe/r4O791vLNsLY/WuwxMqlV78v0HlPK/ZXd5xazHFZbjJ6TGjyX21N3Wv9o3Xn/nAS/5bQHf7TuCsa51p5duPPCY+Lvhc0iFKf19KL0vn8fPAJNZvZqxo+D7yyd3KI/a0ok5Nc2nOpGRK6Wsi47d+55iPv6gk1VCGiXT+1W4950xvQNVlcfMpbu1laL+a0sArlTynesuvF7rw8B4H+HxaCt9bYSgzPi1mJ28nSnfUVyGXMqjV1re8fvQfbtpxFjX/Esix941UlfE/ad19HS1YMYzC7YbSjgJ99WWsypvm8ZY1BwLVJ5Nw1r/+jQPXHw2e5/mT1/Qs42VttlKb3+4yaa1W2iY7S//wE/xjRX1euK45F88fGxxRfbhs0zdANdh6Odj/YimfbmAvR/0NVFOPBfC3dr1lk9YBalNNnUxq+I6am7TmoPpO6tQIY+n3wVq61OzBE+pF8/MTljL2fRK5t7WQ5n/20aW9Oykxpzct/E+rvH+mkFdKDNtFBMjjM73U3dGTeGFAEq/2zuDRs37wkHN/qV7YWZ73MJ1Wd7Dk1NNV5PmuOikPessNjSO43tWPvNLbW7aOCKC0317SK9ZdduoQQcHm0+W+OVP5kq+1er9DKwsL3GRY/AJ67RxLDc/E8dHJM2ljpFb5t07n0Q928IAn72nZdju5LNCPvQzaE3ecJhtap9Cls5+52L1TXvVPL+j+8BdUbviJJ3//zGFhr3m55nqweZ6nTBx3JK+i3Sp2+FiW57y6nDvesVZV5uWT7dSB9GW4t9y0Zx41vL2NLZrOyPOM9JZfljlToG8lj951gUvb1KDTi6vRiLBKKjS3Vn0sXcnhP1O1qG0G5ZzpSZGL3FT+76rS+WdVfnCuPY/JtMW7xEO6iqO0eno0t/40gtPPvWDfsmU0zquqetHgIw3/5K82NkzjqWGdqfTNLU6756lmDd7Amh16sla3aDKdlUVD8nfRmSXu8pVmGrtv95TTPmTQp45BasS3l6SXvo3yJ/rLrsONZOQHU3Z6by39yirpRvfWPKiHnkxskcEHx/rKoFH7eXKOgdr610duzJktu03P4tlBVVRRt2zeGDNXtij3VavxzzUzbzd31B/LMSUdOCd2tqzRzUNenu1PPl4e8vov5v4O7+nHUi/ZeUNTsravoDmFXir7ppvcHR/NF0N+0IXAZ3lDrJ3l/Fvb+UpBX85sZsJKDpcON6Npy35HueZAkNxV30E5hRfm/XtvJ5v95ImdXtOQRy/ZK78/hTfxUs08+1Lck+ry6EMrOTO3G9163406OJ/kvfyHTeOqyNDwbKp/6ldeg9HpdGngQ3Lc8jnP5vssKnw8W9Uc/p7eb8ylHlGj5bcCF7l8rodctaSz+pjYgw2vrKU502pz9esecr2rVkV8CJJHduqUk5ohf5S7Ss0hA7XAVF+ZHJpJ29/6cN7kKrLtEhtlIgJU2eIA+cjNXqV21ZNU4aPM+tbna/skn3zkw4OvfCEzPQ95+NoGfrvSX3362JTsbsyik6t6c2aXPdw+YAXp39xP+dWC+Vu4vvS+14eT744j+mCnHDsup2fVhsn9u1JpyDcb9TqiN23baaZem/vJQ8VZ5N2xkCYf3kYvmw5RG4/2oOPZs3lM03Qa5DifU+fXlT0/LOCNkRt5TvZSzugXKCOutCVN/hfq08NEPtj4lGJblNC5QmNp/vkXLU52kae2vaaGHttpUdv/uOZXD3WaX+ct9zOSd7aHc2Kun0xO9JOdzVzkgxhf6fHVj/cO8ZUtC99yWUcrmX7ZR9JnV7XmuLc0vu6uErFWr7bwlkGDc/F+9paeF/byVT8vGfugN1t376nKenvKwRHMr72zOfbbCZ7WdTmVNqiktI79ud39X3TZxktmXjrIq6+6Sa96o6nJzTjqYV9GFj8X0MC6ZmqocFE8qjJv2rdNPM4avylnOSmfI07qWDdFLeVU6prZkr52MVW1p3rI9G8fePvoueq/eXOk7Lo8b3eHyWSq/546LCrhDvp/6OkPD1XkoVPFT/7yNDeNLKrWghNr/mCnBZ5qfTsH+X1fMR3K38mlOhdKfqNlq5JFnNbPWG7sqyVzKw+5yXsZFdgMppzov+zl5SeX3DHmhnqj1YZVGrVydyCNqD6Ryrvqq5rz4ul2pru8vMhX5S21lK94G9u9MVYvus3nNlf9VegJUxU+3UX9vv2b3FK/UdN7PirwcBbdyXpD6a93Buu/mkihky3Ux9AnNL6nhzI62I5d6h6k/BkjyTN3DG2sGcZGc5l+DPaVbnEnyKA4mUusF3PjeWd4SdtFdPX5FFnWeDMtxG+mjVd/0a+CKIrMsZI3Nz/NmxnjocZ+sVDvulqoFptn8ejah2n6r9qq2mxzeTS3M88NqSsrbc3l+aNu0naek/Kd6a4qRVf+9spD7p1dyV8D2vPMz05q25jzHP56ARu++UPLhmvV03nOqtnWJaxzNFNd+v6gyrVBcuKvrWy05BeXuzVRt2MOkP7qHPrUbwdN/vGZC1JrkuuvVXw3fxsl1VnMb9bPUf7zW5NHVndemVrEJvcf89vPE2hH0maufs5AOk0PkBXRdWTBvrmcfd5ZnqiWSc+DtrHVzsnyadXP5ProRF7nwp+cJWbJ+e0tVEADc7n5VFc1tLSqcl5vLGdl6+SgLa5qVkkq1Y9tRcFT7GTWMy/56f0r8gwUMvk/d1n1gIF838pIrWsu5MPXiVSlnwunrPlLZ3akcUncWp4y6w89uOUh28xpzMfOe6vF5zdQQcka0j9ezAljZ8svJhW0/ft4rng6ma0z39DNzvNkgHMWtWxkqGoEJvAjE6EOlfmqua2cpfb+HKJh33mqxVdqa9OF07Y0lcd/fKJQ0yLaHVbBQ0eOp9Y7nGRvA2Ne1T1Ihfs/5b6rw/hykL/ye+5APbs+Z89UnYrptoVv5P7m3ekBqktpIH3PCKPIDzVopX0oLQ7dz+Pv1FYXtKlUiXfu6LMuMnXvPVp5chcNtUihY2lNyL1ad54VFqA0OYbSpKA1hZm2psnfpvLIm5XksfYv3SjUyBn9mpF2fiMaVrcBje7sLBvUqkdLBp2mz9+/kVXPlfzQz17Oa+Mp9xyzkdG39JR1r2py54v3NPNybvCkjVj/I13k5WmVHJdcT7Wx9lN7yqtT05176Oc8W3kydTNfyLLnM/291GNDC1V2zF/Wr+pOhyf150FrDNSxAZPYslcY9boyW7254ao66b9ju5h99OPeaOpsN46C6qWQvDJLdq82lgtb5QYfzp9AM5d/oNCnX3le20bcv3YW/d4RTIerBsnn3z9R9bxNFLI9hfqm11G9plvT9BQLdfd7FO1paCm/R5RTafErWpF6lCo7Gslt3yxV192bqcJ6JOWt6cEVu3pyywWu3PDYQSprtpb79+5ITWkB1zm0gHst/U5jOxjJk2OXkd1kS3W0mqk8NHADvZVBdKSGD+1t2YNc6/kq89ABdPJoCY0Z5S5zKr1luwV1ZK+lO2hQ+WP+UriNNtNnWjH7LTcYW0rDnPZzm2te6muolVz2eRPvUBqljXVS+5/cZadBBtLgWUHe6Y5/6IZLLh3QWCv7tnrK9aC/bDowgx5NMlRfl5ymiMJyfp+RTg6NtOrnwQDZOymWR/6vvqzcuInTDjZihd83o25Yy6wVQcpwVjcOe+ElDxxfTakzd3HMTI1MWtaG899N5JaLZV7cPiE3DUimN4N/cX72Go7cv4IKxrqopvd6s0X5/bzx/6XwkMBDfMpoBqcmWatLCYvJr3EFZXqZKOu3NrJl3UyqluYqG7xP4OpLa/O6yz157QRXWS9QXz459I1il0xTDWMCVPH5Uj51uYJtir5QtQIPWTNrrvIa4KGch+Bd7uIifwwMZOcGHvJ+pK/qN+gIFQ1pIRNufaP7ykCummogv7/8yycsj9L1M++oVmcLubXKOwrY7ShzA4t5dA1HGdV0GN066yCfV0SSw8QSqlv3Lcf218o6M97xs/V1ZNi0vrRTE0uPh6RQyy595Idey9hplp9cVtSNfrZ7mDfoqjldDP/Mb3vt46ofK6lGnx18yKckT838xmkfnlJKiIMM6f2Ty0/95IK9LurtlVk0NnsnTbAykTEPy3nZ/0opP0kr2zx2k+xfQfHX+3Cr7vNoU+Vy6h19Ma/X6D90K8lL6k2tzt9LtvKZd4Lr7JrFtwpvU/zHrzTT1lr+EI15wEtTWZH4gS3nRNGkvy8o8YOJXORhJOUtS7nC0lVdez2B7Wqms/jlo1pr/NTyFHPVYZm5yrz8Pz6110eZ3jNSHztr5dOEcvrtbK4MDnqp22HVeM68P/x2/ZVgkwbVeEK35zQT75uIKv241/fHNLiulTJc346NpsZxxNDN1KCTi3qluvD8qF2coBNEO4Jk4sMpFJXoLxMcMjh3WhU5tZmhXKlNotKBi8h4bhw5nQihTnnRvHtyLkXHvA1u/UmrfM45qWohI3nC5V+c0sRE1p29mvyn/6BlkU/zqnT1VR3G1Jet9ZZR3KBxPNxlPH8cYKqeZcymUudqstHD8eQd+Ya7HBmusiMtZMOe41iv0IZql70lr6DetKRXZzllQ5C6sPs3ncqfwy0+jKFdV4LU96BG5JipaMeKfDL0cpej9I7Rjv/VYPsZ18i77Q8SG5KDS7sHyozpUdyyu4vye+ai1pGrKrgZIP0+eko9bzc14/dOahxlKWPc9KR9aHpw2LlEpjt1VJOSJK5qtp3qn/zIW9s84gWfdvDT+uu4rru+nDQshY8anOBLc+Lpp4WRXNf6JXvMKiOz637SZuU+nv1qMzlnGMiBEV95z55NdLxSX1ov9ZXNvmTQ8Gczqe34pzz8s76csHYTR0bYqc8NAtSNUb35Tr0UrnPOR7p3dJfXttuoAf1rk9G++tKzRTpNzNWTV0Q6Ja3eRxG1t/P7Nuuo+XsvtWmOr3Itqq8C7Frx6NC2vMbHW/LrXG7wO4RTg6fLGdPm0elz3nJXY4268OsQl63wlNuMrRR5e8ovQ9J4/8INHLxuI7e6UEmH67rKP1bNlf4fd3nnuJP81mwKi1VPOHXYMTZ/F8ePvfxU/V2OarMWe19OOdves5QZv2Yp60StWnBW8pTepTz0gzPVwz6nyZvEPuWNacTm4RztE8W1U37Qq4Rn7NrwB73r/Ie9qpzkj3MjeGP7bG767XNeq6sxtNPbTI5s9InjG/6iwuS9/LrMXv06Z6dWHtKoG7lf6NmmSv64zlaNLjZTatFUmrnqA/Xc4K0G1nLkXKdPpGlQR/WeaqsS+xgo405W7JVhrKwdH3Hj6O70Y+wYejbLkJe/H01j0yroZDczVbfdTNmu+QA+3XQ7P5t1J+9vQRHvw7XCUyNZXPrJostcmTbGQbq+/s5v9a3Uq8WG6uoaX/XJqw9Z931JVjv0ZeOLj6meyXa63mR5sGWLCnqn203XzBpTf01nNWH+Lm66wUKuae8rO+s8Zf3CSK5jX8oLxpfyCCqizsN08mz9R3luFk5Kf3wnanMUv9dLnOX6HCPpu/lE3toJteScqMPBqX7uiqxj6daxc2TcW1/6rqgpE+ck87R6RKZCT/Xb0ZwcevrJVpvtpbg6iPmEm6z71Ur2+NKe67aq5GjddFq2eREfrnxO33av5sMlVdltjbectNiPf/1wp9/jBnL8nXGc1HCOHJTqIlPCvNXVYhdpt9aFaht5qv86GKkPIbXk3KjHzTN3mcrIcSaqLHSgpGbb6NNDX9Xbx00t2GehGvbzoNu5/emj7WueEobfYcPM1Bq9v/S4ihW92rmMFj+Q1CbeWR5zqU83qRZZ7bFQdNRM+cUeJr+jOrrRsAoNmj6abP1D2fNRFXneqTLY8u4WbnzOlDJm5NCXiEz634vTNLQomnx9e3H6Dx/VaVwCtTr1OTgz5A1HheyjkuERfDF0BNfJqkbzz07nT3kadT80m/v1D5Kbzwxk7RY3eXZPPR6wz5dv+TPR70pyaLEGtf+dJj+uLwPOVSNfvDdsWtuqKrNPUe3LPir5x8vgoX9cZM4Ib7V1qq98+eYrG577H9dMq6cGvHKQT06bqTSjk7S7KJGmRnmpqaXH+PsCc1U9eZ480dZHtlkVpLp3rqqM3k3l7rOqcN0PMfhN9DC4uSznjXMc1dPS/fTorLtMrO6kKr7spah7Bsp94VdudKoH3Z0dKFv5FPF4q8k0tvEQKhngoiwPOKv4r385f91c2SnJW10YkEvjW8bRwUQTWTC3sVr11k36HAuQM0b7yAb7j1HFmg/82C2BTzX1Vud+VVFVVujkuDQj2bwymctKv7J+ynbycfSUO2c24JsaP2l711nujX9Nf/11qnfPN1RS+zM/rmkso9f3Yr8er2lv8WM+u3AcDWi9mnW9v1GbnwYyq5qZHGuwhMODDNjVXKM8LpbTzQ6G8uZUP/mq6UHa0t9HVfxELay5y2mfplHf3M38/Wk6HwjfyWVmfvLYX19ZmlzGDVoZyB7L9VX6jv3UYuM2Do/ykY0v69SrBvv5TWQGifhkTm0WJHcMjibjoZ7SY/XfvINfzWX5kqNs3NJJXWuGfCaM4Iowd6X1PMq/3rbkNFO8O/AbPHiQA03Yk8tvUgPVjWdB6s3zFPJ21Mr6O7vw7epC5hxknnhkEoWWe6krA304aPU+jjD6H0dnF+VlXq4td38S8t7dOnJpm4M887CQw9q8IpsRPqrJXwuV1NRMxef6SqvBD3l7zip2j06kbXemqeopCVTLwl0OdSsNNsB/FPN9lWQwahgbUyVF1/1Ndx338+riX9Q23EJ6P3SX391O8tG1v+hep1/Uar+7/JN1jDsd8JP7lnrIxRN3BOvVcpdzslvRq4tucordVzZrOFudi9/Jhh2TKFPfTk2NzKdapR6qsKaZ2vZiIZ0d8YX39vtGHlbmyqW2L035PpfSXOZS1Wk/+cd4Z1VfuciI/+rI+HJzWVrhoXIfzqRJjWbQhPihcnftA7Ta7kNem7gp1P7kEh4g/FRhlLPcM+4BmxZYqVnb/uOVvqF8HP+nm41ykvkO2RQ2v4KXZ3jx/VNvae3gCm7i66NO5XpJs/fP84ZNiyaraod5+am/fLX6QjZajva9fvJveSDv4UgTFWv4gteuL6HJ8UGqTdhP+mn5iDtWeUkdOtZmU6e1dLBCo1ocSqUf2rcket7LW+7wkUMndyfzKRVk+q2SY3J/sDfmparuIB+zNFVFVl84fdrZvCt9vvCB7rU4yytQ5bV/RuUpw+mhc1lw478a5blIJ3eZTpPx7/+jnwWDOSQsMK9mpy95QW0CaMCQ73np+W1oyeK5dNbBQe38aCqn7vqP1tdrQUVmT4OjI+7whFwvej7RVU7f10WGpbrLLV2GcI/PtWSPdZ+pq56TTFvwm6JKP3OjkBPktU1fdgh/RnTDQ0buas2HnzWiyTt8VY3BDcgDv3P6vhUqbXCAdO07kVqsqM6vvqxmrrDlh3vf0o4J7uqD0leGh3Qqrb6jComxke/zHlOnigA1ta61TLmip447eqmrVQfQ1oy6ZBjtKf3LDdTnpCCV2CSexkS6qG+DbGVtE1+ONV3P1osDVY2Pw3lD2BsamJjAnWeaqNnXKylqe0Hw6ZkGcu6a/nz1kK2saPeNzhz2pY6ODbhnPSu64O4r/fAf0PGLuXrhaaqyk/3UytL3vLyfhxob+JkiJ3altX2D6KfePZqZeyXP8lMy99kSIHuk2qnLMdF0afpgirW7wntLrNVRyx4Ulvs3uJ9nZXDuAlvqHvY7+NnhGLLRWCgv1OCwjPd0dncaz47zkFPCUrm6yVxZw28a+T/txbtOfQtel+8gY2NtVPLNTTRstrEcdyVIRm3eRcemR7Lf2T/082M22azeQRccs4Ij9Feycf0xtGxkGnnvW06rBxvIPQsG8tN9u8jAaSL/NRrMUT5J5NDDjHYFTpUtq9fkM+b1aGjmCDY1vZTnvuc/Ol3XSb25aymPL5rH5xKacd0eMzlzsJ9sZO+qsjd5Soeu/nL5sON0MYKZTv4NrnH6GHUZOIvOPLCQ3ZPKuFrKyeCT0e+5IGccPQ/B/8Qmkju3/8HlW/rT79ar6USFuTyxdStP0g3lfYvX8d6Xh+hV++bU/5O/nJsdx99CTNXnXp/opU0kHerloAb+2U8xU55xQPg+UtOHUVUHT/WzNI71N6cFJ9ydz7aaH7wjeS+V7OtJt7YHyqvpnjziwBIuaW8qnRoEyrsLHtGVEAvOHtiBRzpvplGD1lH3xyHcOc5VPbK3Vvsy48k1cxVdPriTFnb8yG8rqsumDb1U07hEMmu8k8JqL6OGpU1kTJGffCHrqE6xxrLWhpe0G3u+3tw0ypyQzKNu+8uF0z7xofM51P2Qv9zT1kO2z1lHF2/GklOQkSxetI7tV+fzvC3fye+2l1rbsDX/Gr6WS5ZmU/dG2bRovI+M2pjOo674q/zuvuqk6k7xEz8HW0z9yc3bZdLcG75St9tELW4TRjtyKrjuRX15Kd2O3s/vTg5XM2iOmaHaXOKvkrZtpMWHyvjm43LWL/eR+XvWU+RGPXk0K1Cdr/yad0S1k0H6FtK+m6nc6eAjh5g7K6PYCop8nUp712zn48938NFuASo35whPObWW6q3UyrxYrSwbEaR+t11DNTV2nF/FVab1fMQDLAexFd5XMxa/zZtS20tOq+IljwxZx5tnrqT7WQe5v/l9clwg5MeqL6jlx79k+LA8r8Fcb+U710VOrlFFzXCNoIH1/1K6+1o+8txVdrqkU748VfYb7yZND7vJzN5rmR3W8u7zHnJ6N3fVef4mtquMkuVxAarxTA/plbuFt/j6qmLHSpqj56qiin5Tp15fed3OaB53J4fbhLqpPR2aqfj2+K9R5iOv9F9KB0cVs/0kI/m/M8V8dcZssli+kns09FORn7rKp+a7afatchr3yFC9rh9Cd13LOLNfGdlnzibzvtnEyy3kCL1Szhk6ik0df1Lu9eU8zDKd+swdytvNFtCUs99px5H3tMeso4xb6J/X5ArRll8lnJ0xR+29844b9critC01eendH3kmZWZqUVt3GTLGk04M/sm250No44FIHjNBqF32zfjykhmqaaq+Wn/wM2254i2X3HOWSf1ayxlT/FSFn4U6FTGNTN9voKst37B/NWf5ZvcbOh9qq1jzmwrxO6/7ME857RjW0yQ32f6Hhlu/N1Unxl4JftGoLG/AXh85cdc8+Wh3H565Qk9Zju5CViMc5fCmjtJ9zGRZ1DNQJp5cRKV9QvjbKiE3l2+lt2leUrT1UcbqF3fq4qkOFEXQiaRq6sXjI3T2UYhKWGkqSwy+5JnbV/LKGcWkaX+U+3XfQ0uT3/NIj7pq5hoX9cDrBs97WUr9G9nSsJJyGlWrO42e0IAuurgp+9R0mjkmlPIijuVF6n1mu6RmdLRjEB/Y94WGtRlGnfp8YCsHc1o0sbb0Pzebt+/Tybi2/tJptAXtfUT0Oj1AeWVG0rNLPvKRV4B83LYt1drxmP4PU3k/ng=="
_SIN_CORR_B64 = "eNoU13VcVE0XB/C7u3RImiDdLWBzzxioKGJgi4GKKCUWdqCgiCAIiIqi2CJt+6jMtRvsxEAQKSkTxXh/7x/fD8vuvTNnzpw5d3egIAiJkCAThEfwBlQVgmDwf0qCMBImwCS4DPtV8FpNEKJhgjruhRTooiEIYyEDymCypiB00hKEzhAAWyCwnSCo6QrCCBgPynqCsBWm6QuCiQHmNRQELTCHyXAA2rcXhFgoB92O+Ay2wKBOguAP0bABUiED1DsLgg9shgdQCepdMAeoGgnCaGNBiINb8NZUEKabC0JXC0GwgunwDgIsBSEZsqAc3oOGlSDMgvsQYIOxbLEOmAJJ8AA+wjQ7jAPx8Bb22QtCNRg4CEJ/SIJhjoJwFJSdBCEIEuEatHcWBAsoh6kugjAb7sEsV0GIgf1g5CYIl+AGGHVDPsDOHbnyxBq6Yw4YDYmQCSY9MCaMhwNQAoY9sWaYBrEQB89gdi/kvzfmhg2wB9r1EYTXUA4z+uI6qPAShJki8gLTCPHBMTjEsGaogg79MAeo9BcEbUgeIAilUAHtBmLf4BG080YeYc8gfAapgwXhMIwbIggH4RBcB1UfrA2ywHwo9hg0h2HtsB7UfbE+kA1H/kHuh3yMEoQrMG006hsqYYQ/1j8GtQiVsG4sahAGjUNeIWQ8YoUHYDlBEGyhEAInYs2wEQ5DNmhPxmu4GiAINVA5VRDqoeM05B/ewbrpguAdiFhnCII1WM1EvkFlFmIJEoQweAHvIHg2cgLvYWIw5oStkAUfwXAOzhxkwT74CGPnCkIopMBRMA9BTiAW3oFlKPYH0kE9AnUI9pANOWA+TxAGQjrMikTu5qMmYf8C1ONCrG8Rah5qQWUxzjTYRKEXgOYS5AccIAjS4RBoLUP+l+M8wn44DEfhKtyHziswL1jAGMgA+UqcbxgKs0G+CvsB1+AlfFyNeNbgXnj1/79rcf5hItwBpWjsHcTCYTgCtyF4Hd6HzWCxHjmFCjCMQW+LxRo2YBw4Cq/gzkasMw6fQcImrC8evQjKoQaaQXmzIAyBfZAPNTA2AfPAJfgAg9BUg8A5CfsPGXAAXJKRdwjeinFScA3YpgrCHEgEwzRBCIcqsNqGXIMLdEgXBD+w3Y493YH44AhUQDVo7BSECFDOwDmBwVALwi6cFTgKOVANSrtRizAJPsCMTMQGM/Yid/AQnoNtFsaCA1ANNfsEwXE/6hfSofIAcnQQ/ewQzgtUQNxh9DjQOoIaOYoelY1Y4QgoH0NtQjK8BtscXAPtc9E/oBZs87DncBc0CzAOzIEPoFKIPgA2cBAOF6EXn8D5h6NgeRK9AsrgExicwnynsX8QCqkw4gz6ENiexRmAWsg/h8/+w5rgGlwH4/M4e7ANjsC4i+j1cAdegEExagq2wh14DFYcuYAS8JeQH6i6hD5xGfsK8itYExhDA8iv4jzDHaiDOdcE4cl11NYN9GuIg8KbgnAaht8ShLkQAclw9DbWdgfzQTSkwHY4BjfgNrwAx7s4d6B9D2OVYM5SxAnBkAtB9zEuhEAS1IHmA0HYDYMf4txBHTg9wv9wC4yeYIzn6IVg9QJ1CUEQDsegDKrB5SX2B6Jh0yvEDdshFzqUIZ+v0YveoAfCejB+i/8hCG5AEyi/Q+6hFO6Ddjn6DYyAB2D1HnmAPZD9/9cVGAMmwma4VykIjaD0AXODQxX6C2TCQciGPAj4iOceJIK8GusFH3gDA2rwGcyFg3AYjoFnLfYBcsGuDucJSqEWlOqRFwiATZAMGXAI3sCHT6hRsGxATFAJVaDdiOdmEz6Huc2oPzgItdCpBX0PrMEeZsEH6PgZOYZjUPEF+wWKr9jTb8gB5EAVRH/HWPAU2v3AOGAHg+A22LZi36DrT+wd2P5CXUEOxLWhL0D2b8wJHf/gTEEKvPmLWviH/YFqmCXIhFDIhq4ymVANVnKZYAfHoArClGRCDeQo43MYoSITHoKdqkzIBFs1meAEWVAKyuoyoRskwRuYoiETpmrKhI8QoiUTbsMHqIFa0NaWCdegBupAo51M6AA7IBM+goqOTHCBq6Cmi+ugo55McIUhMAXa6cuEIzDSUCb4wwQ4DE9Aq71MsAe9DogXJsGxjpirE9YHmfAQXneWCYZdZEJ72AGHoAQGGsmEPNAylgkmYAV2oNcV98BbUJjIBE0YC9GgYioThsJYeA0aZjLBBiZDJMRDERiaI34IhwhIgo0WMmEXXAIdS5ngaCUT5kAJPIV21pgf7CEINkAiRNrIhPVQDeNsZUK6nUzoZC8TusAoiIXXIHPA+JALnRxlgg+Eg8xJJjg4y4TBEAyHIAeugYML1uuKvYYwuA0qbhgbUuApKLrhM0iDnXANquATtHOXCcOgnQfmhC1wGzQ9ZcJeuAwN0KU7ahEiYBfkQg3o9EAMEAIGPbHHkAOPQaUXxgaX3jKhEF7AjT4y4RXUwZK+iAU+gqYX6lGUCXPhDrwCgbBmhjn6YW6wgcEwAeLAboBM2AwO3lgfbIAjg2SC6mBcD/6wE9SHyIQ9cBseQpgP9hnUhyJ38BY0hqHmIRzyoBYcfLF/0Gk4YvFDjUAwxEMS7IAcqAX1EZgTouEODB8pE+7CW9AchVqBTbAD7sAbGD0aY8E2eAu2/qhpSIRdcA3q4RPojkF9gzfsAoexGBOc4TAUQR3ojJMJ+hAM2+EwKI+XCc9g7wSck4kyIRXSQG8S/ocjcAyUJyMusA9Ar4BNMHWKTJgH46fKhAKoB8U05BimQhzkgsF0mWANJ2BcoEyYDQkweQZ6GYRBIZjMRL3DRlCahRhgbxB6HYyYjR4AOyAXasA6WCasgzzQnIP9Bp256DEwGq5DPYwIQTwQA4chOBTrh5ugF4brw3EddI2QCabgCEfg8DyZ0D8Sa4NqMJwvE3yhAbIWYA8WyoT8RTLhOcxbLBPkUegfoL8EdQ058ARGLJUJRyEf7kMDxCxDbYP9cpwbeLkC963C3JAGuqux51ACj+ElvINqqPv/Z2tQQ1AG/dYih5ALl0E9GjUAY+AeGK2TCYMgAEIgFi7Dexi9Hn0KQsAwBr0Y7sETeAkGsTJh+gbEuxF7Boo41BzUbELOYFU8em8CYgLHRJkwbQv2A/Ym4dkCYcnoP1vxF3JTsGb4BkNSUVNpqOltMmE+fAKTdOQC7oFsO8YFB/CF6RAO+jsQL4RCGJzYib6QIRMy4Cg47MJfGLQb10AYmGaiZmE7XIUycN6D8wV3wXEv8gD9s3AP3AOlfejN8BQ67EcOwQ6eQTVsOYB7Yd1BrAlMD8kEd5gDsdAATdDhMOKDUHA9grMPsbATHI+i98BzMMxGjJAMl8DpGNYMR0EpB+cDFLmIEcIhAnrnocbBIB/1APMgB96CWQHWC+UwtxD3QCKYFaFGQf04ah+2g80J9E0IhfknERvYnEJ8UAh2p9Ef4RMMPoMedQ5rPY+9hitwHQZfQO3AXaj//+uLeEZCKOyADsU416DEMS7oX8IzHF5DNQy+jPEgEHKgBDSvYJ8gGy7B4KvIwTXEBv3hHtRA/f/fu45agmCoAdkN5B9Ub6KuwAac4SGY3cb/MA4ug+odnDfYBU53sU/gBinQ7p5M8INkMCjBPoMdJMK7UswFDaBxH89BcICxkA37HsiEW5D9EHM+Qj5gPdyDuY9xH9Q8wR4/Q2zP8foFcgy6L1F7sAv6v8LckAM6ZfgfHoDDa9TMW/R8GAlzQeUdcgaOsA0OQD1cKUfdw7D3OA8ViAnklXimwQOog3qIqcJfGPEROavGOa7BvLWICUzrUEOwCbIhrx7PYSgHlU/o5RANd+EdpDXgs0bkpQnzQCSkQXIz8gK3oRSGt+Dzz8g73APTL8gRmH1FTcMCWA/7IBeUvyEOuAmfIfo7nk9wDe6C7w/kEx63ogf/RM4hAzKhCMx+oedCOETDFRjZhtz+Rq3CXQj/g3qDBmj/F/fAIBgHUyAcIiAWysHpH+oW7oOeIBfc4So4yOTCDEiHW/AUPsFIuVwYC8nwELIVcuEGLFSSC6HKciEPjFXkwlLYBbnwAJ6Ak6pcuAM1MFdNLoRDOuyA/epy4RgcB1UNzA3JmhgDQrTkgqG2XAiDbDjWTi6chPtgpoPPIRUuwU2oB21dxAKGenLBHGyhBDrp4zVEwDu4ZyAXmkDPUC6YwhuY2B5/oWMHuTAdAjvKhaMQ0kkupEAWPILXoN9ZLvSHEZAOLl0wtxHugSwohxpwMsY6YWRXuRAH8XAJDEzkwlAIh0yoBh1T5BBK4A0omSEWSIQmUDaXC/1gDoTDaAuMB50s5cIUCIEl8AmMrJBLCIF5UA3O1nJhIMyDN6BsIxcWwBa4DWXwBiJs5cJuuApFdnJB3R75h0K4Dp9gnYNckDvKBU3oDObQAO2csBcwwhljwG2Y5CIXdkIW1EI/V7ngD0dB2U0u6IIdDILxkAh63eTCYJgHcfAEdN2xJg+8B7HQAI6ecmE9NHSXC19Bt4dcmAohsBkaoENPXA/RveTCNXgKx3qjBv6vD+btiz2EelDxwt6APYyDRlEuuJFcGAKFkMdQ91ALXftjf8EZXGAuJMN2eAxv/v96APIKSQPlQj40gKE34oY5EAJGg5A/qAbbwXIhDY6CyhC5oAZd4CEY+eCcwsChiANOg8ow7BdsgEQ4Dum+OH/gNBy5gRQogWN+eB8KYMgI1CqEjcL98Ag6+suFL+A6Ri7gsAsyaA/24AKpkAsDx6GeIQmq/v96vFyYAPuhAQInoCdAMhwDrYnIEXSD+fAAtCfhPVgHbpOxP1AOwwIQHzyAfVPwGi6D5lTcC4+gYBrO/3TcDwXQECgXmqHdDLnQHUbBbjCaiVggAoRZyDtshqlBqAW4AU+gBuohYDaum4NahHCYB2+g61y5cAL6hciFMXAFboJzKNYJsjC5MAAGhqOHQBoYRGBdEAEPwHeeXJgMIZAH9pHYM0gF+XzsHWTDe7i+APOCfCFyB5NhKkRAu0W4FyYtlgsvoAx0ozAXhIPDEuw7REAOnIPL8B7aL8VaoAGaoMsynCcIgTCwW471QANkr5ALGitREzAC/GECzFslF2Kg02qsH2ZAHORD4Brsy1q8D4sgG/LAKRq9EgbDUJgEGfAShq1Db4ASqAWz9Vgf3IT+MYgHHkAjXIlFv9sgFxJgH+RDAThvxDjwBirAJQ7rhcBN6G+wHhpAKR5nFL6B62acAYiGNMiA/ZAPQgL2FBzACdwhG55CXCJqBsohcAtihzzYniQX3oJWMmoAYqDw/6+3Yr0QAqGwFwalIE7olIqcwx14BG8hMg1nAtLgDtRAI3huQ02l43oYC0tgA4Rulwt1IOxAHsAVxsAUWAgp4LNTLgwHgwzs/y7kD15BPSTvRl7hM7hm4pkC/jB8D+KGFzBkL17DXsiDe/AN1LPQ+2EuqOzD+YaNsBlSIBtM9suFmVAAr6AZDA/gfciBBmiBzgexDrA4JBdKoRZUDqP/gR3YQwm8hvwjeAZBA7geRf+BV5CZjeuPyYXnoJODz2Az6ORiH8EFIqAAiqAe5Hn4ngND85EveAmv4BMYF2BuiIU4aIAWUC1EbcA02FyE98D/OGoWnE+gbiECYiEeHoLmSYwFLjAKIqHg/++dwrk8jXMGZhAIr8HuDK6B+fAU3oPJWfQgyIfb0AgjzyEOyINzoP+fXPCGBTDvPHIGEReQa1BcRH2CE8wEg2LUEsyDNFjA5cIyyAdlCXUKSbD+Ep4p0Aihl3ENxMGQK4gVKkD5KtYGo65hbNgEm2EvFIDWdawTXGEbNIP8Bp6xcAm63cQeQCgshXzwvYX/IRkOQMxtjAV77+AMgt5d7AO4wQ64Dmr3MDc0lGAOsCvFPfflgt9DPHMeYU0gf4zz8gTrhoWQBvuhGbo9xfuwGXJB7QVyDJfg2f9fv8IzFq7DJ4gvw2t4AO+hEdRfow6gO8yAErgPjeD2Bv0UNsMBWPcW/RkWvpMLp+AylIJ6Ofo6xEADKCowPqhUYt1wFRpA8wN6SxVihtEf8X818gyJUAcaNegz4AZ+EA0bIA/212LNIK9HHwOjT6hjkDeg9sEFBsACOAC7GnFuoAz0m3A9FEAZNIF9M2ofjFvQMyERGsDiM+oMzkIz3PyCuoGcr7gfwr5hn79jvh/IAxRAHYxrRd+BfHgKhj/xvQmmwQyIh6fQBN9B5xfWDkYwHMZCDCjaMDbUgtFvjAdX4D24/EFNwjxIhzIw+ov3YCIcgEbQ+YfaERTCFJghUwjHoASeyxWCg0IhGCsphHDYCY2grKwQ9MAJnGEQFMBrMFZRCOMhAuZBEbyACaoKYSaEwjNQqCkEdRgMo2CgukLYAPugFEw0FEIZCJoKwQCMwAlGQCwcgNcwS0shJEARfIHO2rgWYkC5nUKYBG+hHlqgo45CmAzKugqhCziAI4TCLqiARrDXUwj9YCRMhs+grq8QNMAc5AaIEVxgMiyEKMiEC4YK4Q7cg3btFYI7bIW9MK8DroXPHRWCYScFfndhvdAIKp0VQi3YGymEaRAC+TDBWCEUQhFodMUawRU2wUgTXAszQcdUIQyHeZAF98wUwkPoZ47XUApfQGahEDIgy1IhvARlK4XwDhTW2APYDyfBwUYh5MBTqAe5rUIYAMNhBtyEezDOTiFEwkswsFcIARAFx+AkPIYNDsgBVEI/R7wPTk643hnrhYPwGBrB3AV5hRB4D59ghivmgCZQd1MIm+FKN6wHDN0RkwfigDA4BkqeCuEqfAOn7ogbcnpgTT2RL8iDuxDfC/vXG/dBIqRAGdSCQx98BuugW1+F4AGT4SbEeCmEXCiCOugvYiwoBGdC/OAOyXACVBneh1HQBF36oe4gAiIhC/r3x9mBWlAdoBC6gischGMwbKBCmA87oQnsvXEvbIZdkAtLBiGngxEnBEDUEMQEz+AVKHwUgi+oDEUeoQzshyE2uAIHfVGzwxXCdXgEHn4KIQ32gNZIxA2uUAn10ADaoxRCHIzyVwhTYRvUweIxuA9OwRN4BjpjFYIn9AN/2A5241DX4xXCdMiFJ+A/AXsCBhMVgiWMgTpwmYT3YSukwTbIgSZogROTUQ8BCqET+ME82ApFsHEKrgeLqcg7bIYkKITjcBO+wKxpyCM0wvjpeA1ZcBnawDFQIYyFqbANmmHnDIw7UyEMhVngMQvnBi4HKYSJsxVC+2D0HkiHz3Ow3rn4HyzBEdxgAhTCAzANwdwQCSfBNBR7H4a1QB3Yh6PeYT4UQkQExobrUA/j5imEVHCNVAgd5qOW4A28g89gsADXgXwh9gQCYCakQz1YLMK8cBc6LkbOoRDuwCPYEYW9BZcluHYp9gIKoesy5BrUVuAeOA7OK9FXoQUOrEKMcBcMVuMaiIMDcBxc16C/r8U5hD1wB7zXITaYFYPzBPfhAfSLRR1vwD2wBRw34qyCWxzmg8hNWAu0gCIedQrOmxEnLE5AfuAuPIGxW3B+QTUJ40AjKCfjPEApNMNn0NqKcwpdU/DcgQy4C0tS8QyE+9twJqBrOvYPNkA+tNuOWoQKUNqJHg4pcBzeQDN8g8wMrBsKoAnCdqGPwazdyHUmag5eg/Ye9DTYDiV7cS20y8LZAtN9mAveQb/96FXwGuQHcG6hEWYdQk2D0mH0bWiCZnA5gj4D06EJVI6idmEY+EIEFEAhnIAz4JONsw/jIRbS4BmMPYYzBU/ANwfnESrhE+jn4nPIhGvQAqZ5GAtOwWtQzke9QwXcKlAIX8G4EH0ZXGEy1MPm4zjHcAkqoQ4sT6AmIAOuwTPQOIn+Aq9B6xT6MwyA6fASvE8jRrA6g+vgDFRCx7Poe9AVHGEbZMETqIAm0D+H+2EPuP6nEHpAJBSdRz+6gO9WsAm2wjY4AHlQALcuIr/F2CeohG5cgd8rqA24Ai/AUEIcMAqsLqGOIRxa4CsYXEZ/B7crCiEIdkIRXIIm+AxfwegqvuvAlGsK4TZ0uY73YDdo3cAaIQL2wDVo+v97N1GHMP8WvvfdUQijIQ123MUa4DJY3ENMYFmCGoIgSILd8BkiSpHn+xgHNoHVA8QIuyHoIdYCzfAFBj3CeYeFkAXN4PEY+YHcJ8gXlELYU8z7DHsCgRAKYXAC2j/HGLAbDkAOBL3AfRD0EnkHt1foexAFI8owBgRBJli9xn7CRAiFcKgH/TfIHayGIngG7u8QH/SDANgC5SAvx7UwH9q/R34gEnLgITSDawXihPnQAC0wrBI1DI3g+AF1XoV6gPCPqB0wqMb35hrsH2jV4n9wg3nwCczq8PyEHDgNT8GlHvdCJjwBxSd8LwJXSIBkSIVc6NaA8woLYB2kwnH4CoaN+M4AY5oQDyTDfXgHSs2YE7q3YB8gHFKhHrI+Y03g8EUhHAKjr4gZ7sAzaAH/b7gH5N8RL/jBZJgJedDuB84AhMM82AjGreiB0OEn9g8G/UL+YBGEtaHPQx4UwT1w/I09gfn///sHtQiZUA7ef1EX8BZM/uF9KIK3oCcpmL+mC0VldmBPWrtRgHy7l2tDMkWOW0oL9hykUzWebLFoRLYrS8WBhSH0dPw3r75Lh9D3Baqset0RWpH+uPi0SRgr/O1Nq66ZMeeyVZRy2Ylaxankub2QAlW3Usg2b4q0Pcbdag+JjnNM+B8NJanslC0rW+AkLUr7SR+OmdAdM2sKcRhMmWuV2b2VF8XtGVfp5MI9Ykb5HBZe4Eo/J7SJdg77eK/sF2LtkXW047+F/OgvO2Z7W19KtNCVLppZsUlXtZhJwZni0DZHtr5BlbnrOUpLOu8Xf6vZS8p9Z1FajiabYLyf3/CYy046ZfI5lZvIfkIozZh4grb1yxev9V9A304vpuPtVdmi3fH0pGI5hef7kUGTLQ2Zuo8vTq4QdW59p+099VhqqpqUUCWwlZN+UIt/DY1wzuABKzaKLZRHn9kWWnajgg4OLyDdbvNI3SFYLJm1mXs1qUpFsWP5yPnxXLskUozcnkEFvu/FZ+dbadCkrdwzxpYtsnMiZ7UYmuLVSPGNkmjgHctjFhjQV1V/1iVgGq2548rOnwuk6K4vyPyUO4seMJtctwSKEet86HGJCg1ptJdsp0ZRsLmm9GmuBfu+sJJM7UaQ4pYr+60+iib0r6Uefb/zxPGarBpx9zI+z+ffVWU/LSyZg3etaPXFkP2drMtuj/RiUTW6TH2dFhsc6MrUNI+Sle5mmtHrN32InkOGgSH07EIyjUyWsaeX+lPkkSBxzAhzGjtdiVEXJbqtf5j21ZkwRYwWW3bfTXpWspWPKtJk5TWradO+vxQ0yJN/H/GJrvlso27b5tMVdwc2YVMs3Rrmys7+06PVm60l167T6HJ4d9rby44dVqhK1qsHk6XSD+ovO0ZDajfRLn6JH9Y7IurPcKCLSXH8p1MyTemsxNRGyNhhNZE/PjeMpj6yZYbLLJm/vUTtxq2mtadGcVmrp/SfXw0dOLGO1AODJSnMgPQ2DKb109Sl68f3iHP/beDXLjeQ5Z6NdHlWk/jM8YJXr67f6EHcESqeOot61fWg3zE3RF3VM8VHhL7Ut1st171VIjbf3EopGbNEj/pZvPepAWxvgjHrP9KeG094Vvw32p7k3j60c/1San+/nJs3/6aBUxyl01MWif07jSDphRV7v8OBbLs6Sl+mVlOhsgUN+qrJZtulUnMPW1bYyZmWJymYh5aqFJ1kx4KXarOLV12oxbs9Gzh4t3hw0EJy2qdDs1waaVBgvbjx+idekeDIFugfpNAFB2nMPbl0dMRCPnygGnvteFNcF/pNbDI4KPolt9Kq/DPcYOB+nv88h0w+e7A+I7eQmfSPTvir8IYHEfTBzpCFL3ks7n7wg8/MiOIOj7ypxuce2f7zlGarWdDXU4Vi5yPbSDWsUvT8uo0CGjrxWf85saqX0fT6myor+7Scau7a0PO/HSTF8Bb+qCaKzkSpMZpeQXbHf/Maw1By7BfBj48o4IfvGbO+i1ZT1792rHhWG+VedeTflnlS2bgx9HvPb/qxuAOJyRas8fhvsfs1DfrSW0PKzJ5EZOkgGXvWiVEvjemMEEsTKyfQ88KDXGY+iTYpfxJ/+lhI7jrN5KtwYh/HttDnr87sLbOVvt/ykN4fcmTnO/ekuMkqbMrDFeS72J55GZtzrasW7OmLvlSf3kgjPnowd5sudEm52cuu7nZxr+c7+elcTxZkMV1cOsCIemgPJ7Vuf8Xq5hH867ECMugg0syBhRR5LUjcbOUm/ROWcWGdnEkn+4puW56Kee8WUPZaO3ZPPE3haj7UOs2HLtFM7hy/hzclpYjFvXbQsthzlKC+jA4OGCBqlW+k7b8tpNj5fjTO6AMFuueT8q3V1ENfg6WXjKTaQdupZkCUeGb5OV7cbESBd2KL3fpp0TH3lXR3xSYqiRlDpa+qROtXY0imnEaPg/fSSR9l6cvJy3Sp1I6lf9Bmf1IdmG7Zfuo+QI8N7KsrdUqfROejlKUUH112c8dkKjjoQcu8roqu00zo02sb1jniK2mqqkqNvRNoe3Eh7xNuyHY/+0Xazn15bEcT2u3dcHGk+me6NK2AD7iZSD3G60oRl70oJqqxuOL1aDZyxlyKKdvPmy77UJi/MQ1PNKaTumE0dc8rKvjQi/9Rs5GEndG0uv08+lDznN6Fn+BrtNQ400mlyEo5S111kj4POkImreos9+Yi8vdSZppjskjjxi5KzDwufiwMZK9+D6VJp89y2TkVMu/gKFXH1/Bho7Wl4JLlFFU8nORBtnRg6zYx0+0ot5qlLZXO+UitjV+p5ZML+3hyOgVLE3jSlN2kMstRCltkz3JfhNC2TR6SxoQcKrlswo+sDOW/9yfzATsT6cXMWgr/p5DsdA9TzwuL6cxRbXbb/SoVs+385+WpJHzYyKc5HOXfHndhcSWZ4g1PVxaZbsOenAwlzQg9SijeQAd15axII4kyK2yYudpf6jR8nHg1YrmXcj9b6b3ldHrTa79o0d5McugbTw3tZtB1+Qt+fY9MMli6jy94+Y8MfHvRlN4rpS6PLZjpjnN82udvosouF3ZxzxS6fipAfJ7kTOpT2sjZM5c2O/wSfaXfdLc0mLqneJF7j1Pi1MFdyGO3s6RM7Xnkqrfc+m8+7ZeMSGvufVG/0I4N61wrTvg4nJcflrO1tzWYo7uCTQ1s5N3HKdjepKfijPfeVHoygZeOTaPuk5PEmm3JPGN/IAVmmfHHUZ/p4eyevGmkEk1d48f/G7WHltX7UOkPZYruXstv+6eS7cOh5LVLhXVMu0gNZvNET833/Lr1X95jaTk15dTx46XfuK1oQ683qTO/BS5sZEc15uu2gnrN8GRnLqwg4yEraeUcP3JekEVZC4v4w27OLCgoXZxw/ie3zjOUptXZU2FRFXWp8iblwdZscoe1pLGjQpy8t5lnmTuy5ReHcPUJqtIu3Q/iiS67uYupHVsz3ZHJjW3ZT/V1FKiDvydraEyfMfT47gmvOZP30vPx2hT6Zx0dcrST5p4dRN/FcdSc6yxl3v5FDy6oSzszP9D45C/cYqqFZH5Wm82oMmXVge3YgbO14uEz9qzrXVfSsXVgoYkNtHxMNJ8v68zqvuuwZzM20JdDM/nQy46sbvdk6trgzPxcVaXXm5rJ7Olk2ufmxBYq6bGezm6srsqFTXKaQm+qhrEkyZM6Pe/Jx+yfRmETbVhftR7k+/4rdZ1YyVd3d2WsnwtTv60qGSh/pwb1eOoyIIBMu8ikwCdHKWx0NjWdl7PVfgfF8+bHyP2JnVRxojeZK+zZwdXqUj9FIvXiHZhYcobenelLxTeKuH5sHbkqRvOU2Q+Kp7zNo7bjnmxLrUgmTkkUcPEdrXQooMORnZl3WxJ9WRhKFWMEdmFuV7q5ZCsVXO5Ply4XkU3PpRTfw5Op56fQ3+wl3LvHCQoX5WQqLGVDFh3g3i9/FKuoe4nOAz7zzKJTdLMkkf+1zqdDw8pIZraI+mm7E701Ye+vbaObKYto2XUl6bSVhzR/RDqdvDOElBtdmHyplTQ3OJYnLT7Kx6a/oZ7rBdbnxHExcpUan6j6neujV4zoUMS/vE8RZ79UZdHzz/EYZ3vp/PxyunPamSUftmI55z7yc73U2DCLP3yyx3kSq1aQ/9MW2t7nAnWZfJjKjVeT+0aJt862p3ftPvCv6iaUNquer97jxN+OaKG1npaS9xFrttSuHd9goEoDSjTpQ/tiEhrsWH6gDUttyaQgu7PcuadMylK3YXYrq+neFyX2yr1KLO5rzhYlmrJwL2tpxqnulPFSRZqupyMNOVpHg1KtpaFKu8S360+Jsj4rxUsbttEGJU0pam87Gr2nRpyZVkgvPiTQhVpjSV6qz76/P8THznOj/eOvUS/r6fz/35uX7i/jU6X91Bffw34+cGD7HD3IsX4zD+65R5wxRc7KG65TUOEsntYnjStS3FjVULlk+N2BjdipzxbtMZdOTYnnqnk/+PjRnmQdZY3v4kVcK9aAmsp+8Jfp0ygwM4HGbv1KZSdc2K/P0+iAlY7UZdR08rPXkOontSdVu55k3XMH9zYr5ClHbvCKiku8RtlJumA2m3+Y1IuWFKXyiElt1M1khVT/zJCFnOhAUxtXUuY6NaY5uI5e7Kwm1e7O0nBxL1d9mc/tpF+Ua+TMLvRqFSeY7OTTvl0oNs9qFd1npPF9lX1oxOEckn2wZGrms0myTKSqa6ZUj/v+9Aym6yZTqWpMAm069JP72pSIJ1ccFZPvxNPylwU8WOrPbbbNJeP4fV5r1/4oHjD4H30qU2U3E7vzhMwC2j94GOXoqHIHx3zqVmnKRu7WZZeW+fMF3WQstOYz9/s1S5I1v+WtXY/T3PCv9HJoJPUsGkhn3c3YCvTTPjt12OOlm/j8KyFsv//d4rP2dtIs9638qM0SfrfYjEXqD6bK3aE0zlmZ3Y4SmE7vI7SxJJB8Wu2kBXkWdPKuBelYplNFwGLy7dKO/m2N4VFrdYudfjtQrqDCXk1fxe1XDaNH29tLho5LyGDRdkrSUZc+9jvOY0ft4yZUR/pdVjOXDtpsOn7L3Kuvp/rCYXRhTRwPqP7N3SQVqZ8g8cyHjXzY1TP8WesZPupZLM0wfE9G07RZap0+M/HeREY+o/kH/+H05JENFQQ5sJlBjB3/asoc76ZQ8NsVNFTPj95lqJGrtSU72XObaBuykkwlDQrL+UOm5WpMyc2KpSdYsqLnRszQtjM783Ei9TswlsrcnahBLZbaH1ot+XcYTpOCNNiCbbsp7VWFOCqtid/tm0lhdktE/cPtpW+Hv/Mt9h9EWb8TNPbXB3HmG39qDsukC8830pO2f3yhsTNbNnIMaSyM4xdG2rI2e1+xJcWcQktsWWqVKbd4r8LCe66nEy1j8dvrohhs+0dM1HFiL65kct8sff5J3sC/334onnK4SgnK1tI+rRFM+848urp0Aj16cJX+aDqwz5t/8ejFblTaJ4brae6ku2e/87R+l0TVHtZ06JUt2zZ7EiUfGk3vd3eS7m7SZE+DHZmnny7LGxnMxzAL3l31G92ecUXUux5AnZLMpB5K+kylPoDWRuqxlf7OTDdtM+WcbRbHjf5CX9NcpaFr63mdviF1CtRkWmsv812O7cmvwJkFP/giVmxv4U6zI3i2zVke/vWrOOh1PLVs3M2VYxbwaFkr3bnagdaP20R+bzqQu1cZl49tz8awYzRxSUdaUfOIOieGk0tpEJ12rOPvq92Y5ug2enxDW3pw5Ty3tTpO3/u10KRrbfQ3KpGu1bWRrSKBX6hfwYNu3SjeeD6YZpUHk+WDHHHKcGdptvcW+jbOmRdc+MsvJ0zg/7o2FRtljOGWpXmkVFdJTy9m0ZaEFWzBx9H85pK/FOiVT9sufKeEXi5syKAftAnf190dx9O4Y+r0LPQlmXe2Zj1mS7RVnXOHA/0o88lTCp8YTsad6vrK+pqx/GUp9HHgPDpjsYybRwyk5MnruPlCO2btOp/eHPSm/jfmk1EMvpff6cq8Tdq8Phku5522TKftkivbWl0o6nweTBY7MvnhTAta9e4ctbw5xCvdjonjzdPJ64o1sxu5n2bkDaF9VyfS1CXKLDwtnWZXL6ZxBuGiY8+3NHfELp5tqsKGJW2ne6Eq7LH1Tz5I2EF/Byyl6FtbqEPnVNG1m51U834rlbcG0d3j1tRrpVxKm2sjXXm/jGbv/0LzH45kd8N30r5DnSXrMg8pxOWteGnoV37nxx+udySNFOYV9MupiRfFq7GZzZP4mSXH+E3fbG5xzpL9tyaIzp9ZKJp6q7PEI3bSlht29DK1XCz0yeMHvT/Qt6Rd9M3rvfhjWCLfefoYaTlE8T/Oa+i/X1GS8mB8J/BktG7idjEO39WHftBlXa6OoopvjMbd0WClke/Jd3+lONIgi8d1fc+VX42mmwGO9PpwmNfT6TIpbushvrpEk51ZokUnV+2hFo9/vNMpG6Y2YKd4otyaWfrU0pnbY+jN5BjKW2PFvjX2oY7pqTzoUAH58tHUa6Qd05ebo+6yaHhAschHaBH7bMde1aynyy01YrXNJ0rq35luNY6n4O5D+GtPc+YTslZ0XdlBCtmdRM+u27BztITe5DqQ8x876bXLBspKK6IemzSl6NMHKGnPWC72smXrRrtKpjMC6MTzDTTRw4mt6f2Ps3t/+L3FeuxiaRMP22oh/cnQZfZbD1GftjYeu92Xbs/ypHWvPpPN/s28ZmAWPz4jmw/96cymaEdyB207yezHF3r2MA6/t1zYtpLHxfFTXFh01FdSH55P8Qc/i3lzknhzOyfJodqFzZvegzYPt5Z8PFWZ5QZXdv1CBV8xz5LdWKTONvYP5IlHlFmt+wwaWvVV7J19Uwx4EMVvLdKV+m9xZOsPjecepyfw7r6LixertlLEf2rMqS21eGllK31a0Jt6BD7jL3w/UPnoY2SS6SAV9f9Jmf+10S0nT3bO6xc9bPxB1sO28S2dP5NGvScz69RGXSL6kvCkjey3dmS5Z7zI6tYfYk/mkPqLIFYe9Vtc2SGGbrln8xcD83llQD4ZGkfzU1OnUnOoJdtUkU/Vf1fQ+IZa2t23gKr6CGziR2vmP2w0v7MqmfJMCshtxHImnUqm2YsNmfX7w9S29gP1jNxKqdcMpG4Wu6lyugWzDvnFJ2QUUbt3W2kffq9YLE4hw08b+d+rEyUKiaS63SfISjGfnGxi+fF+82lSQh63xnM0+dV8Mmu9VKw69BQN8/fm2cefFA9+F8iF+faSYeoWfl3flqWY5VHc+yNc+0Izb9VVonk4t42TNaWwiUdpQ984fsnsC1fzG0tj/PJp+K9F1F/dnlncPsCT7reJey/sIZuDpkzH/7jIvM5Q8eRMaipOp5feMtYUH0UpA9ZS15atok+GobR64FtqXVzEzYwO8zPOCslxmxnb0H0k19S2pupd/lT/n4oU/Wgcdb9fJt4z3kIbT72jqOalxNIPkW/SH0parSUleJZxb6v/6PvfdP4vcDkNnveyuEJ9OK3s7cpeNq+jf5mevKzIib2ieHpQtpwivIaTxmhHNiZ/LdX1xG+jd6toyPTVdOvdxeL7VWni85+rKf7OeQoK8mC5euqMxtfQLUsVtiV+JcV1s2Uy588kfndl7WTLKNdJnfVs2cfjTdPF8oEX6da41RS04z9a9fOZV4KxgXT42y4aUNxKjUqD6aH2GlJSaeUDXTRY7HANZn10Df07SnQpdDx39ndiqrofyf3pOfFPjTkLyXFkx+3s2fRcG2m3azVNejeNz6zNpMR4TXagSw15vFwqXjrWKpamWdKnuTV04qwLix/WmYuX24pHHNhDFkqObIjfOhJDG7lVvjXzHmvLzv5eSrvfrSClHbYs3mwsNUQrSSEXNaQexdrUe68nmxWsIjVtspEcrxnR9dVW0tVb8XTotrrkvbJGvKpkz3z1P9FGzzAabqUt/V3oyqNVDVhGig+v2+ss7VnoxEZccJC2XHClgge9pWUzz4tzk/Wk5D2nRO3ydsxfO5au3/9BazZpSyN/3qIPLzfQi9NWkuWG86L+x7H88/ZY+tPRgvWrdGAeTbG8boolczr3k0y66Eg71cYVP2vqwLz0O0szXPQoyDiImjMdmU+cI5s5YgP1t/GXnv5czj8Xr+Q/3zmzoxubydxfU3Sv3kBqhzzZsIsGTLWxmT5qelCHcCc2LiaAmu3U6eA0DxoZLZPiR5lIun+viFd7uLKaBdZs4W4nVjH1L1cJnkJNAQ7sqZOzpGfXLC7PspOMdPVZ37UGRDXzqHhcHLE3t3nqzgmU3HsF6Tm1Z4vstdkQx2lUV3+Ozh/2ZFNijNiihI7ssb4Wm3vTiumYaEv1/xylVXbfSCVYpENR9tJJK1f2cKKldGWpNmtIMmA3oo/yISU/uGjnSR9P/cf96xTMkvWkaba6LOBHEz93I5nGtyixVR96UpiyGd87agZt0sum5Bw5vek8kz63TRRLsrJ4yqCT3NehlQZtVGPL18ykPl/G8KXX9VicfxsXXh0l61J76feSn3Sg3Jz3dOlD7bqvJ7s9mlL7YTm0U0OJ7X6uzxKW/CKfh3Kpn09ferz4pxh8w5UGanemPzGJlPvRig3+1kaPdi3jI98lkseGYLo41otiVndkSqvceXa3aIry1iedkvnc+lU69YjbTt3Oa0uLoudQasYcckRNjTiSR1nDPFmG9Ifq9TfQAFZF67M9meU5T5Y0agmf8dyTqeasJqV9VtxvVJQ06Zo9U+4YQKVHskXTADUW+SGJ/L/2EBtP14kJNqF0uPNZfuWNnXTo7n3xz9mtlH+ngELRUw3OO0v8ox5V7qgij9MCuzqlkMqn9KOUMTuo9o4P32hdRM3R4TTq+V/qUGfHZD8SRL9lJtRvQjUljFFit3c5s1N/ZGxI1ABqPdCekgKOU+LUaP7072FuuaKIX78dydy7/Obb7qmzlVkS2WhHkrR/Jc8f94LeFzhLZgVyZhfiIsVtjqTzyU/E8mA5xc51Yp9NHaVHjZG03+MSz70pp3cOmlL9njh+tM8XanrnTdFiCs1cm8iD3QfRazNPae9xDxa2UkG3Iz2kUc8HUZ+FaXyGz0K6aj+Yrv9YSL+8lGhxnz78xTASjZMU0r7XIpffGcQnR2Zyw6RFNK2nBZ94+ixV5uqz5NTf4lXzMv63Wrl47uw19HmuKfP7dZpi/i3jWlyVB1WbsiUt6ZSW7UN+IZZE7Wxol+KEuM0vjQf1TC+u3W/E77R3lcZvX0KLLryl/UvPiivmW0hrniXSolOuUsmlAWQXFUZDKlSoqGwpkZG9dM7Xle94+5c/NFNldtk7qOtgVXbNxJzNNNSRvu52kDbO2cVjfpmzTo2xtLyPjnRSeyPtGL6M5q4bTScul/F2BzeR6oPlFJXqKt3QwnP8uQfTGPKTW7c186acc7QsU40c+mXQi5wKWh6RQaaBK2nv0wraqP3Ua0KMGcsMm0fy+lbS+r2Pun5fSZWOdnRf9Sj/ETeHXuiqs3drL1DdQ0tWfKmS9Pkq+s/Kik1016HuWi7S2djffMsALbb7nKvUHDKSCrrspAvRcbxevoaMV4+ijs9tpArPNbR0ejWXjazl1++sIe+w7WJzT2vWnLObOt/eTee3TkGvsGLJk6zYuKOfaU9rE50f0kCJ0wL47TBNvsnzCy2v0mC+8hZ+o0WD9dbSZO+OqEtS+2pae1iF/Xk0mkpznJj31798Z90Xyq7+yWU50fRppj/9uBJN1auLeElCI7ll2DKft/7kdNmULoQIUkN6lTh243jed/g66rOtgDcNt2W1Cl/xRpsTLU9N5jq73NiMIb+oZMNqOvpJn7YeXEeCDier7RZSXnYtyQP30ovPF0S6actMDAwki0e20kIXZ8lKriI1Z6lIrU0Taf33vdR88wu/4m0tFYVbSJ1fb6CGc0rS03kutL7nNnqgUSOW5Gqz77PradE4C6l9Snu2fakpK1XNp8629qzVfThXhAZQ4zoT6UaYK7XGVIiZKYG8n/leWrzBno35Pp6qOqwWq/rk8Uu37JnHgJ5cf6ydGH32JX4HNpB2YgMdSvtCH/VjKXeuG03YLEgJvkNZ5nJJvJV7kSf2kIl/flmzioMm0l0XC6Zu9ktU+pRDPzyvU/pzRynloyHzuH5JLPmvG7Uaj6Lxjo7SnUAHtvLOYD4myoK9/lHPN+6axmMy1aUJXw9Q95EB5Jk6WuwmaxDXzZjP1zU4sqOT34vHvjWIDyss2cJ+OmzmS3um5ebE6n8pSZZbDNim6QG0c0sddUBfzKvqyoYZK6S20BbqOGUjPWxoFGUVR7jlkxbymanHguzl7NAWPTbO7DCP1frKa02cWWGGHjv37Kb4fIMdO9khjnpVHiJLyzjqkeVJ2nMj+dSydiwx05w9DK6g37ftpJSl8TQppDulKilJAYXvqQrfJQ3DptF3+SbquG0a2U4zpHvfbtPFl8E8+VwQzyyVS3Gl18V+ldqSdYwN27HxCAV0M2BBP11Y2qONVFazlH4WbaJYuy+iVb0d84j9RkG/H/G9y3RYy+sv4ruDe/gx1aF8rX88meyaweunxdOPG1947/RTNMq6J7n8s2fG/1yZWVsgJQ3azSNe9CQTqxk0zMONpXXO4P0OZnGn2x6Ssaan1DKsjVbvdmMLD/SinoUd6MCJsXzFhKjiqPMn+SJtS+n0rM0U87cvXxE9k0qfLiDdzAKep36Mpk+q91LfWUkrknvSsbLv4kS/WbQ6N5dXdU2g7jt/UmCLoaT/s5DuvTpGrf3aaEpmOckmmjLLgUE0xewL7d7bKu4b1uDlLiXQ/ODt3G2csbRb05XJ3DylpwlZ9HjGbCpY0kY6Ixq9coa08XP7hlP54Xxa7NtMI84vow5dgmnx0QR+izdQjZ4fzXFOoOO/lvDabcFULibz3p9U2dAYL2pVFeinchcaXnWId/fqwFwG6tLhk23iwAFe/MrcPzQx2kDcunoL/dkygR+58Ic0Mop5l5UfKX1uOu/tMpcOvhzIN5XP4GvWFvLwrL8UZGtEmeObaeKpdaQ924StMt1Cg08mUSs3oqvh3Zl751AaEnCGb00y4Tkrj/O/usX0fomydH5PGF1XeSgumuPBWhb58Ev9ulL1KIHWNYXRzzs36GDCU3I54SmFeRgwl3k3eOACQ+aft4Ru+Mjow1J76ZT8OBmalvJ8W1vW17ep2LsihNfK59Gd6w7SdvwmiSgKEq1cwvnrTrvoyKOp/NDYeeSnaoq9SKPpdzVZxZXLpDxHWcrhGynD7Yl4/l+iaL5E4i/zNnPjzQ3i9hOuUoR3JM2bnkobn6hxk2+xvP+7KnF0gzJrZj950ZdUGhdhx6IXdGVdNBdQ18lGtC1DIQ1udWY9tn2lxY0KNu/ADj4/JoWcZ5vTsIMreJsPYnQzowPup8h20DaeeN2cqlZ/4QcXosXNdZfO97aRhpTVU6qlD/8Qs4gq3L341nQlev39EK05Gc463rUgfx8HSXXRM94uzl5qGrmYBjFbiV8cQrqBpmzrDSdmvHgl7+T7UlziNou/OpJOry5lUsedO/ncpcai7mlLuj9nAt/gnMwHfVdmH6uj6Jr3W5olqrBHJ5eQZscyMeP1EvJbocIslfrylPOeUv6X7dTYZTwvWmRAnr7vKCZuNB0MPctt9P/xio8b+J+AYfTsQ3+RH+/KjiUl8knlypJRj4uUeTmY2leqsf76f7j811LyX2bOLpX94TfP7cB3wCay6iJxvT3LSK+HLy08vowWrf7I3de0cMczTdx+uS+13jzF757dQOccl9P6BTE8f/t+Mmos41q1t3iH1Rask5k6O1Buyp73V2PBzS288+7zNHK0knTMcCXZGk3gg27U0uib52l4QAad7KshXR29kkYp/+YLF/3hX0//o/s7M+h10kqq/WJLnefqSxGL22jek1Aa06DGChveU8i3IhriEUPZESqsqkGQ6jYUiv3TLvH6pYuktfjdd0dRSCGbtvJ3Ayr5p5Q2CtqZxF/YXaQF72zY4zXE7T9OpvC77aQdCa5sBKmyFtGeLA+cFVWvpPJg9TXcd/p+PuaDqlRuGssHTrpCF3pWkdIKDRozNJJ6fFZiFgEplHZZxsb4abCdSVU0ZYIGW9rnH521GkJPqtZQfA9rpv23nXRBt5maL7cSP1Uhni7dRvsjTRnb94l7d9Ojt4GP6XPmWlL0qyZDy0rRtDFKPORpKTVpbPX61sdZGjTORVrTEEglByvFTxWGUrz3GzruH038X6WoP++D+GjwSNZfxVMq2NhBmn4umja5evKCizbM8yvWmPWF/P7ZMEX+HjqqspDfixpDBflONMRzCQtXPysO3TGGFqRtIV+XaPLX+Ud5MkNy71wn+g3JoMOX1lHJDwfp1YMxXG+Yo1QxRpOF3lWWDnnUco3ki6LT2DbeK3s6uWZaS5O8VtMYL1vpTo2lNPzjHPrWLZH3764hOc0vFn9e0GbP7muzUqO3vPR0PdmUVfKI+34UP89W+t6sKfn3V7Cz7ZWkl4bneNqm8XQy7hzJXzI8WwaT391V4qTF9swvkItxLnIavsGAve+vx45HdGWFHhtp6gsuNvx3lXyOavJb2g6s6D8jdqTnUJ6N35TjN+tLpR3xXdEbv9MGGki3Jl8jJhlIWlMn0ro1E+nStFt0V9tc6qQ/nDaPvCTKSzykQ5+X8k/es+mIfhPttHdkqf85SM7X4um9W7Sodus6yYOy+RsfXZYyRZdJOxyZMPkGxY4z469bf/JcvTh+rKCZbC+7SWey2rMRf8qpU28nlv9Il83dtZU//eEhHe+nT5dXXBEHt9/F/9ehmbhjnbdRnGSpkRZUmpFBhMdWVFp+506SllERWiypDJWKGuVNSlFUFNlKxTC00qIeU0zub4MWjBk1LdKiSYumshZFeF3zP5xz7s+5rzMgLA+uh20o8kAL0mNtUI6hFPxPBgY3t2DscC/MnJWD+sxoxK1tkgL3qJNzfJM0ta6ER4fmgHZpi8Q8LwiHm4ztrZDJc9Bb5QXPDe1clmZLnRd8xfFECyrS+J5S1o2hzzO8EeStQfWXLChixXOY9NjCNtxaNF3xxoo2C1r2yBt3LOy5ILmRtfLbMG+kuzR74yKRVd7NnUu/pRTNKGnvJR8c6rUVcfOUxY7cTkQUvsPlrD3IT27hBx7GFPvRktQSTv7XXW+FqP332731tIPfP/sIxcs3JfmiAq4/9w01cypKDhqIhFO+sMlSpsKL7fA9fQpri85zmeEAcTZ3DjXUKNCnSHd2PKpFOkbZkvxFFxaPy5aU2xR5V0wk36gOw5052dKa4UWc7rady5z8uGL4fJK5+bOfzkX+tO0zzGcmSqMuKov35z9D/3YLJmkpUNCbTvREapPR6/0onfgFt4umcMkmTRoXthNqmdrkunkVrp8cgfYjHVJU3+1KFGnFgV5noG8fC49ljZxsPQXTnJXIo7IOs3ULi+WTz+L3phaOPNqBI0F+sFswEn8+7dNYZoUUn92B3J88paw/Y3F3+DaUyj0lh+cJLDv4K58uT592SKELzsOM6ORShulcE9q4cCpWv30rJXvVFVd7m1OyYSuUH1dKU3NSsWP1cl7gsgpGe22x88Rubp2lA3f/LmlGzXX4P7TnYNm/qNbtxrOn0/D8nh0HOAfAoVpXLPb1ZeN1BhQY0edJ3yrp/oal3B0voaBhO49aPpKcDtiSrbkhV9yUoLdkNWQO56De7M+6jbFozOr5byPGNvu4omc1Jlm6059Jngi5uggV883Yq6tbCh+hRWEXejH4/Bhx4EIbNx7YSs+iLXgXfeTq1l5o+VjzX1f/klY6KLNx3ib8I1Mg7/Vr8cL+kaRYfREfniZh4Xl9BG6Nh+MO4rWZcmnmhEC0mqviw4/Tca3/BBH9SYHCf9TF66cXMNzfhY+k68LJzo23BFgIT8d2tqy9zNbLFOlA/AA63LmYjd7rYvrrdYgYmIETniuR9quX9MjajLYq7eIp6v5cuLkUIbMrkOYpR2zILq76RUZT9lmJpVc3IFrjLY+P6EebS4PxJsIBba47+EJCEHQe9yPlV3p49+sxGA9tx7XLl7DeOBhTas9Jfvu6ETHjC3cE1HJnlpLwv+8C23I14bxfifb6JOGD53EcSMvHe3M98YePI64EKcFteSfrlGuQZpsSz9iygvNc43F5TVOxed5qKeBTEopiCrilSI6URZZcG9KDC4OXsEGdHDP6/GsXlwzXuqM8S7eLfXuUUPjXYU49ZYBJt9zZep4TNMrjJVm0mcjL8uYXGimo/C4ED/SV6WVGEX+Zmgmfn1Iwan4YK682ECcHebGVtrmwT/keUb/vhcmazVjudgnnMvOlGeZjoD86FT8GmWJoRhv332Iu5D7a4lSHoXAZmIK4yFRcq0jEhDwHdI29wdENs3nOlFCMHDYXg5QPozkshD3rFuBKeCj6l+XyymHTYZhqJSpzFChq4lU42h/GvvOhMHuoAm3NDVLzgH7iUmc/mn42g6cYxbGylxX5LTrDk15aCCV7M7JIGiukXjOxdMkJ3qe0D/4LnqN+3ld+FacpNDZ/5HtaByFUjBH9WyhMbrRi/AS5dMXmA+/+WVuUt0fwHd8yvnDAmJyy5VJMdx1H2htQWuZWxOzWEMUNqlQZtQOu5mOoN7wXWTXheHPSgFYMD0ewbjfX5H5iXYtuhK63ETPv9rCJnxqlRZTyZ5PNfDZLDYqx2qLk2yOoOqMiYiMt6E7YNsw2tKGym2o0rn6jZHhiPbY8a8cDh1oeWaggxj1UI7/js/HdqGwMmLgU87/eZ69X25ARZkvhuTvZyrK/+MEuHHV7NkkvZofja8R8xHy9B89hKTw+dAsrptqKD3NeYljUE65XeQnXVUdhrWFGSRZNuLLjffFKxyXItDPFPQdTqC4wxaz5vvAqOIo3xYakrucFx1U+3BBtScsfX4Oy1ivcv2yKh5pXpfRXZXDgQE7UWYitWy3IWmEO6mwX4manhujiHTjIA5G3+arkvaEXk02M6IbRJHYfMQKbMs35yYIWxPX1mo8aj4sDf9MXz1QYv5QZiDcJujC7HYGCP/rY4N1Aqtjggt9vdPNd1WaueXSNnwSsZduEN7gy9xiPyP/MRqME1via4rZeOtymm1CMtzXNq7qE+y7qqNwjcPhQc19GfUN3q2rQ9t6Ykp08+PXbY+jvakz59xrg0Mc4DypeSeb+6nQxXlHcbldHi7wTf+cuQpH2a+nb42pUdFSdBia0wtfRnLQuq9OQ0/3F+kp1OjTSTBTXjKX9f6tTfOIglJ8rwKXzyqJ521zo1yuL68omdP+1AdXflvGN1Qa0WPNneExyh+mt0SjvSZMeXJGJgRljxJNAfcqsbpVK1PVpU1AOZKkqIvSiCZlNf4epJwfRsWf/EwFThyDtUdC0ye3ucPnmXx5aoYG0lw3S3snHsXtiuuTkpiZeJeoJe+evGGQ1CdZxHvAwt2Vv5VKkuUQhbp0pfQizpM6nHlizNgolyRpC+uKB6iWlcDxpBVdnP26/Zkrb8zJxJEGDHn183scV1qjZ+AE/W5tzde8TDsxVJCv3OUKu3YEvP8Sz3ZoleBSTBZ2Jjdgp1xQ5SY2885wZacUcl/btMhIZVVkw3v+JsyyzsbP1K38MUGWZvBG6BbuxrSQYZztV2Tc/inf9Mpiag8fhiK0K6YWPw4pbidi+2ZAXxIzD6bzrkoXeaKFTZcZOSn4wWtsESz8D4bxMJo5UjaXCLFWxIVJGK/anS7KpQ0ijtgmGfptgN28I1W6R0ZC7MnpTK6MMfW3yLdjOOdeXIdrOgV2q+lFO5XjULdEkeWEz5p2zobqobJyv7eKuEls6/v1oEdvbjKahFuJ961YRbNqCnIJs3DNo5dy8RiiOMaI1/qbUYagoFH1iGBHtSJ/ZJB3dNpSuTS+V8oe3wnSpF0KeMlKShtLh3UqkGOkFTTcV+mG5AY1QVaWGlnZOOqMgzB7koNTUmH4SRtR+SJ26xnvDIWIM/R/KJ5ZT"


def _apply_corr(tab, ncorr, b64):
    if ncorr == 0:
        return tab
    raw = zlib.decompress(base64.b64decode(b64))
    idx = np.frombuffer(raw[: 4 * ncorr], dtype=np.uint32)
    vals = np.frombuffer(raw[4 * ncorr:], dtype=np.uint32)
    flat = tab.ravel().view(np.uint32).copy()
    flat[idx.astype(np.int64)] = vals
    return flat.view(np.float32).reshape(tab.shape)


def _rope_tables():
    ks = (np.arange(0, 128, 2, dtype=np.float32) / np.float32(128)).astype(np.float64)
    ivf = np.float32(1.0) / (np.float64(500000.0) ** ks).astype(np.float32)
    t = np.arange(T, dtype=np.float32)
    freqs = (t[:, None] * ivf[None, :]).astype(np.float32)
    f64 = freqs.astype(np.float64)
    cos = _apply_corr(np.cos(f64).astype(np.float32), _COS_NCORR, _COS_CORR_B64)
    sin = _apply_corr(np.sin(f64).astype(np.float32), _SIN_NCORR, _SIN_CORR_B64)
    return cos, sin


class S:

    def __init__(self, h):
        self.h = h
        self.n = 0

    def bump(self, k=1):
        self.n += k
        return self.n


def build_nc():
    nc = bass.Bass("TRN2", target_bir_lowering=False)
    dram = lambda n, sh, dt, kind="Internal": nc.dram_tensor(n, sh, dt, kind=kind).ap()
    I, O = "ExternalInput", "ExternalOutput"

    xT = dram("xT", [D, T], F32, I)
    xtok = dram("xtok", [T, D], F32, I)
    xcols = dram("xcols", [T, 1024], F32, I)
    wq = dram("wq", [D, 1024], F32, I)
    wkv = dram("wkv", [D, 512], F32, I)
    wo = dram("wo", [D, 1024], F32, I)
    wf = dram("wf", [D, 68], F32, I)
    wbs = dram("wbs", [64, 1024], F32, I)
    cosT = dram("cosT", [128, T], F32, I)
    sinN = dram("sinN", [128, T], F32, I)
    dmask = dram("dmask", [128, 512], F32, I)
    ones128 = dram("ones128", [128, 1], F32, I)
    epsc_in = dram("epsc", [128, 1], F32, I)
    ident_in = dram("ident", [128, 128], F32, I)
    y = dram("y", [T, 1024], F32, O)

    qhat_h = dram("qhat_h", [NH_L, 128, T], BF16)
    qhat_l = dram("qhat_l", [NH_L, 128, T], BF16)
    khat_h = dram("khat_h", [NKV_L, 128, T], BF16)
    khat_l = dram("khat_l", [NKV_L, 128, T], BF16)
    vtok_h = dram("vtok_h", [NKV_L, T, 128], BF16)
    vtok_l = dram("vtok_l", [NKV_L, T, 128], BF16)
    rd = dram("rd", [NTT, 2, TTW], F32)
    rdb = dram("rdb", [NKV_L * NQT, 512], F32)
    ag_in = dram("ag_in", [1024, T], F32)
    ag_out = dram("ag_out", [4096, T], F32)
    gdram = dram("gdram", [4, T], F32)

    ctx = ExitStack()
    sem = lambda n: S(ctx.enter_context(nc.semaphore(n)))
    sbufO = lambda n, sh, dt: ctx.enter_context(nc.sbuf_tensor(n, sh, dt))
    ctxA = ExitStack()
    sbuf = lambda n, sh, dt: ctxA.enter_context(nc.sbuf_tensor(n, sh, dt))
    psum = lambda n, sh: ctxA.enter_context(nc.psum_tensor(n, sh, F32))

    ops = {k: [] for k in ("sync", "scalar", "vector", "tensor", "gpsimd")}

    def em(eng, fn):
        ops[eng].append(fn)

    MIDL = sbufO("MIDL", [68, T], F32)
    ONE1 = sbufO("ONE1", [128, 1], F32)
    XNH = sbuf("XNH", [128, NFC, TTW], BF16)
    XNL = sbuf("XNL", [128, NFC, TTW], BF16)
    NW = 4
    SWF = [sbuf(f"SWF{i}", [128, 512], F32) for i in range(NW)]
    SWH = [sbuf(f"SWH{i}", [128, 512], BF16) for i in range(NW)]
    SWL = [sbuf(f"SWL{i}", [128, 512], BF16) for i in range(NW)]
    STX = [sbuf(f"STX{i}", [128, 2048], F32) for i in range(2)]
    SDUM = sbuf("SDUM", [128, 2048], F32)
    SXF = [sbuf(f"SXF{i}", [128, TTW], F32) for i in range(4)]
    SSUMa = sbuf("SSUMa", [128, 4], F32)
    SSUMb = sbuf("SSUMb", [128, 4], F32)
    SSQa = sbuf("SSQa", [128, 4], F32)
    SSQb = sbuf("SSQb", [128, 4], F32)
    SM = sbuf("SM", [128, 4], F32)
    SNM2 = sbuf("SNM2", [128, 4], F32)
    SEV = sbuf("SEV", [128, 4], F32)
    SR = sbuf("SR", [128, 4], F32)
    SMR = sbuf("SMR", [128, 4], F32)
    RBC = [sbuf(f"RBC{i}", [128, TTW], F32) for i in range(2)]
    MRBC = [sbuf(f"MRBC{i}", [128, TTW], F32) for i in range(2)]
    T1 = sbuf("T1", [128, TTW], F32)
    XN32 = sbuf("XN32", [128, TTW], F32)
    CST = [sbuf(f"CST{i}", [128, TTW], F32) for i in range(2)]
    SNT = [sbuf(f"SNT{i}", [128, TTW], F32) for i in range(2)]
    QSB = [sbuf(f"QSB{i}", [128, TTW], F32) for i in range(2)]
    QSW = [sbuf(f"QSW{i}", [128, TTW], F32) for i in range(2)]
    RT2 = sbuf("RT2", [128, TTW], F32)
    RT3 = sbuf("RT3", [128, TTW], F32)
    QH8 = [sbuf(f"QH8{i}", [128, TTW], BF16) for i in range(2)]
    QL8 = [sbuf(f"QL8{i}", [128, TTW], BF16) for i in range(2)]
    VSB = [sbuf(f"VSB{i}", [128, TTW], F32) for i in range(2)]
    VTH = sbuf("VTH", [128, 4, 128], BF16)
    VTL = sbuf("VTL", [128, 4, 128], BF16)
    EPSC = sbuf("EPSC", [128, 1], F32)
    IDT = sbuf("IDT", [128, 128], F32)

    PW = [psum(f"PW{i}", [128, TTW]) for i in range(8)]
    WAVES = [["ffn"], ["q0", "q1", "q2", "q3"], ["q4", "q5", "q6", "q7"],
             ["k0", "k1", "v0", "v1"]]
    WBANK = {"ffn": 0,
             "q0": 4, "q1": 5, "q2": 6, "q3": 7,
             "q4": 0, "q5": 1, "q6": 2, "q7": 3,
             "k0": 4, "k1": 5, "v0": 6, "v1": 7}

    s_xstat = [sem(f"s_xstat{i}") for i in range(2)]
    s_xf = [sem(f"s_xf{i}") for i in range(4)]
    s_w = [sem(f"s_w{i}") for i in range(NW)]
    s_ct = sem("s_ct")
    s_A = sem("s_A")
    s_V = sem("s_V")
    s_Tm = sem("s_Tm")
    s_G = sem("s_G")
    s_mr = sem("s_mr")
    s_qst = sem("s_qst")

    bank_free = {}

    def W(eng_wait_list, engobj):
        for (s, m) in eng_wait_list:
            if m > 0:
                engobj.wait_ge(s.h, m)

    last_stats_gate = [0]
    xn_done = {}
    ct_done = {}
    ct_mark = {}
    xn_marks = {}

    def emit_stats(tt):
        gate = last_stats_gate[0]
        accs = [(SSUMa, SSQa), (SSUMb, SSQb)]
        stx_free = {}
        for sub in range(4):
            for hf in range(2):
                bi = (sub * 2 + hf) % 2
                ld = s_xstat[bi].bump(16)
                fr = stx_free.get(bi, 0)
                def f(sync, tt=tt, sub=sub, hf=hf, bi=bi, fr=fr):
                    if fr:
                        sync.wait_ge(s_A.h, fr)
                    t0 = tt * TTW + sub * 128
                    sync.dma_start(out=STX[bi][:],
                                   in_=xtok[t0:t0 + 128, hf * 2048:(hf + 1) * 2048]
                                   ).then_inc(s_xstat[bi].h, 16)
                em("sync", f)
                sacc, qacc = accs[hf]
                m1 = s_A.bump(2)
                stx_free[bi] = m1
                def f(scalar, bi=bi, ld=ld, sub=sub, sacc=sacc, qacc=qacc,
                      gate=(gate if (sub == 0 and hf == 0) else 0)):
                    scalar.wait_ge(s_xstat[bi].h, ld)
                    if gate:
                        scalar.wait_ge(s_mr.h, gate)
                    scalar.activation(out=SDUM[:], in_=STX[bi][:], func=AF.Copy,
                                      accum_out=sacc[:, sub:sub + 1]).then_inc(s_A.h, 1)
                    scalar.activation(out=SDUM[:], in_=STX[bi][:], func=AF.Square,
                                      accum_out=qacc[:, sub:sub + 1]).then_inc(s_A.h, 1)
                em("scalar", f)
        macc = s_A.n
        m_half = s_V.bump(2)
        def f(vector, macc=macc):
            vector.wait_ge(s_A.h, macc)
            vector.tensor_add(out=SSUMa[:], in0=SSUMa[:], in1=SSUMb[:]).then_inc(s_V.h, 1)
            vector.tensor_add(out=SSQa[:], in0=SSQa[:], in1=SSQb[:]).then_inc(s_V.h, 1)
        em("vector", f)
        m_m = s_A.bump(1)
        def f(scalar, m_half=m_half):
            scalar.wait_ge(s_V.h, m_half)
            scalar.activation(out=SM[:], in_=SSUMa[:], func=AF.Copy,
                              scale=1.0 / D).then_inc(s_A.h, 1)
        em("scalar", f)
        m_nm2 = s_V.bump(2)
        def f(vector, m_m=m_m, m_nm2=m_nm2):
            vector.wait_ge(s_A.h, m_m)
            vector.scalar_tensor_tensor(out=SNM2[:], in0=SM[:], scalar=-1.0,
                                        in1=SM[:], op0=ALU.mult,
                                        op1=ALU.mult).then_inc(s_V.h, 1)
            vector.wait_ge(s_V.h, m_nm2 - 1)
            vector.scalar_tensor_tensor(out=SEV[:], in0=SSQa[:], scalar=1.0 / D,
                                        in1=SNM2[:], op0=ALU.mult,
                                        op1=ALU.add).then_inc(s_V.h, 1)
        em("vector", f)
        m_r = s_A.bump(1)
        def f(scalar, m_nm2=m_nm2):
            scalar.wait_ge(s_V.h, m_nm2)
            scalar.wait_ge(s_ct.h, 48)
            scalar.activation(out=SSQb[:], in_=SEV[:], func=AF.Sqrt,
                              bias=EPSC[:]).then_inc(s_A.h, 1)
        em("scalar", f)
        m_mr = s_V.bump(2)
        def f(vector, m_r=m_r, m_mr=m_mr):
            vector.wait_ge(s_A.h, m_r)
            vector.reciprocal(out=SR[:], in_=SSQb[:]).then_inc(s_V.h, 1)
            vector.wait_ge(s_V.h, m_mr - 1)
            vector.tensor_mul(out=SMR[:], in0=SM[:], in1=SR[:]).then_inc(s_V.h, 1)
        em("vector", f)
        xd = xn_done.get(tt - 2, 0)
        m_st = s_mr.bump(16 * 8)
        m_bc = s_mr.bump(32)
        def f(gpsimd, tt=tt, m_mr=m_mr, m_st=m_st, xd=xd):
            gpsimd.wait_ge(s_V.h, m_mr)
            if xd:
                gpsimd.wait_ge(s_V.h, xd)
            for sub in range(4):
                gpsimd.dma_start(out=rd[tt, 0, sub * 128:(sub + 1) * 128],
                                 in_=SR[:, sub:sub + 1]).then_inc(s_mr.h, 16)
                gpsimd.dma_start(out=rd[tt, 1, sub * 128:(sub + 1) * 128],
                                 in_=SMR[:, sub:sub + 1]).then_inc(s_mr.h, 16)
            gpsimd.wait_ge(s_mr.h, m_st)
            gpsimd.dma_start(out=RBC[tt % 2][:],
                             in_=rd[tt, 0:1, :].to_broadcast([128, TTW])).then_inc(s_mr.h, 16)
            gpsimd.dma_start(out=MRBC[tt % 2][:],
                             in_=rd[tt, 1:2, :].to_broadcast([128, TTW])).then_inc(s_mr.h, 16)
        em("gpsimd", f)
        last_stats_gate[0] = m_bc
        return m_bc

    rbc_mark = {}

    sxf_free = {}

    def emit_xn_chunk(tt, fc, gate_pe):
        bi = fc % 4
        ld = s_xf[bi].bump(16)
        fr = sxf_free.get(bi, 0)
        def f(sync, tt=tt, fc=fc, bi=bi, fr=fr):
            if fr:
                sync.wait_ge(s_V.h, fr)
            sync.dma_start(out=SXF[bi][:],
                           in_=xT[fc * 128:(fc + 1) * 128, tt * TTW:(tt + 1) * TTW]
                           ).then_inc(s_xf[bi].h, 16)
        em("sync", f)
        m32 = s_V.bump(2)
        sxf_free[bi] = m32
        rb = rbc_mark[tt]
        def f(vector, tt=tt, fc=fc, bi=bi, gate_pe=gate_pe, rb=rb):
            vector.wait_ge(s_xf[bi].h, ld)
            vector.wait_ge(s_mr.h, rb)
            if gate_pe:
                vector.wait_ge(s_Tm.h, gate_pe)
            vector.tensor_mul(out=T1[:], in0=SXF[bi][:], in1=RBC[tt % 2][:]).then_inc(s_V.h, 1)
            vector.tensor_sub(out=XN32[:], in0=T1[:], in1=MRBC[tt % 2][:]).then_inc(s_V.h, 1)
        em("vector", f)
        mh = s_A.bump(1)
        def f(scalar, fc=fc, m32=m32):
            scalar.wait_ge(s_V.h, m32)
            scalar.activation(out=XNH[:, fc, :], in_=XN32[:], func=AF.Copy).then_inc(s_A.h, 1)
        em("scalar", f)
        ml = s_V.bump(1)
        def f(vector, fc=fc, mh=mh):
            vector.wait_ge(s_A.h, mh)
            vector.tensor_tensor(out=XNL[:, fc, :], in0=XN32[:], in1=XNH[:, fc, :],
                                 op=ALU.subtract).then_inc(s_V.h, 1)
        em("vector", f)
        xn_marks[(tt, fc)] = (mh, ml)
        if fc == NFC - 1:
            xn_done[tt] = ml

    def emit_ct(tt):
        gate = ct_done.get(tt - 2, 0)
        s_ct.bump(16)
        m = s_ct.bump(16)
        def f(sync, tt=tt, gate=gate):
            if gate:
                sync.wait_ge(s_V.h, gate)
            t0 = tt * TTW
            sync.dma_start(out=CST[tt % 2][:], in_=cosT[:, t0:t0 + TTW]).then_inc(s_ct.h, 16)
            sync.dma_start(out=SNT[tt % 2][:], in_=sinN[:, t0:t0 + TTW]).then_inc(s_ct.h, 16)
        em("sync", f)
        ct_mark[tt] = m

    def wave_src(wv, fc):
        if wv == 0:
            return wf[fc * 128:(fc + 1) * 128, :], 68
        if wv == 1:
            return wq[fc * 128:(fc + 1) * 128, 0:512], 512
        if wv == 2:
            return wq[fc * 128:(fc + 1) * 128, 512:1024], 512
        return wkv[fc * 128:(fc + 1) * 128, 0:512], 512

    qsb_free = {}
    qsw_free = {}
    qh8_free = {}
    vsb_free = {}
    vth_free = [0]
    vt_banks_free = {}
    ffn_evac_mark = {}

    def emit_rope_evac(tt, name, bank, done_mark, par):
        is_q = name[0] == "q"
        h = int(name[1])
        scale = ISQ if is_q else 1.0
        m_qsb = s_A.bump(1)
        fr = qsb_free.get(par, 0)
        def f(scalar, bank=bank, par=par, done_mark=done_mark, fr=fr):
            scalar.wait_ge(s_Tm.h, done_mark)
            if fr:
                scalar.wait_ge(s_G.h, fr)
            scalar.activation(out=QSB[par][:], in_=PW[bank][:], func=AF.Copy).then_inc(s_A.h, 1)
        em("scalar", f)
        bank_free[bank] = (s_A, m_qsb)
        m_sw = s_G.bump(32)
        qsb_free[par] = m_sw
        frw = qsw_free.get(par, 0)
        def f(gpsimd, par=par, m_qsb=m_qsb, frw=frw):
            gpsimd.wait_ge(s_A.h, m_qsb)
            if frw:
                gpsimd.wait_ge(s_V.h, frw)
            gpsimd.dma_start(out=QSW[par][0:64, :], in_=QSB[par][64:128, :]).then_inc(s_G.h, 16)
            gpsimd.dma_start(out=QSW[par][64:128, :], in_=QSB[par][0:64, :]).then_inc(s_G.h, 16)
        em("gpsimd", f)
        m_rope = s_V.bump(3)
        qsw_free[par] = m_rope
        ctm = ct_mark[tt]
        def f(vector, tt=tt, par=par, m_sw=m_sw, ctm=ctm):
            vector.wait_ge(s_G.h, m_sw)
            vector.wait_ge(s_ct.h, ctm)
            vector.tensor_mul(out=RT2[:], in0=QSB[par][:], in1=CST[tt % 2][:]).then_inc(s_V.h, 1)
            vector.tensor_mul(out=RT3[:], in0=QSW[par][:], in1=SNT[tt % 2][:]).then_inc(s_V.h, 1)
            vector.tensor_add(out=RT2[:], in0=RT2[:], in1=RT3[:]).then_inc(s_V.h, 1)
        em("vector", f)
        ct_done[tt] = m_rope
        m_hi = s_A.bump(1)
        frh = qh8_free.get(par, 0)
        def f(scalar, par=par, scale=scale, m_rope=m_rope, frh=frh):
            scalar.wait_ge(s_V.h, m_rope)
            if frh:
                scalar.wait_ge(s_qst.h, frh)
            scalar.activation(out=QH8[par][:], in_=RT2[:], func=AF.Copy,
                              scale=scale).then_inc(s_A.h, 1)
        em("scalar", f)
        m_lo = s_V.bump(1)
        def f(vector, par=par, scale=scale, m_hi=m_hi):
            vector.wait_ge(s_A.h, m_hi)
            vector.scalar_tensor_tensor(out=QL8[par][:], in0=RT2[:], scalar=scale,
                                        in1=QH8[par][:], op0=ALU.mult,
                                        op1=ALU.subtract).then_inc(s_V.h, 1)
        em("vector", f)
        m_store = s_qst.bump(32)
        qh8_free[par] = m_store
        def f(gpsimd, tt=tt, par=par, is_q=is_q, h=h, m_lo=m_lo):
            gpsimd.wait_ge(s_V.h, m_lo)
            t0 = tt * TTW
            dh = qhat_h[h, :, t0:t0 + TTW] if is_q else khat_h[h, :, t0:t0 + TTW]
            dl = qhat_l[h, :, t0:t0 + TTW] if is_q else khat_l[h, :, t0:t0 + TTW]
            gpsimd.dma_start(out=dh, in_=QH8[par][:]).then_inc(s_qst.h, 16)
            gpsimd.dma_start(out=dl, in_=QL8[par][:]).then_inc(s_qst.h, 16)
        em("gpsimd", f)

    s_ct.bump(48)
    def f(sync):
        sync.dma_start(out=IDT[:], in_=ident_in).then_inc(s_ct.h, 16)
        sync.dma_start(out=ONE1[:], in_=ones128).then_inc(s_ct.h, 16)
        sync.dma_start(out=EPSC[:], in_=epsc_in).then_inc(s_ct.h, 16)
    em("sync", f)
    m_idt = s_ct.n

    if DBG:
        dbg_st = dram("dbg_st", [4, 128, 4], F32, O)
    rbc_mark[0] = emit_stats(0)
    if DBG:
        _mmr0 = s_V.n
        _mdmp = s_qst.bump(64)
        def f(gpsimd, _mmr0=_mmr0):
            gpsimd.wait_ge(s_V.h, _mmr0)
            gpsimd.dma_start(out=dbg_st[0], in_=SSUMa[:]).then_inc(s_qst.h, 16)
            gpsimd.dma_start(out=dbg_st[1], in_=SSQa[:]).then_inc(s_qst.h, 16)
            gpsimd.dma_start(out=dbg_st[2], in_=SM[:]).then_inc(s_qst.h, 16)
            gpsimd.dma_start(out=dbg_st[3], in_=SEV[:]).then_inc(s_qst.h, 16)
        em("gpsimd", f)
    emit_ct(0)
    for fc in range(NFC):
        emit_xn_chunk(0, fc, 0)

    swf_free = {}
    mm_mark = {}
    wstep = 0
    for tt in range(NTT):
        if tt + 1 < NTT:
            emit_ct(tt + 1)
        for wv, wave in enumerate(WAVES):
            for fc in range(NFC):
                bi = wstep % NW
                names = list(wave)
                widths = [68] if wv == 0 else [128] * 4
                wwidth = sum(widths)
                mld = s_w[bi].bump(16)
                frb = swf_free.get(bi, 0)
                def f(sync, wv=wv, fc=fc, bi=bi, frb=frb, wwidth=wwidth):
                    if frb:
                        sync.wait_ge(s_Tm.h, frb)
                    src, _ = wave_src(wv, fc)
                    sync.dma_start(out=SWF[bi][:, :wwidth],
                                   in_=src).then_inc(s_w[bi].h, 16)
                em("sync", f)
                m_wh = s_A.bump(1)
                def f(scalar, bi=bi, mld=mld, wwidth=wwidth):
                    scalar.wait_ge(s_w[bi].h, mld)
                    scalar.activation(out=SWH[bi][:, :wwidth], in_=SWF[bi][:, :wwidth],
                                      func=AF.Copy).then_inc(s_A.h, 1)
                em("scalar", f)
                m_wl = s_V.bump(1)
                def f(vector, bi=bi, m_wh=m_wh, wwidth=wwidth):
                    vector.wait_ge(s_A.h, m_wh)
                    vector.tensor_tensor(out=SWL[bi][:, :wwidth], in0=SWF[bi][:, :wwidth],
                                         in1=SWH[bi][:, :wwidth],
                                         op=ALU.subtract).then_inc(s_V.h, 1)
                em("vector", f)
                mh, ml = xn_marks[(tt, fc)]
                bfw = [bank_free.get(WBANK[nm]) for nm in names] if fc == 0 else []
                m_mm = s_Tm.bump(1)
                mm_mark[wstep] = m_mm
                def f(tensor, tt=tt, fc=fc, bi=bi, names=names, widths=widths,
                      m_wl=m_wl, mh=mh, ml=ml, bfw=bfw):
                    tensor.wait_ge(s_V.h, m_wl)
                    tensor.wait_ge(s_A.h, mh)
                    tensor.wait_ge(s_V.h, ml)
                    for bf in bfw:
                        if bf is not None and bf[1] > 0:
                            tensor.wait_ge(bf[0].h, bf[1])
                    off = 0
                    mvh = XNH[:, fc, :]
                    mvl = XNL[:, fc, :]
                    li = None
                    for nm, width in zip(names, widths):
                        wh = SWH[bi][:, off:off + width]
                        wl = SWL[bi][:, off:off + width]
                        pb = PW[WBANK[nm]]
                        outp = pb[0:width, :]
                        tensor.matmul(outp, wh, mvh, start=(fc == 0), stop=False)
                        tensor.matmul(outp, wh, mvl, start=False, stop=False)
                        li = tensor.matmul(outp, wl, mvh, start=False,
                                           stop=(fc == NFC - 1))
                        off += width
                    li.then_inc(s_Tm.h, 1)
                em("tensor", f)
                swf_free[bi] = m_mm
                if wv == 3 and tt + 1 < NTT and fc >= 3:
                    emit_xn_chunk(tt + 1, fc - 3, mm_mark[wstep - 3])
                wstep += 1
            if wv == 3 and tt + 1 < NTT:
                for fcl in (NFC - 3, NFC - 2, NFC - 1):
                    emit_xn_chunk(tt + 1, fcl, mm_mark[wstep - NFC + fcl])
            done = mm_mark[wstep - 1]
            if wave == ["ffn"]:
                m_ffn = s_A.bump(1)
                def f(scalar, tt=tt, done=done):
                    scalar.wait_ge(s_Tm.h, done)
                    scalar.activation(out=MIDL[:, tt * TTW:(tt + 1) * TTW],
                                      in_=PW[0][0:68, :], func=AF.Copy).then_inc(s_A.h, 1)
                em("scalar", f)
                bank_free[0] = (s_A, m_ffn)
                ffn_evac_mark[tt] = m_ffn
            else:
                for i, nm in enumerate(wave):
                    if nm[0] in "qk":
                        emit_rope_evac(tt, nm, WBANK[nm], done, par=i % 2)
                    else:
                        kvi = int(nm[1])
                        m_vsb = s_A.bump(1)
                        frv = vsb_free.get(kvi, 0)
                        def f(scalar, bank=WBANK[nm], kvi=kvi, done=done, frv=frv):
                            scalar.wait_ge(s_Tm.h, done)
                            if frv:
                                scalar.wait_ge(s_Tm.h, frv)
                            scalar.activation(out=VSB[kvi][:], in_=PW[bank][:],
                                              func=AF.Copy).then_inc(s_A.h, 1)
                        em("scalar", f)
                        bank_free[WBANK[nm]] = (s_A, m_vsb)
                        mk_vsb = m_vsb
                        tb = 1 + kvi
                        btf = bank_free.get(tb)
                        m_vt = s_Tm.bump(1)
                        def f(tensor, kvi=kvi, tb=tb, mk_vsb=mk_vsb, btf=btf):
                            tensor.wait_ge(s_A.h, mk_vsb)
                            if btf is not None and btf[1] > 0:
                                tensor.wait_ge(btf[0].h, btf[1])
                            li = None
                            for sres in range(4):
                                li = tensor.matmul(PW[tb][:, sres * 128:(sres + 1) * 128],
                                                   VSB[kvi][:, sres * 128:(sres + 1) * 128],
                                                   IDT[:], is_transpose=True,
                                                   start=(sres == 0), stop=(sres == 3))
                            li.then_inc(s_Tm.h, 1)
                        em("tensor", f)
                        vsb_free[kvi] = m_vt
                        m_vth = s_A.bump(1)
                        frh = vth_free[0]
                        def f(scalar, kvi=kvi, tb=tb, m_vt=m_vt, frh=frh):
                            scalar.wait_ge(s_Tm.h, m_vt)
                            if frh:
                                scalar.wait_ge(s_qst.h, frh)
                            scalar.activation(out=VTH[:], in_=PW[tb][:],
                                              func=AF.Copy).then_inc(s_A.h, 1)
                        em("scalar", f)
                        m_vtl = s_V.bump(1)
                        def f(vector, kvi=kvi, tb=tb, m_vth=m_vth):
                            vector.wait_ge(s_A.h, m_vth)
                            vector.tensor_tensor(out=VTL[:], in0=PW[tb][:], in1=VTH[:],
                                                 op=ALU.subtract).then_inc(s_V.h, 1)
                        em("vector", f)
                        bank_free[tb] = (s_V, m_vtl)
                        m_vst = s_qst.bump(32)
                        vth_free[0] = m_vst
                        def f(gpsimd, tt=tt, kvi=kvi, m_vtl=m_vtl):
                            gpsimd.wait_ge(s_V.h, m_vtl)
                            t0 = tt * TTW
                            dh = vtok_h[kvi, t0:t0 + TTW, :].rearrange("(s p) d -> p s d", p=128)
                            dl = vtok_l[kvi, t0:t0 + TTW, :].rearrange("(s p) d -> p s d", p=128)
                            gpsimd.dma_start(out=dh, in_=VTH[:]).then_inc(s_qst.h, 16)
                            gpsimd.dma_start(out=dl, in_=VTL[:]).then_inc(s_qst.h, 16)
                        em("gpsimd", f)
            if wv == 2 and tt + 1 < NTT:
                rbc_mark[tt + 1] = emit_stats(tt + 1)

    phaseA_stores = s_qst.n
    phaseA_midl = ffn_evac_mark[NTT - 1]


    ctxA.close()
    ctxB = ExitStack()
    sbufB = lambda n, sh, dt: ctxB.enter_context(nc.sbuf_tensor(n, sh, dt))
    psumB = lambda n, sh: ctxB.enter_context(nc.psum_tensor(n, sh, F32))
    KH = sbufB("KH", [128, T], BF16)
    KL = sbufB("KL", [128, T], BF16)
    VHt = sbufB("VHt", [128, NQT, 128], BF16)
    VLt = sbufB("VLt", [128, NQT, 128], BF16)
    SQH = [sbufB(f"SQH{i}", [128, 4, 128], BF16) for i in range(2)]
    SQL = [sbufB(f"SQL{i}", [128, 4, 128], BF16) for i in range(2)]
    P32 = [sbufB(f"P32{i}", [128, 512], F32) for i in range(2)]
    PHt = [sbufB(f"PHt{i}", [128, 512], BF16) for i in range(2)]
    PLt = [sbufB(f"PLt{i}", [128, 512], BF16) for i in range(2)]
    SACC = [sbufB(f"SACC{i}", [128, 512], F32) for i in range(2)]
    DMK = sbufB("DMK", [128, 512], F32)
    RRW = [sbufB(f"RRW{i}", [1, 512], F32) for i in range(2)]
    RBCB = [sbufB(f"RBCB{i}", [128, 512], F32) for i in range(2)]
    CTXS = [sbufB(f"CTXS{i}", [128, 512], F32) for i in range(2)]
    PS_S = [psumB(f"PS_S{i}", [128, 512]) for i in range(2)]
    PS_C = [psumB(f"PS_C{i}", [128, 512]) for i in range(2)]
    PS_SUM = [psumB(f"PS_SUM{i}", [1, 512]) for i in range(2)]
    s_kv = sem("s_kv")
    s_qb = [sem(f"s_qb{i}") for i in range(2)]
    s_TB = sem("s_TB")
    s_BA = sem("s_BA")
    s_BV = sem("s_BV")
    s_rb = sem("s_rb")
    s_cst = sem("s_cst")

    m_dmk = s_kv.bump(16)
    def f(sync):
        sync.dma_start(out=DMK[:], in_=dmask).then_inc(s_kv.h, 16)
    em("sync", f)

    last_exp_bank = {}
    last_pread_bank = {}
    last_ctx_bank = {}
    scores_last = {}
    sum_mark = {}
    recip_mark = {}
    evac_mark = {}
    cst_mark = {}
    rb_mark = {}

    for kv in range(NKV_L):
        kv_gate_pe = s_TB.n if kv > 0 else 0
        m_kv = s_kv.bump(16 * 4)
        def f(sync, kv=kv, kv_gate_pe=kv_gate_pe, pa=phaseA_stores):
            if kv == 0:
                sync.wait_ge(s_qst.h, pa)
            else:
                sync.wait_ge(s_TB.h, kv_gate_pe)
            sync.dma_start(out=KH[:], in_=khat_h[kv]).then_inc(s_kv.h, 16)
            sync.dma_start(out=KL[:], in_=khat_l[kv]).then_inc(s_kv.h, 16)
            sync.dma_start(out=VHt[:],
                           in_=vtok_h[kv].rearrange("(c p) d -> p c d", p=128)
                           ).then_inc(s_kv.h, 16)
            sync.dma_start(out=VLt[:],
                           in_=vtok_l[kv].rearrange("(c p) d -> p c d", p=128)
                           ).then_inc(s_kv.h, 16)
        em("sync", f)
        for t in range(NQT):
            li = kv * NQT + t
            par = t % 2
            mq = s_qb[par].bump(32)
            qgate = scores_last.get(li - 2, 0)
            def f(sync, kv=kv, t=t, par=par, qgate=qgate, m_kv=m_kv):
                sync.wait_ge(s_kv.h, m_kv)
                if qgate:
                    sync.wait_ge(s_TB.h, qgate)
                srch = qhat_h[4 * kv:4 * kv + 4, :, 128 * t:128 * (t + 1)].transpose([1, 0, 2])
                srcl = qhat_l[4 * kv:4 * kv + 4, :, 128 * t:128 * (t + 1)].transpose([1, 0, 2])
                sync.dma_start(out=SQH[par][:], in_=srch).then_inc(s_qb[par].h, 16)
                sync.dma_start(out=SQL[par][:], in_=srcl).then_inc(s_qb[par].h, 16)
            em("sync", f)
            pend_cx = None

            def emit_ctx(j, sb, m_pl, t=t, par=par, li=li):
                m_cx = s_TB.bump(1)
                ceg = evac_mark.get(li - 2, 0)
                def f(tensor, j=j, t=t, sb=sb, par=par, m_pl=m_pl, ceg=ceg):
                    tensor.wait_ge(s_BV.h, m_pl)
                    if j == 0 and ceg:
                        tensor.wait_ge(s_BV.h, ceg)
                    vh = VHt[:, j, :]
                    vl = VLt[:, j, :]
                    tensor.matmul(PS_C[par][:], vh, PHt[sb][:], start=(j == 0), stop=False)
                    tensor.matmul(PS_C[par][:], vh, PLt[sb][:], start=False, stop=False)
                    tensor.matmul(PS_C[par][:], vl, PHt[sb][:], start=False,
                                  stop=(j == t)).then_inc(s_TB.h, 1)
                em("tensor", f)
                last_ctx_bank[sb] = m_cx
                return m_cx

            for j in range(t + 1):
                sb = j % 2
                m_sc = s_TB.bump(1)
                eg = last_exp_bank.get(sb, 0)
                kvg = m_dmk
                def f(tensor, j=j, sb=sb, par=par, mq=mq, eg=eg):
                    tensor.wait_ge(s_qb[par].h, mq)
                    if eg:
                        tensor.wait_ge(s_BA.h, eg)
                    kh = KH[:, 128 * j:128 * (j + 1)]
                    kl = KL[:, 128 * j:128 * (j + 1)]
                    tensor.matmul(PS_S[sb][:], kh, SQH[par][:], start=True, stop=False)
                    tensor.matmul(PS_S[sb][:], kh, SQL[par][:], start=False, stop=False)
                    tensor.matmul(PS_S[sb][:], kl, SQH[par][:], start=False,
                                  stop=True).then_inc(s_TB.h, 1)
                em("tensor", f)
                m_exp = s_BA.bump(1)
                pg = last_pread_bank.get(sb, 0)
                def f(scalar, sb=sb, m_sc=m_sc, pg=pg):
                    scalar.wait_ge(s_TB.h, m_sc)
                    if pg:
                        scalar.wait_ge(s_BV.h, pg)
                    scalar.activation(out=P32[sb][:], in_=PS_S[sb][:],
                                      func=AF.Exp).then_inc(s_BA.h, 1)
                em("scalar", f)
                last_exp_bank[sb] = m_exp
                m_msk = 0
                if j == t:
                    m_msk = s_BV.bump(1)
                    def f(vector, sb=sb, m_exp=m_exp, m_dmk=m_dmk):
                        vector.wait_ge(s_BA.h, m_exp)
                        vector.wait_ge(s_kv.h, m_dmk)
                        vector.tensor_mul(out=P32[sb][:], in0=P32[sb][:],
                                          in1=DMK[:]).then_inc(s_BV.h, 1)
                    em("vector", f)
                m_ph = s_BA.bump(1)
                cg = last_ctx_bank.get(sb, 0)
                def f(scalar, sb=sb, m_msk=m_msk, cg=cg):
                    if m_msk:
                        scalar.wait_ge(s_BV.h, m_msk)
                    if cg:
                        scalar.wait_ge(s_TB.h, cg)
                    scalar.activation(out=PHt[sb][:], in_=P32[sb][:],
                                      func=AF.Copy).then_inc(s_BA.h, 1)
                em("scalar", f)
                m_pl = s_BV.bump(2)
                sg = sum_mark.get(li - 2, 0)
                def f(vector, sb=sb, j=j, t=t, par=par, m_ph=m_ph, sg=sg, m_msk=m_msk):
                    vector.wait_ge(s_BA.h, m_ph)
                    vector.scalar_tensor_tensor(out=PLt[sb][:], in0=P32[sb][:],
                                                scalar=1.0, in1=PHt[sb][:],
                                                op0=ALU.mult,
                                                op1=ALU.subtract).then_inc(s_BV.h, 1)
                    if j == 0:
                        if sg:
                            vector.wait_ge(s_TB.h, sg)
                        vector.tensor_copy(out=SACC[par][:],
                                           in_=P32[sb][:]).then_inc(s_BV.h, 1)
                    else:
                        vector.tensor_add(out=SACC[par][:], in0=SACC[par][:],
                                          in1=P32[sb][:]).then_inc(s_BV.h, 1)
                em("vector", f)
                last_pread_bank[sb] = m_pl
                if pend_cx is not None:
                    emit_ctx(*pend_cx)
                pend_cx = (j, sb, m_pl)
                if j == t:
                    scores_last[li] = m_sc
                    final_pl = m_pl
            final_cx = emit_ctx(*pend_cx)
            m_sum = s_TB.bump(1)
            rg = recip_mark.get(li - 2, 0)
            def f(tensor, par=par, final_pl=final_pl, rg=rg, m_idt=m_idt):
                tensor.wait_ge(s_BV.h, final_pl)
                tensor.wait_ge(s_ct.h, m_idt)
                if rg:
                    tensor.wait_ge(s_BV.h, rg)
                tensor.matmul(PS_SUM[par][:], ONE1[:], SACC[par][:], start=True,
                              stop=True).then_inc(s_TB.h, 1)
            em("tensor", f)
            sum_mark[li] = m_sum
            m_rr = s_BV.bump(1)
            rbg = rb_mark.get(li - 2, 0)
            def f(vector, par=par, m_sum=m_sum, rbg=rbg):
                vector.wait_ge(s_TB.h, m_sum)
                if rbg:
                    vector.wait_ge(s_rb.h, rbg)
                vector.reciprocal(out=RRW[par][:], in_=PS_SUM[par][:]).then_inc(s_BV.h, 1)
            em("vector", f)
            recip_mark[li] = m_rr
            m_rb1 = s_rb.bump(16)
            m_rb2 = s_rb.bump(16)
            evg = evac_mark.get(li - 2, 0)
            def f(gpsimd, kv=kv, t=t, par=par, m_rr=m_rr, m_rb1=m_rb1, evg=evg):
                gpsimd.wait_ge(s_BV.h, m_rr)
                slot = kv * NQT + t
                gpsimd.dma_start(out=rdb[slot:slot + 1, :], in_=RRW[par][:]).then_inc(s_rb.h, 16)
                gpsimd.wait_ge(s_rb.h, m_rb1)
                if evg:
                    gpsimd.wait_ge(s_BV.h, evg)
                gpsimd.dma_start(out=RBCB[par][:],
                                 in_=rdb[slot:slot + 1, :].to_broadcast([128, 512])
                                 ).then_inc(s_rb.h, 16)
            em("gpsimd", f)
            rb_mark[li] = m_rb2
            m_ev = s_BV.bump(1)
            csg = cst_mark.get(li - 2, 0)
            def f(vector, par=par, m_rb2=m_rb2, final_cx=final_cx, csg=csg):
                vector.wait_ge(s_rb.h, m_rb2)
                vector.wait_ge(s_TB.h, final_cx)
                if csg:
                    vector.wait_ge(s_cst.h, csg)
                vector.tensor_mul(out=CTXS[par][:], in0=PS_C[par][:],
                                  in1=RBCB[par][:]).then_inc(s_BV.h, 1)
            em("vector", f)
            evac_mark[li] = m_ev
            m_cst = s_cst.bump(16)
            def f(gpsimd, kv=kv, t=t, par=par, m_ev=m_ev):
                gpsimd.wait_ge(s_BV.h, m_ev)
                dst = ag_in[512 * kv:512 * (kv + 1), 128 * t:128 * (t + 1)]
                dst = dst.rearrange("(j d) q -> d j q", j=4)
                src = CTXS[par][:].rearrange("d (j q) -> d j q", j=4)
                gpsimd.dma_start(out=dst, in_=src).then_inc(s_cst.h, 16)
            em("gpsimd", f)
            cst_mark[li] = m_cst

    s_cc = sem("s_cc")
    cst_total = s_cst.n
    s_cc.bump(1)
    def f(gpsimd, tot=cst_total):
        gpsimd.wait_ge(s_cst.h, tot)
        gpsimd.collective_compute("AllGather", ALU.bypass,
                                  replica_groups=[[0, 1, 2, 3], [4, 5, 6, 7]],
                                  ins=[ag_in.opt()],
                                  outs=[ag_out.opt()]).then_inc(s_cc.h, 1)
    em("gpsimd", f)

    ctxB.close()
    ctxC = ExitStack()
    sbufC = lambda n, sh, dt: ctxC.enter_context(nc.sbuf_tensor(n, sh, dt))
    psumC = lambda n, sh: ctxC.enter_context(nc.psum_tensor(n, sh, F32))
    CXH = sbufC("CXH", [128, NFC, 512], BF16)
    CXL = sbufC("CXL", [128, NFC, 512], BF16)
    CXF = [sbufC(f"CXF{i}", [128, 512], F32) for i in range(4)]
    WOF = [sbufC(f"WOF{i}", [128, 512], F32) for i in range(4)]
    WOH = [sbufC(f"WOH{i}", [128, 512], BF16) for i in range(4)]
    WOL = [sbufC(f"WOL{i}", [128, 512], BF16) for i in range(4)]
    LR = [sbufC(f"LR{e}", [1, 512], F32) for e in range(4)]
    RK = [sbufC(f"RK{e}", [1, 512], F32) for e in range(4)]
    CMPt = sbufC("CMPt", [1, 512], F32)
    EEX = [sbufC(f"EEX{e}", [1, 512], F32) for e in range(4)]
    SUMG = sbufC("SUMG", [1, 512], F32)
    RGI = sbufC("RGI", [1, 512], F32)
    GEo = sbufC("GEo", [1, 512], F32)
    GBC = sbufC("GBC", [64, 512], F32)
    MIDG = sbufC("MIDG", [64, 512], F32)
    MGH = sbufC("MGH", [64, 512], BF16)
    MGL = sbufC("MGL", [64, 512], BF16)
    WBF = sbufC("WBF", [64, 1024], F32)
    WBH = sbufC("WBH", [64, 1024], BF16)
    WBL = sbufC("WBL", [64, 1024], BF16)
    XC = [sbufC(f"XC{i}", [128, 512], F32) for i in range(4)]
    YS = [sbufC(f"YS{i}", [128, 512], F32) for i in range(4)]
    PC = [psumC(f"PC{i}", [128, 512]) for i in range(8)]
    s_cx = [sem(f"s_cx{i}") for i in range(4)]
    s_wo = [sem(f"s_wo{i}") for i in range(4)]
    s_xc = [sem(f"s_xc{i}") for i in range(4)]
    s_CA = sem("s_CA")
    s_CV = sem("s_CV")
    s_TC = sem("s_TC")
    s_yst = sem("s_yst")
    s_gd = sem("s_gd")

    s_wb = sem("s_wb")
    m_wb_ld = s_wb.bump(16)
    def f(sync, m=m_wb_ld):
        sync.wait_ge(s_cc.h, 1)
        sync.dma_start(out=WBF[:], in_=wbs).then_inc(s_wb.h, 16)
    em("sync", f)
    m_wbh = s_CA.bump(1)
    def f(scalar, m=m_wb_ld):
        scalar.wait_ge(s_wb.h, m)
        scalar.activation(out=WBH[:], in_=WBF[:], func=AF.Copy).then_inc(s_CA.h, 1)
    em("scalar", f)
    m_wbl = s_CV.bump(1)
    def f(vector, m=m_wbh):
        vector.wait_ge(s_CA.h, m)
        vector.tensor_tensor(out=WBL[:], in0=WBF[:], in1=WBH[:],
                             op=ALU.subtract).then_inc(s_CV.h, 1)
    em("vector", f)

    cx_marks = {}
    pe_fc_mark = {}
    pe_stop = {}
    ys_mark = {}
    yst_mark = {}
    xc_last = {}
    ys_last = {}
    midg_done = {}
    GE = [sbufC(f"GE{e}", [1, 512], F32) for e in range(4)]

    def emit_gates(cb):
        c0 = cb * 512
        g_ld = s_gd.bump(16 * 4)
        mg_prev = midg_done.get(cb - 1, 0)
        def f(gpsimd, c0=c0, pam=phaseA_midl, mg_prev=mg_prev):
            gpsimd.wait_ge(s_A.h, pam)
            if mg_prev:
                gpsimd.wait_ge(s_CV.h, mg_prev)
            for e in range(4):
                gpsimd.dma_start(out=LR[e][:],
                                 in_=MIDL[64 + e:65 + e, c0:c0 + 512]).then_inc(s_gd.h, 16)
        em("gpsimd", f)
        m_exp4 = s_CA.bump(4)
        def f(scalar, g_ld=g_ld):
            scalar.wait_ge(s_gd.h, g_ld)
            for e in range(4):
                scalar.activation(out=EEX[e][:], in_=LR[e][:],
                                  func=AF.Exp).then_inc(s_CA.h, 1)
        em("scalar", f)
        m_ge = s_CV.bump(1)
        def f(vector, g_ld=g_ld, m_exp4=m_exp4):
            vector.wait_ge(s_gd.h, g_ld)
            vector.wait_ge(s_CA.h, m_exp4)
            for e in range(4):
                vector.memset(RK[e][:], 0.0)
            for e in range(4):
                for ep in range(4):
                    if ep == e:
                        continue
                    op = ALU.is_ge if ep < e else ALU.is_gt
                    vector.tensor_tensor(out=CMPt[:], in0=LR[ep][:], in1=LR[e][:], op=op)
                    vector.tensor_add(out=RK[e][:], in0=RK[e][:], in1=CMPt[:])
            for e in range(4):
                vector.tensor_scalar(out=RK[e][:], in0=RK[e][:], scalar1=2.0,
                                     scalar2=None, op0=ALU.is_lt)
                vector.tensor_mul(out=RK[e][:], in0=RK[e][:], in1=EEX[e][:])
            vector.tensor_add(out=SUMG[:], in0=RK[0][:], in1=RK[1][:])
            vector.tensor_add(out=SUMG[:], in0=SUMG[:], in1=RK[2][:])
            vector.tensor_add(out=SUMG[:], in0=SUMG[:], in1=RK[3][:])
            vector.reciprocal(out=RGI[:], in_=SUMG[:])
            for e in range(4):
                vector.tensor_mul(out=GE[e][:], in0=RK[e][:], in1=RGI[:])
            vector.engine_nop().then_inc(s_CV.h, 1)
        em("vector", f)
        g_st = s_gd.bump(16 * 4)
        g_bc = s_gd.bump(16 * 4)
        def f(gpsimd, cb=cb, c0=c0, m_ge=m_ge, g_st=g_st):
            gpsimd.wait_ge(s_CV.h, m_ge)
            for e in range(4):
                gpsimd.dma_start(out=gdram[e, c0:c0 + 512],
                                 in_=GE[e][:]).then_inc(s_gd.h, 16)
            gpsimd.wait_ge(s_gd.h, g_st)
            for e in range(4):
                gpsimd.dma_start(out=GBC[16 * e:16 * (e + 1), :],
                                 in_=gdram[e:e + 1, c0:c0 + 512].to_broadcast([16, 512])
                                 ).then_inc(s_gd.h, 16)
        em("gpsimd", f)
        m_midg = s_CV.bump(1)
        def f(vector, c0=c0, g_bc=g_bc):
            vector.wait_ge(s_gd.h, g_bc)
            vector.tensor_mul(out=MIDG[:], in0=MIDL[0:64, c0:c0 + 512],
                              in1=GBC[:]).then_inc(s_CV.h, 1)
        em("vector", f)
        m_mgh = s_CA.bump(1)
        mgh_gate = pe_stop.get((cb - 1, 1), 0)
        def f(scalar, m_midg=m_midg, mgh_gate=mgh_gate):
            scalar.wait_ge(s_CV.h, m_midg)
            if mgh_gate:
                scalar.wait_ge(s_TC.h, mgh_gate)
            scalar.activation(out=MGH[:], in_=MIDG[:], func=AF.Copy,
                              scale=LORA).then_inc(s_CA.h, 1)
        em("scalar", f)
        m_mgl = s_CV.bump(1)
        def f(vector, m_mgh=m_mgh):
            vector.wait_ge(s_CA.h, m_mgh)
            vector.scalar_tensor_tensor(out=MGL[:], in0=MIDG[:], scalar=LORA,
                                        in1=MGH[:], op0=ALU.mult,
                                        op1=ALU.subtract).then_inc(s_CV.h, 1)
        em("vector", f)
        midg_done[cb] = m_mgl
        return m_mgh, m_mgl

    cxf_free = {}
    def emit_cx_chunk(cb, fc):
        bi = fc % 4
        m_ld = s_cx[bi].bump(16)
        gate = pe_fc_mark.get((cb - 1, 1, fc), 0)
        stg = cxf_free.get(bi, 0)
        def f(sync, cb=cb, fc=fc, bi=bi, gate=gate, stg=stg):
            sync.wait_ge(s_cc.h, 1)
            if gate:
                sync.wait_ge(s_TC.h, gate)
            if stg:
                sync.wait_ge(s_CV.h, stg)
            sync.dma_start(out=CXF[bi][:],
                           in_=ag_out[128 * fc:128 * (fc + 1),
                                      512 * cb:512 * (cb + 1)]).then_inc(s_cx[bi].h, 16)
        em("sync", f)
        m_hi = s_CA.bump(1)
        def f(scalar, fc=fc, bi=bi, m_ld=m_ld):
            scalar.wait_ge(s_cx[bi].h, m_ld)
            scalar.activation(out=CXH[:, fc, :], in_=CXF[bi][:],
                              func=AF.Copy).then_inc(s_CA.h, 1)
        em("scalar", f)
        m_lo = s_CV.bump(1)
        def f(vector, fc=fc, bi=bi, m_hi=m_hi):
            vector.wait_ge(s_CA.h, m_hi)
            vector.tensor_tensor(out=CXL[:, fc, :], in0=CXF[bi][:],
                                 in1=CXH[:, fc, :], op=ALU.subtract).then_inc(s_CV.h, 1)
        em("vector", f)
        cx_marks[(cb, fc)] = (m_hi, m_lo)
        cxf_free[bi] = m_lo

    wo_free = {}
    for cb in range(8):
        mgates = emit_gates(cb)
        for fc in range(NFC):
            emit_cx_chunk(cb, fc)
        for oc2 in range(2):
            gpar = (cb * 2 + oc2) % 2
            for fc in range(NFC):
                bi = fc % 4
                m_wld = s_wo[bi].bump(16)
                frw = wo_free.get(bi, 0)
                def f(sync, cb=cb, oc2=oc2, fc=fc, bi=bi, frw=frw):
                    if frw:
                        sync.wait_ge(s_TC.h, frw)
                    sync.dma_start(out=WOF[bi][:],
                                   in_=wo[128 * fc:128 * (fc + 1),
                                          512 * oc2:512 * (oc2 + 1)]
                                   ).then_inc(s_wo[bi].h, 16)
                em("sync", f)
                m_woh = s_CA.bump(1)
                def f(scalar, bi=bi, m_wld=m_wld):
                    scalar.wait_ge(s_wo[bi].h, m_wld)
                    scalar.activation(out=WOH[bi][:], in_=WOF[bi][:],
                                      func=AF.Copy).then_inc(s_CA.h, 1)
                em("scalar", f)
                m_wol = s_CV.bump(1)
                def f(vector, bi=bi, m_woh=m_woh):
                    vector.wait_ge(s_CA.h, m_woh)
                    vector.tensor_tensor(out=WOL[bi][:], in0=WOF[bi][:],
                                         in1=WOH[bi][:],
                                         op=ALU.subtract).then_inc(s_CV.h, 1)
                em("vector", f)
                mh, ml = cx_marks[(cb, fc)]
                m_pe = s_TC.bump(1)
                bankg = []
                if fc == 0:
                    for ct in range(4):
                        g = ys_mark.get((cb - 1, oc2, ct), 0)
                        if g:
                            bankg.append(g)
                def f(tensor, cb=cb, oc2=oc2, fc=fc, bi=bi, gpar=gpar,
                      m_wol=m_wol, mh=mh, ml=ml, bankg=tuple(bankg)):
                    tensor.wait_ge(s_CV.h, m_wol)
                    tensor.wait_ge(s_CA.h, mh)
                    tensor.wait_ge(s_CV.h, ml)
                    for g in bankg:
                        tensor.wait_ge(s_CV.h, g)
                    li = None
                    for ct in range(4):
                        pb = PC[4 * gpar + ct]
                        sh = CXH[:, fc, 128 * ct:128 * (ct + 1)]
                        sl = CXL[:, fc, 128 * ct:128 * (ct + 1)]
                        tensor.matmul(pb[:], sh, WOH[bi][:], start=(fc == 0), stop=False)
                        tensor.matmul(pb[:], sh, WOL[bi][:], start=False, stop=False)
                        li = tensor.matmul(pb[:], sl, WOH[bi][:], start=False, stop=False)
                    li.then_inc(s_TC.h, 1)
                em("tensor", f)
                pe_fc_mark[(cb, oc2, fc)] = m_pe
                wo_free[bi] = m_pe
            m_ffn = s_TC.bump(1)
            def f(tensor, cb=cb, oc2=oc2, gpar=gpar, mgates=mgates):
                tensor.wait_ge(s_CA.h, mgates[0])
                tensor.wait_ge(s_CV.h, mgates[1])
                li = None
                for ct in range(4):
                    pb = PC[4 * gpar + ct]
                    mh_ = MGH[:, 128 * ct:128 * (ct + 1)]
                    ml_ = MGL[:, 128 * ct:128 * (ct + 1)]
                    wbh_ = WBH[:, 512 * oc2:512 * (oc2 + 1)]
                    wbl_ = WBL[:, 512 * oc2:512 * (oc2 + 1)]
                    tensor.matmul(pb[:], mh_, wbh_, start=False, stop=False)
                    tensor.matmul(pb[:], mh_, wbl_, start=False, stop=False)
                    li = tensor.matmul(pb[:], ml_, wbh_, start=False, stop=True)
                li.then_inc(s_TC.h, 1)
            em("tensor", f)
            pe_stop[(cb, oc2)] = m_ffn
            for ct in range(4):
                xbi = ct
                m_xld = s_xc[xbi].bump(16)
                yg = xc_last.get(ct, 0)
                def f(sync, cb=cb, oc2=oc2, ct=ct, xbi=xbi, yg=yg):
                    if yg:
                        sync.wait_ge(s_CV.h, yg)
                    t0 = 512 * cb + 128 * ct
                    sync.dma_start(out=XC[xbi][:],
                                   in_=xcols[t0:t0 + 128,
                                             512 * oc2:512 * (oc2 + 1)]
                                   ).then_inc(s_xc[xbi].h, 16)
                em("sync", f)
                m_ys = s_CV.bump(1)
                ysg = ys_last.get(ct, 0)
                def f(vector, ct=ct, gpar=gpar, xbi=xbi, m_xld=m_xld,
                      m_ffn=m_ffn, ysg=ysg):
                    vector.wait_ge(s_TC.h, m_ffn)
                    vector.wait_ge(s_xc[xbi].h, m_xld)
                    if ysg:
                        vector.wait_ge(s_yst.h, ysg)
                    vector.tensor_add(out=YS[ct][:], in0=PC[4 * gpar + ct][:],
                                      in1=XC[xbi][:]).then_inc(s_CV.h, 1)
                em("vector", f)
                ys_mark[(cb, oc2, ct)] = m_ys
                xc_last[ct] = m_ys
                m_yst = s_yst.bump(16)
                def f(gpsimd, cb=cb, oc2=oc2, ct=ct, m_ys=m_ys):
                    gpsimd.wait_ge(s_CV.h, m_ys)
                    t0 = 512 * cb + 128 * ct
                    gpsimd.dma_start(out=y[t0:t0 + 128, 512 * oc2:512 * (oc2 + 1)],
                                     in_=YS[ct][:]).then_inc(s_yst.h, 16)
                em("gpsimd", f)
                yst_mark[(cb, oc2, ct)] = m_yst
                ys_last[ct] = m_yst

    y_total = s_yst.n
    def f(gpsimd, tot=y_total):
        gpsimd.wait_ge(s_yst.h, tot)
    em("gpsimd", f)
    if DBG:
        dbg_ag = dram("dbg_ag", [4096, T], F32, O)
        dbg_gd = dram("dbg_gd", [4, T], F32, O)
        n2 = s_yst.bump(32)
        def f(gpsimd, tot=y_total, n2=n2, gtot=s_gd.n):
            gpsimd.wait_ge(s_yst.h, tot)
            gpsimd.wait_ge(s_gd.h, gtot)
            gpsimd.dma_start(out=dbg_ag, in_=ag_out).then_inc(s_yst.h, 16)
            gpsimd.dma_start(out=dbg_gd, in_=gdram).then_inc(s_yst.h, 16)
            gpsimd.wait_ge(s_yst.h, n2)
        em("gpsimd", f)

    if DBG:
        dbg_outs = {}
        for nm, src, sh, dt in [("dbg_qh", qhat_h, [NH_L, 128, T], BF16),
                                ("dbg_ql", qhat_l, [NH_L, 128, T], BF16),
                                ("dbg_kh", khat_h, [NKV_L, 128, T], BF16),
                                ("dbg_kl", khat_l, [NKV_L, 128, T], BF16),
                                ("dbg_vh", vtok_h, [NKV_L, T, 128], BF16),
                                ("dbg_vl", vtok_l, [NKV_L, T, 128], BF16)]:
            dbg_outs[nm] = (dram(nm, sh, dt, O), src)
        dbg_midl = dram("dbg_midl", [68, T], F32, O)
        dbg_rd = dram("dbg_rd", [NTT, 2, TTW], F32, O)
        n_dbg = s_qst.bump(16 * (len(dbg_outs) + 2))
        def f(gpsimd, tot=phaseA_stores, midl_m=phaseA_midl, n_dbg=n_dbg):
            gpsimd.wait_ge(s_qst.h, tot)
            gpsimd.wait_ge(s_A.h, midl_m)
            for nm, (dst, src) in dbg_outs.items():
                gpsimd.dma_start(out=dst, in_=src).then_inc(s_qst.h, 16)
            gpsimd.dma_start(out=dbg_midl, in_=MIDL[:]).then_inc(s_qst.h, 16)
            gpsimd.dma_start(out=dbg_rd, in_=rd).then_inc(s_qst.h, 16)
            gpsimd.wait_ge(s_qst.h, n_dbg)
        em("gpsimd", f)

    with ctx:
        with nc.Block() as block:
            @block.sync
            def _(sync):
                for f in ops["sync"]:
                    f(sync)

            @block.scalar
            def _(scalar):
                for f in ops["scalar"]:
                    f(scalar)

            @block.vector
            def _(vector):
                for f in ops["vector"]:
                    f(vector)

            @block.tensor
            def _(tensor):
                for f in ops["tensor"]:
                    f(tensor)

            @block.gpsimd
            def _(gpsimd):
                for f in ops["gpsimd"]:
                    f(gpsimd)
        ctxC.close()
    return nc


def _prep_inputs(inputs):
    x = np.asarray(inputs["L0_attn_x"], dtype=np.float32)
    Wq = np.asarray(inputs["L0_attn_Wq"], dtype=np.float32)
    Wk = np.asarray(inputs["L0_attn_Wk"], dtype=np.float32)
    Wv = np.asarray(inputs["L0_attn_Wv"], dtype=np.float32)
    Wo = np.asarray(inputs["L0_attn_Wo"], dtype=np.float32)
    rw = np.asarray(inputs["L0_ffn_router_w"], dtype=np.float32)
    A = np.asarray(inputs["L0_ffn_A"], dtype=np.float32)
    Bm = np.asarray(inputs["L0_ffn_B"], dtype=np.float32)

    cos, sin = _rope_tables()
    cosT2 = np.ascontiguousarray(np.concatenate([cos.T, cos.T], axis=0))
    sinN2 = np.ascontiguousarray(np.concatenate([-sin.T, sin.T], axis=0))
    kk = np.arange(128)[:, None]
    qq = np.arange(128)[None, :]
    blk = (kk <= qq).astype(np.float32)
    dm = np.ascontiguousarray(np.tile(blk, (1, 4)))
    ones = np.ones((128, 1), dtype=np.float32)
    ident = np.eye(128, dtype=np.float32)
    wf_ = np.concatenate([A.reshape(64, D).T, rw.T], axis=1).astype(np.float32)
    wf_ = np.ascontiguousarray(wf_)
    bstack = np.ascontiguousarray(Bm.transpose(0, 2, 1).reshape(64, D))

    in_maps = []
    for c in range(8):
        b, g = c // 4, c % 4
        xb = x[b]
        in_maps.append({
            "xT": np.ascontiguousarray(xb.T),
            "xtok": np.ascontiguousarray(xb),
            "xcols": np.ascontiguousarray(xb[:, 1024 * g:1024 * (g + 1)]),
            "wq": np.ascontiguousarray(Wq[:, 1024 * g:1024 * (g + 1)]),
            "wkv": np.ascontiguousarray(
                np.concatenate([Wk[:, 256 * g:256 * (g + 1)],
                                Wv[:, 256 * g:256 * (g + 1)]], axis=1)),
            "wo": np.ascontiguousarray(Wo[:, 1024 * g:1024 * (g + 1)]),
            "wf": wf_,
            "wbs": np.ascontiguousarray(bstack[:, 1024 * g:1024 * (g + 1)]),
            "cosT": cosT2,
            "sinN": sinN2,
            "dmask": dm,
            "ones128": ones,
            "epsc": np.full((128, 1), EPS, dtype=np.float32),
            "ident": ident,
        })
    return in_maps


def kernel(**inputs):
    in_maps = _prep_inputs(inputs)
    nc = build_nc()
    res = run_bass_kernel_spmd(nc, in_maps, core_ids=list(range(8)))
    out = np.empty((B, T, D), dtype=np.float32)
    for c in range(8):
        b, g = c // 4, c % 4
        out[b, :, 1024 * g:1024 * (g + 1)] = np.asarray(res.results[c]["y"])
    globals()["_last_res"] = res
    return out




# version 04 (final)
# speedup vs baseline: 1.1300x; 1.1300x over previous
import base64
import math
import zlib
from contextlib import ExitStack

import numpy as np
import ml_dtypes
import concourse.bass as bass
import concourse.mybir as mybir
from concourse.bass_utils import run_bass_kernel_spmd

F32 = mybir.dt.float32
BF16 = mybir.dt.bfloat16
AF = mybir.ActivationFunctionType
ALU = mybir.AluOpType

B, T, D = 2, 4096, 4096
NTT, TTW = 8, 512
NFC = 32
NH_L, NKV_L = 8, 2
NQT = 32
ISQ = 1.0 / math.sqrt(128.0)
EPS = 1e-5
LORA = 4.0
PHASES = 3
DBG = False

_COS_NCORR = 2168
_SIN_NCORR = 3087
_COS_CORR_B64 = "eNoV13VcVE0XB/DZXbpRwqY7VWw5Y7cIFirYYNdrt4iiYhAqiqAigoqB3Y+cMbALFbsRsQMTUPT9+cf3c3dvzMw5c2burp5GiPZaIRbAYn0hrhgIccZQiBXGQhiYCGH6jxnOmQthbSHEEuhpKURbK5yDKtZCGFYRQq+qEPq2QqyGO6C1E8IYRDUhQqA3LATD6kKcgqSaQvSpJUQq3IF7EFpbiAKwdxAiDtZCL0ch1sBJsHISIsxZiBQXIZJdhejqJsRVdyFiPTA2T4wfkuE2LPMSoq+3EBfgCiT7CHEO+vgK8QDa+GFckA/3IClAiBaBQrSCu/VwD1gGYRwN0D8sbIg2oADaNEZfTYTo0UyI+aDXHO3CArgLXYKFqEpCtITWUohVsA7athBiKRi2FOINVGslRA2o2hp5hbA2QvSDZEiDQqjZFufgEjyAJe0QMxS0R447CpEAV6EQbDsJEQlpnXE/1OoiRHc4B926CrEcbsK6EFzrhiPk/xOKZ7sjVlgPF6BdT+QXroQjt30wt32RfwiFRXDy3/d+mFeIg86ROA7EGMFoEO4bjJzCejgDVkOQL9gQJcRFqBqNPmENXIMewzFOqDMCeYL+sGEk4huF9kYLEQ82YzBuWADdxgqRCOnjUGPj8RksJwhh9j8hbkCXiWhjkhDhkzFWOAN6U5BLMJ6KGoMEyAD9aVgD04U4D2EzhDgLF2DjTMQPS2bjM/ScI8Stucg16Obh+RghnkDV+fgei1qAmgsRC0TAI7CNwzPwfDFysQTtgH48agJugMlS5AYcwGMZavef5aj7FYgTVsENqJaA5yEiETHDPaiSjHqFARC+EmNYJYQFPIFrq4UwXyPEIIiDLmtxHhxSsf7AcJ0QA2ETPP33OQ1rEZ6BQzraWo/PUGcD8r4R92UIcRmsNgnRAQZAAURlotY2IyewIluI2luwniF2K9bgNtRODsYJN+HeDszVTuQLzkLLXcgz3IHHYJiLmoOb4LAbNQT9IQHWwcA9uB9C92Le4DI8g777UOf7hXA8gJqBxxB3EGM/JIQzmB5GjPAceh/B/dDiKPJ7TIjq/2E9n0C/kABr8oDRPlRX+AydT2LdQsYpjOs04oYBZ9AX2OUjJ2fRJ5idw3hhAyScR03CSbgChhewb0E+3IfrF1GPlxAXLL2MsUGNK6gPsLqKmoOlYH8N8wb9ClDDMPgG1v1NnIPMW9iHCoVwuo2YIBau3cUaBLP7qFe4Df0fYj4fobYhGWweYxxQ4ylyBuFQ/RnaAWdYAmehCIye4xy0LhLi9AshXsO1l5jTEuwRUATPXqHOXiPON2jvLdoD7Xs8A5mw7APeJR/xDOT8O35C3mDRZ8QNTqXIJzwBwy/I91fMJyyENfAU6nwTIgZOQuJ3jAHif+BzGcZYjrhgE1yGFhXI1y+MCVKhEB5And+IoRLX/mAdwRBYIDRiEWzUaISJViPWwFMYqNOIc3APauppREdYrK8R8bDYQCMWGGpEZyONOAU3oLuxRhRCWxON6A0psAbOwDlIM9WITDONyIIiOGmuETYWGtEPrkCaJfoFGyuNaA294AW8AwdrfIeB4FdVI8Kgo41GPIQ+thphYIf74EU1jdCrrhEuMBAsaqAtWACb4RQMrakR6bUwBlhcWyNO10EcThgnRDhrRE8XjXgOL8DMFe1DBNwCezeNcIaBkAnV3HGES9DKU4N3L77DJWjjpRE9IBlugpm3RnSATXAaLsENSPDRCA9f3Afr4B6E+2lEFX+NuAuaAMQPd/4dAzViFFSph8+wHDT10Sbcga5BGhEJCZACvRtoxHpY1hB9gnEj3NMY8w02TTAvcB3sm2qEYzONaAG9YR1cB+vmyDv0BodgjbgMV8Ga0C7cADOJHEM6GLdArUBcS424D8/AsBWO4NoaR9C2Rb4hG6q20wir9pgvSIdr8ALMOuAzPIQiaN0ROYeWnTB/ULWzRkRBNth21Yhi0A9BH9ADLLvhOlQNwz1wB7K64zM8g4IeiLcnnoP2vTQiBrJ7I65wjejbB/mEzZAFhn1xBLMIjBWKwToSdQBD/x37o5ah5QDU00DU3CCNqAVLBmtEKpyDIggZgnmARUMxb7ARcuE5rIhCDqKxhmAF6A3Ds1BtuEa0gwVwD+6PQIwj0e8ojciHK6Px43WMRpjCKTCcoBE1/odaglvgOFEjFsIpWDoJccGQycjZVHyG82A/DfM7Hdehw0yNGATzYQM8m4W2ZyOfUANqQ0voNwc1BongOBf1Cv1h8DzUbIxGPIJa87EWwCAW9Qr9YA3YLUDOYT3sAMOFGhEK3WEtZMch3sUacQFOx6PPpcgRvIBBy3AdbkCP5ZhjqLoCOU9AXcNCWASX4Do8A12iRnSCHtAfBkEm6JLQD5yEi9A/WSNWwnUwX4n+YMMq5GI17ocsOAO2KdgDoeca9AMb1mJdgE0q5gbaQ2+4BXfAZR0+w6g0jUiCNCiEmunIG8SC2XrMA1zfgPqCHhvRH2gykEvYskkjzoI2EzUEyZuRryw8l4374BwUQ+0tyDMUw0vQbsUYYUUOjnAN/LbjOujvQMw7UQNQdxfqCdrkIrbdmO892APgJrwA3V7EBtfh3H48DxcPoN2D6B+iDyEPUHgYcwXtjiA/cBGeQ/WjiAmqH8MaOoG48lDDjOfhBdxXiOUk5hLs4A50PIV5g25wBgxOYy7gJrwAlzOoQeiQj7yC9VmMBV6cw/E89qILOMIZaHUR8wI1L2FM8BAGX9YItyuIHwyuog7BBiyuacQAaFegEY9B/wZyeRPr7BbqolAj6kAmXADNbdQ0nAG7O8g/dIIn0OUe1jCsg8f/Pj/AmniIeX6EdySkQovHqEFYC+nQ54lGDAPjp8gFRMANuAfXnmEMz1ELEFOEdiEbzsMTMHyBNqBFsUaseom5gmJILEEMr7CfQRG0eK0Rq99oxG0weIccQDhchdsg3qNvCIWVsO6fD5h3iPqIvHxGv1C7FLmA/tDuC/IJ4V+xB0L8N9QrPPmO3IHeD+wvcAPMfiJOWAvZcA8sy3AdEuA2iArMwy/MGSyvRFxQAif/oJ6g5l/kCaKEVmyEl2Cu0YoIiNdqxRZw1WlFD6ipj88GWrHQUCu6GOEcLIeaxlpxE3qb4LqpVuRDFXM8D6ugxFIrzKy0oh0MgGGwELbAbahhjf7gBmyposX/f4wBYmArDLPB2GCJrVb0tNMKK3ut8ICOEFpNK0yra8VqOAvRNdAvuNVEv7C5NtqAc/+OdbQiygHXHLXC2QnPQxG4O2tFe1gIK2ADVHHViiTIhhewyE0rlsEDsHLXijTIgtPg5qEVa6HIE+16oV3oDtbeWmEDQ2ABnIQ4H62w90XMfogdevmjTzgbgDFCv0CtaFEX7cNZeAmG9bRifZBWXIB2DbRCNNQKHVyGR/80Qr4aa8UtiGuqFV2boS14Cl2ba0UfKICnUC0YcwfrwJm0YiAkQCpYSVyD+2DfQivCYT2chqyWiKc18gIOkAUXIaoNYmyLeYXVkAoloNcOuYL77bXiCcR30IqVsB6yIbyjVsyHOGjVCfmB9V20QtsV44Fb8ApqhaAveAG3uuF7qFbYheEILSAMtsJFCO2OnEGtHloxApbDObgF2p5a0bY3xgd34GQ46hQM+qIN2Ai2/VC/MByuQxFsjkCeInEO+vXHXEI0JMAquAMvBiAvA7WiOkQPQp6h+2Ct6AvaIah9yIL70Hko6hmewoAotAsF0aj7YTg/XCtOwcsRqMmRuAZboADugOUorXCBTjBktFacgXMwfIxWxEL1sZg/WAnR45Bb2ALa8RgjRE9AzYHt/xDTRK2IhPWwDVwm4TuMhRXQYhrWBGRNxzyA9Qz0MxP3zEK9gd5s5B305mIs8Ar05qH2YQVExmhFBuRD0nz0EYvYQH+BVgyGGJgPq+AaVF2IuoIWccgNDFyEfMAjKAHtYtQMZID5Eq24Ci/BJh59wXzQLUX8MB9WwtZleB5EAmobouEk2CahLxgGi5O14iFYrkSfYLsKuYBQiIYM2AILV2tFOvinYE3BBchYi7zC6VSs43XYw6BuGvKUjnGsR92B5QbUI0TDMLgGazdqxfkMxLEJMYF5Ju6DFtAaOsFZsNqMWgcXGAqJYJ2FeQVdNvYdWA3hW1D/0Hkr+oZ40G3XChOoCn1gG9wAtx2IBV6C2U7ULURAf1gFV+EaGO/CuGBhLuKD4r1aEbIP++9+zCPYHMA4D2rFhMPYq+Ay3AG3I6jDYxgvGBzHu+E/XD+BOoK3eagR1gpH6AZ9YSAYKMQG6yALrsKWk9gTTqEWoNNp5OcM2gIX0J3Duwe2QOvzOH8B44NX4HYR7UIMXIEH4H4JMUExrLyMc+B2BXmBxxB1FX3DKbgFfa7hCJrr2A+gG9yH6gVoC7pBxg3MEbyGbjeRP3C8hbwVIjeQBta3sd9CDFyHqDvYfyDuLt4ncBHO3EPuIOS+VmyCAQ+04hLcfIh98RHqChZBMcQ/xvifILdPteIuGD3DZyiBFc8xT9CzSCtyQPcCezisBZdirUiBTVDvJdqEYZAIr8C4BDmEEbAI7kPMK+zTcA5MX6NO3mA80A8GgftbrDNYCIthJWTALXgKz+DVv2vvMG44D7Hv0SYM/Ih8//MJcw23wPoz1iR4lGIcsBiywOEL3nkQAnehGMK/4h7IgbRvqG249h3tgN0PzBP0gzVQAot+4lwZYipH7ivwfoeBv7Cv/kab0K9SK5bCmj+YQ6ETj+EFDNXoxErIhsswVKsTW2EbaHU6kQDxejqxAlLgJhRDf32dSISBBrgXHAx1whWGw3wwN9IJE2OdOAXn4TZsMEHb0MYURxhophOF8BS2meMeGG6hEx5WuBe2wS14Da7WOA/bIAdqV0Gf0K0qxgUjbHC0xb1gY6cTzvY60Rp6gXE1HKvrxDXoVgNjq6kT7Wuhvdo60QWWgXsdjAkeg5ED2oMFkACRjrj2j5NOvHNGPlx04h6UwGVXnbBzw5gg2wPj8sR1OO+F89464Qlh0BsSfHTila9O6PvphD0kwybYCmYB6AcugEGgTjyC6Lo6sRb06iHu+jpRAIsa6IRVQ4wdDBvhOyyBRKjWWCdCoRjcm+hEHzBoinn4p7lO1IJuEAWrIRtu/TsXjHGBIelEV4iDK6CROtECjFro8DtJJ7bALYhuqRPDoBBattKJ6q11oge4t8F44QZca6sTd+Ah2LXTiUtwC+5D1fboBxLhKnTqoBN9oRBqdUQ7cA22dtKJc2DeGfmE0C7IEZzuirGE6MRdMOyGeYJsuBiqE09AF6YTFrAQsqFdD53Ih+o9Mb5eOAdx4ahNsO2rEzUhGuZDJizqpxMxEegTXCN1YiOIAVgP0BFyoM5AnXCETlAySCd6DsHzkA3thurEGXgATlGIHZ7Bo2idGDBMJ1bBuuGYT2g5Au2MRFywAlxHoS3YCsNG4xxshUewagzqBS7CA9AbqxM1YCIkwksoAevxaAe6QDcYOQHj/Z9OuMEaWA8b4RIUwkOwnqgTgyABtsAjaDsJeYFNICbrRBV4AGbT0BakwSuoPh11Dl3gAoTOwJ4CKXAT7sEraDsT/cMrGDAbawxs5yBOSIEnUGsu5h6uzsN3MI1B29ALrsJteAVd5mPeIQWew71YzM9C1DYMj0P/YLBIJzrDsiWICbbCLbgDHvHYh2A71FyGOCEHRi5HXYH1CpyHfrAIlkBxAuY7EbUOq2BbEmoEDJJRhxAOw2EtmKxE7qEQPFcht2C6GrkDN9gOTikYB4yE02tQN6nIP+RA7DqdSIdNUAzV0vAcbEvHM+t1IgmiN+rEcnDNwJ6wCfFAOmyD7XAB2mbq8DsL/W9GvmAlvIKHWajVbNwLBRCyBc/Atm3oH6JysI7hLThuRxuwfQfW7S7ED8NzUZvwEsJ3I5Y9GBd47sW8QjYM34f63q8TlgexFx/CcxANSTDyMNoDsyN4Dl5CylH0DZ2OYx8Hvf+QZ9gEp+AyjDiB2oVnYJ2HdQODYBNkgh3rhBdcgCewS2HPgTdQ5yTWHeSA4SmMFTrAFdA/jTqGWFgMg85g/HAdCqFNPnIJXWEZGJzHewZGQPULaOcS9jAYAdrL2O+voC1YCpnwGqyu4rlryBUMh5zrqEXQL8CeByEwAKrcQIxwFa6B6U3smWB0C20W4hosuoPcQJ27eP4e9rv7iBdeQ4sHyC9sgnwQD9HWI+yjj7FHPUHeoepTvMNhFYQ8w/ciHCEWnF7g/Qv3YXAx3pUvdWIaLC/B+xGqv0aO/3mDfQJM3+JeCH+Hvt7DB+zB0AtSYSO8AttPeBaiPqPeQVOKe2DEF9wHl+Al1PmKevmG9QB9wf47ngMP6AbLIB+G/0AeYAucgctg8BPjgFZleA/DdjArRwzQ+xf6/Y2YK3E/tPiDeoAM2Aav/+pEnlUrHhVrovI3ucinrYWs2OajGp/4w61fDlIntrrIiQZ/OLhgE5l0epWXsK8q57SdR1trG8nJj6ZxXuix4DEPBR8/bc5WX99zp8XJFL3aloqyqsp7zZezpnIcnTndmNs/CZLGm36S8WQ/aXtrKj1ZX1NtHWSq0ov85I5YX2lxPI0GOvrImGEbuP2+Cdwm8wWVmu6jL4Mf0paDvtzPxpoOJdjKpZ/m0ufBv/mSTS5HeU7iTk/t1OSwPrzoSD6HxrUgh09u8kPeHDox3k7Nz/wV3HmKsRzXty6bfnzJb3rayMBGrvLanBW0+pqbvDrkIod381V9Ct+S847xtDfbXfYd0YMTjpKq93wGe9lZyNq/Z6m/dbvQ94ZByn5PY/7RdowsPsHc80V58KRvUZxddDe4n5eDPNLkER2sSZzr/YjKF1orp94/ODx+C7mM6c427o+C77Z8So/95gTf/PSB19n14yVj7eWbhHny+aF9vK9sFRne8pHfdv3lPenlvKZiNi1cq1NLfKeTNsqJ2+38QAnHRtKlv0N5odUgetdoFY/v4SM19iGsf9ZI1g2xUaePXc972CCQmmzzVP3r2EiTIU+DG/VcTm2GWsqk8inU4kkDbtpMy5OKRtKb53nBhTmruWMtQ9mh1Sa+Geioyg65Ka96ydx0lIl8PayCNuzpS6Ye7/nA8tK8sb91NHJvOGdNfsPWUf5ye2MPafrVVu3v4SHte1vJA2v9lPXC33z9vrksvPcfHbuZwCtzqsnRu0o5oblG9ZKb6Jmei3o6w1od8TfhKbo/ZLzMV+Y9PcC7An1lzYo5XLPrN+5iqqfMMzdzQYGTXLHnMB0d9J5OjSqhLW5mslunGMq8q+hvsaucdLUpvckzUs6GM+nFp86U+cWfW6/TqkumvvwscCFHWjjIYcUdKfrmVJq+9TDdO6Kvukx/wxdaP6b+xzPo1QFXdea6jzyQ/In3VTjyypr+8vrDFzR2vAc1r76Bz7/xVKrLLvJvM0s23SZkpMMHrhwToAyOldOEQQ+5/QFn9dxeT9ltWcrXGnqqeQPe0FCzMv6fQ1Kedcfn3KttBW9/Npk3GP7mwZ2NedFQR+ns9S3vdlw3mfO3Cfc5d4mCLMdx1SvN6Wz5OU4ZsIfjH2/neL+qssXiWXLNyg203HoBjek9mt4YbOdurSrorsvx4CI/E7V+zgKOTnFWWZEuapjmD7fssJ0de7mrAL9tNCdYyEOX+qjfvXJ42q31VGTio/QiHVTMT0+ZeThA1evoT3PqLKeJXX/T2Eu9aa3NWdIfW0bVhrvJg09+0KZu36m4PJT76LnKtT+/ULNT53jwBSf5tng0hbTXqE65deT9oYZqQZ0yijQNp1H1Q8l2lo2clTFaTn74l7X1+nFM/iBe/r4O791vLNsLY/WuwxMqlV78v0HlPK/ZXd5xazHFZbjJ6TGjyX21N3Wv9o3Xn/nAS/5bQHf7TuCsa51p5duPPCY+Lvhc0iFKf19KL0vn8fPAJNZvZqxo+D7yyd3KI/a0ok5Nc2nOpGRK6Wsi47d+55iPv6gk1VCGiXT+1W4950xvQNVlcfMpbu1laL+a0sArlTynesuvF7rw8B4H+HxaCt9bYSgzPi1mJ28nSnfUVyGXMqjV1re8fvQfbtpxFjX/Esix941UlfE/ad19HS1YMYzC7YbSjgJ99WWsypvm8ZY1BwLVJ5Nw1r/+jQPXHw2e5/mT1/Qs42VttlKb3+4yaa1W2iY7S//wE/xjRX1euK45F88fGxxRfbhs0zdANdh6Odj/YimfbmAvR/0NVFOPBfC3dr1lk9YBalNNnUxq+I6am7TmoPpO6tQIY+n3wVq61OzBE+pF8/MTljL2fRK5t7WQ5n/20aW9Oykxpzct/E+rvH+mkFdKDNtFBMjjM73U3dGTeGFAEq/2zuDRs37wkHN/qV7YWZ73MJ1Wd7Dk1NNV5PmuOikPessNjSO43tWPvNLbW7aOCKC0317SK9ZdduoQQcHm0+W+OVP5kq+1er9DKwsL3GRY/AJ67RxLDc/E8dHJM2ljpFb5t07n0Q928IAn72nZdju5LNCPvQzaE3ecJhtap9Cls5+52L1TXvVPL+j+8BdUbviJJ3//zGFhr3m55nqweZ6nTBx3JK+i3Sp2+FiW57y6nDvesVZV5uWT7dSB9GW4t9y0Zx41vL2NLZrOyPOM9JZfljlToG8lj951gUvb1KDTi6vRiLBKKjS3Vn0sXcnhP1O1qG0G5ZzpSZGL3FT+76rS+WdVfnCuPY/JtMW7xEO6iqO0eno0t/40gtPPvWDfsmU0zquqetHgIw3/5K82NkzjqWGdqfTNLU6756lmDd7Amh16sla3aDKdlUVD8nfRmSXu8pVmGrtv95TTPmTQp45BasS3l6SXvo3yJ/rLrsONZOQHU3Z6by39yirpRvfWPKiHnkxskcEHx/rKoFH7eXKOgdr610duzJktu03P4tlBVVRRt2zeGDNXtij3VavxzzUzbzd31B/LMSUdOCd2tqzRzUNenu1PPl4e8vov5v4O7+nHUi/ZeUNTsravoDmFXir7ppvcHR/NF0N+0IXAZ3lDrJ3l/Fvb+UpBX85sZsJKDpcON6Npy35HueZAkNxV30E5hRfm/XtvJ5v95ImdXtOQRy/ZK78/hTfxUs08+1Lck+ry6EMrOTO3G9163406OJ/kvfyHTeOqyNDwbKp/6ldeg9HpdGngQ3Lc8jnP5vssKnw8W9Uc/p7eb8ylHlGj5bcCF7l8rodctaSz+pjYgw2vrKU502pz9esecr2rVkV8CJJHduqUk5ohf5S7Ss0hA7XAVF+ZHJpJ29/6cN7kKrLtEhtlIgJU2eIA+cjNXqV21ZNU4aPM+tbna/skn3zkw4OvfCEzPQ95+NoGfrvSX3362JTsbsyik6t6c2aXPdw+YAXp39xP+dWC+Vu4vvS+14eT744j+mCnHDsup2fVhsn9u1JpyDcb9TqiN23baaZem/vJQ8VZ5N2xkCYf3kYvmw5RG4/2oOPZs3lM03Qa5DifU+fXlT0/LOCNkRt5TvZSzugXKCOutCVN/hfq08NEPtj4lGJblNC5QmNp/vkXLU52kae2vaaGHttpUdv/uOZXD3WaX+ct9zOSd7aHc2Kun0xO9JOdzVzkgxhf6fHVj/cO8ZUtC99yWUcrmX7ZR9JnV7XmuLc0vu6uErFWr7bwlkGDc/F+9paeF/byVT8vGfugN1t376nKenvKwRHMr72zOfbbCZ7WdTmVNqiktI79ud39X3TZxktmXjrIq6+6Sa96o6nJzTjqYV9GFj8X0MC6ZmqocFE8qjJv2rdNPM4avylnOSmfI07qWDdFLeVU6prZkr52MVW1p3rI9G8fePvoueq/eXOk7Lo8b3eHyWSq/546LCrhDvp/6OkPD1XkoVPFT/7yNDeNLKrWghNr/mCnBZ5qfTsH+X1fMR3K38mlOhdKfqNlq5JFnNbPWG7sqyVzKw+5yXsZFdgMppzov+zl5SeX3DHmhnqj1YZVGrVydyCNqD6Ryrvqq5rz4ul2pru8vMhX5S21lK94G9u9MVYvus3nNlf9VegJUxU+3UX9vv2b3FK/UdN7PirwcBbdyXpD6a93Buu/mkihky3Ux9AnNL6nhzI62I5d6h6k/BkjyTN3DG2sGcZGc5l+DPaVbnEnyKA4mUusF3PjeWd4SdtFdPX5FFnWeDMtxG+mjVd/0a+CKIrMsZI3Nz/NmxnjocZ+sVDvulqoFptn8ejah2n6r9qq2mxzeTS3M88NqSsrbc3l+aNu0naek/Kd6a4qRVf+9spD7p1dyV8D2vPMz05q25jzHP56ARu++UPLhmvV03nOqtnWJaxzNFNd+v6gyrVBcuKvrWy05BeXuzVRt2MOkP7qHPrUbwdN/vGZC1JrkuuvVXw3fxsl1VnMb9bPUf7zW5NHVndemVrEJvcf89vPE2hH0maufs5AOk0PkBXRdWTBvrmcfd5ZnqiWSc+DtrHVzsnyadXP5ProRF7nwp+cJWbJ+e0tVEADc7n5VFc1tLSqcl5vLGdl6+SgLa5qVkkq1Y9tRcFT7GTWMy/56f0r8gwUMvk/d1n1gIF838pIrWsu5MPXiVSlnwunrPlLZ3akcUncWp4y6w89uOUh28xpzMfOe6vF5zdQQcka0j9ezAljZ8svJhW0/ft4rng6ma0z39DNzvNkgHMWtWxkqGoEJvAjE6EOlfmqua2cpfb+HKJh33mqxVdqa9OF07Y0lcd/fKJQ0yLaHVbBQ0eOp9Y7nGRvA2Ne1T1Ihfs/5b6rw/hykL/ye+5APbs+Z89UnYrptoVv5P7m3ekBqktpIH3PCKPIDzVopX0oLQ7dz+Pv1FYXtKlUiXfu6LMuMnXvPVp5chcNtUihY2lNyL1ad54VFqA0OYbSpKA1hZm2psnfpvLIm5XksfYv3SjUyBn9mpF2fiMaVrcBje7sLBvUqkdLBp2mz9+/kVXPlfzQz17Oa+Mp9xyzkdG39JR1r2py54v3NPNybvCkjVj/I13k5WmVHJdcT7Wx9lN7yqtT05176Oc8W3kydTNfyLLnM/291GNDC1V2zF/Wr+pOhyf150FrDNSxAZPYslcY9boyW7254ao66b9ju5h99OPeaOpsN46C6qWQvDJLdq82lgtb5QYfzp9AM5d/oNCnX3le20bcv3YW/d4RTIerBsnn3z9R9bxNFLI9hfqm11G9plvT9BQLdfd7FO1paCm/R5RTafErWpF6lCo7Gslt3yxV192bqcJ6JOWt6cEVu3pyywWu3PDYQSprtpb79+5ITWkB1zm0gHst/U5jOxjJk2OXkd1kS3W0mqk8NHADvZVBdKSGD+1t2YNc6/kq89ABdPJoCY0Z5S5zKr1luwV1ZK+lO2hQ+WP+UriNNtNnWjH7LTcYW0rDnPZzm2te6muolVz2eRPvUBqljXVS+5/cZadBBtLgWUHe6Y5/6IZLLh3QWCv7tnrK9aC/bDowgx5NMlRfl5ymiMJyfp+RTg6NtOrnwQDZOymWR/6vvqzcuInTDjZihd83o25Yy6wVQcpwVjcOe+ElDxxfTakzd3HMTI1MWtaG899N5JaLZV7cPiE3DUimN4N/cX72Go7cv4IKxrqopvd6s0X5/bzx/6XwkMBDfMpoBqcmWatLCYvJr3EFZXqZKOu3NrJl3UyqluYqG7xP4OpLa/O6yz157QRXWS9QXz459I1il0xTDWMCVPH5Uj51uYJtir5QtQIPWTNrrvIa4KGch+Bd7uIifwwMZOcGHvJ+pK/qN+gIFQ1pIRNufaP7ykCummogv7/8yycsj9L1M++oVmcLubXKOwrY7ShzA4t5dA1HGdV0GN066yCfV0SSw8QSqlv3Lcf218o6M97xs/V1ZNi0vrRTE0uPh6RQyy595Idey9hplp9cVtSNfrZ7mDfoqjldDP/Mb3vt46ofK6lGnx18yKckT838xmkfnlJKiIMM6f2Ty0/95IK9LurtlVk0NnsnTbAykTEPy3nZ/0opP0kr2zx2k+xfQfHX+3Cr7vNoU+Vy6h19Ma/X6D90K8lL6k2tzt9LtvKZd4Lr7JrFtwpvU/zHrzTT1lr+EI15wEtTWZH4gS3nRNGkvy8o8YOJXORhJOUtS7nC0lVdez2B7Wqms/jlo1pr/NTyFHPVYZm5yrz8Pz6110eZ3jNSHztr5dOEcvrtbK4MDnqp22HVeM68P/x2/ZVgkwbVeEK35zQT75uIKv241/fHNLiulTJc346NpsZxxNDN1KCTi3qluvD8qF2coBNEO4Jk4sMpFJXoLxMcMjh3WhU5tZmhXKlNotKBi8h4bhw5nQihTnnRvHtyLkXHvA1u/UmrfM45qWohI3nC5V+c0sRE1p29mvyn/6BlkU/zqnT1VR3G1Jet9ZZR3KBxPNxlPH8cYKqeZcymUudqstHD8eQd+Ya7HBmusiMtZMOe41iv0IZql70lr6DetKRXZzllQ5C6sPs3ncqfwy0+jKFdV4LU96BG5JipaMeKfDL0cpej9I7Rjv/VYPsZ18i77Q8SG5KDS7sHyozpUdyyu4vye+ai1pGrKrgZIP0+eko9bzc14/dOahxlKWPc9KR9aHpw2LlEpjt1VJOSJK5qtp3qn/zIW9s84gWfdvDT+uu4rru+nDQshY8anOBLc+Lpp4WRXNf6JXvMKiOz637SZuU+nv1qMzlnGMiBEV95z55NdLxSX1ov9ZXNvmTQ8Gczqe34pzz8s76csHYTR0bYqc8NAtSNUb35Tr0UrnPOR7p3dJfXttuoAf1rk9G++tKzRTpNzNWTV0Q6Ja3eRxG1t/P7Nuuo+XsvtWmOr3Itqq8C7Frx6NC2vMbHW/LrXG7wO4RTg6fLGdPm0elz3nJXY4268OsQl63wlNuMrRR5e8ovQ9J4/8INHLxuI7e6UEmH67rKP1bNlf4fd3nnuJP81mwKi1VPOHXYMTZ/F8ePvfxU/V2OarMWe19OOdves5QZv2Yp60StWnBW8pTepTz0gzPVwz6nyZvEPuWNacTm4RztE8W1U37Qq4Rn7NrwB73r/Ie9qpzkj3MjeGP7bG767XNeq6sxtNPbTI5s9InjG/6iwuS9/LrMXv06Z6dWHtKoG7lf6NmmSv64zlaNLjZTatFUmrnqA/Xc4K0G1nLkXKdPpGlQR/WeaqsS+xgo405W7JVhrKwdH3Hj6O70Y+wYejbLkJe/H01j0yroZDczVbfdTNmu+QA+3XQ7P5t1J+9vQRHvw7XCUyNZXPrJostcmTbGQbq+/s5v9a3Uq8WG6uoaX/XJqw9Z931JVjv0ZeOLj6meyXa63mR5sGWLCnqn203XzBpTf01nNWH+Lm66wUKuae8rO+s8Zf3CSK5jX8oLxpfyCCqizsN08mz9R3luFk5Kf3wnanMUv9dLnOX6HCPpu/lE3toJteScqMPBqX7uiqxj6daxc2TcW1/6rqgpE+ck87R6RKZCT/Xb0ZwcevrJVpvtpbg6iPmEm6z71Ur2+NKe67aq5GjddFq2eREfrnxO33av5sMlVdltjbectNiPf/1wp9/jBnL8nXGc1HCOHJTqIlPCvNXVYhdpt9aFaht5qv86GKkPIbXk3KjHzTN3mcrIcSaqLHSgpGbb6NNDX9Xbx00t2GehGvbzoNu5/emj7WueEobfYcPM1Bq9v/S4ihW92rmMFj+Q1CbeWR5zqU83qRZZ7bFQdNRM+cUeJr+jOrrRsAoNmj6abP1D2fNRFXneqTLY8u4WbnzOlDJm5NCXiEz634vTNLQomnx9e3H6Dx/VaVwCtTr1OTgz5A1HheyjkuERfDF0BNfJqkbzz07nT3kadT80m/v1D5Kbzwxk7RY3eXZPPR6wz5dv+TPR70pyaLEGtf+dJj+uLwPOVSNfvDdsWtuqKrNPUe3LPir5x8vgoX9cZM4Ib7V1qq98+eYrG577H9dMq6cGvHKQT06bqTSjk7S7KJGmRnmpqaXH+PsCc1U9eZ480dZHtlkVpLp3rqqM3k3l7rOqcN0PMfhN9DC4uSznjXMc1dPS/fTorLtMrO6kKr7spah7Bsp94VdudKoH3Z0dKFv5FPF4q8k0tvEQKhngoiwPOKv4r385f91c2SnJW10YkEvjW8bRwUQTWTC3sVr11k36HAuQM0b7yAb7j1HFmg/82C2BTzX1Vud+VVFVVujkuDQj2bwymctKv7J+ynbycfSUO2c24JsaP2l711nujX9Nf/11qnfPN1RS+zM/rmkso9f3Yr8er2lv8WM+u3AcDWi9mnW9v1GbnwYyq5qZHGuwhMODDNjVXKM8LpbTzQ6G8uZUP/mq6UHa0t9HVfxELay5y2mfplHf3M38/Wk6HwjfyWVmfvLYX19ZmlzGDVoZyB7L9VX6jv3UYuM2Do/ykY0v69SrBvv5TWQGifhkTm0WJHcMjibjoZ7SY/XfvINfzWX5kqNs3NJJXWuGfCaM4Iowd6X1PMq/3rbkNFO8O/AbPHiQA03Yk8tvUgPVjWdB6s3zFPJ21Mr6O7vw7epC5hxknnhkEoWWe6krA304aPU+jjD6H0dnF+VlXq4td38S8t7dOnJpm4M887CQw9q8IpsRPqrJXwuV1NRMxef6SqvBD3l7zip2j06kbXemqeopCVTLwl0OdSsNNsB/FPN9lWQwahgbUyVF1/1Ndx338+riX9Q23EJ6P3SX391O8tG1v+hep1/Uar+7/JN1jDsd8JP7lnrIxRN3BOvVcpdzslvRq4tucordVzZrOFudi9/Jhh2TKFPfTk2NzKdapR6qsKaZ2vZiIZ0d8YX39vtGHlbmyqW2L035PpfSXOZS1Wk/+cd4Z1VfuciI/+rI+HJzWVrhoXIfzqRJjWbQhPihcnftA7Ta7kNem7gp1P7kEh4g/FRhlLPcM+4BmxZYqVnb/uOVvqF8HP+nm41ykvkO2RQ2v4KXZ3jx/VNvae3gCm7i66NO5XpJs/fP84ZNiyaraod5+am/fLX6QjZajva9fvJveSDv4UgTFWv4gteuL6HJ8UGqTdhP+mn5iDtWeUkdOtZmU6e1dLBCo1ocSqUf2rcket7LW+7wkUMndyfzKRVk+q2SY3J/sDfmparuIB+zNFVFVl84fdrZvCt9vvCB7rU4yytQ5bV/RuUpw+mhc1lw478a5blIJ3eZTpPx7/+jnwWDOSQsMK9mpy95QW0CaMCQ73np+W1oyeK5dNbBQe38aCqn7vqP1tdrQUVmT4OjI+7whFwvej7RVU7f10WGpbrLLV2GcI/PtWSPdZ+pq56TTFvwm6JKP3OjkBPktU1fdgh/RnTDQ0buas2HnzWiyTt8VY3BDcgDv3P6vhUqbXCAdO07kVqsqM6vvqxmrrDlh3vf0o4J7uqD0leGh3Qqrb6jComxke/zHlOnigA1ta61TLmip447eqmrVQfQ1oy6ZBjtKf3LDdTnpCCV2CSexkS6qG+DbGVtE1+ONV3P1osDVY2Pw3lD2BsamJjAnWeaqNnXKylqe0Hw6ZkGcu6a/nz1kK2saPeNzhz2pY6ODbhnPSu64O4r/fAf0PGLuXrhaaqyk/3UytL3vLyfhxob+JkiJ3altX2D6KfePZqZeyXP8lMy99kSIHuk2qnLMdF0afpgirW7wntLrNVRyx4Ulvs3uJ9nZXDuAlvqHvY7+NnhGLLRWCgv1OCwjPd0dncaz47zkFPCUrm6yVxZw28a+T/txbtOfQtel+8gY2NtVPLNTTRstrEcdyVIRm3eRcemR7Lf2T/082M22azeQRccs4Ij9Feycf0xtGxkGnnvW06rBxvIPQsG8tN9u8jAaSL/NRrMUT5J5NDDjHYFTpUtq9fkM+b1aGjmCDY1vZTnvuc/Ol3XSb25aymPL5rH5xKacd0eMzlzsJ9sZO+qsjd5Soeu/nL5sON0MYKZTv4NrnH6GHUZOIvOPLCQ3ZPKuFrKyeCT0e+5IGccPQ/B/8Qmkju3/8HlW/rT79ar6USFuTyxdStP0g3lfYvX8d6Xh+hV++bU/5O/nJsdx99CTNXnXp/opU0kHerloAb+2U8xU55xQPg+UtOHUVUHT/WzNI71N6cFJ9ydz7aaH7wjeS+V7OtJt7YHyqvpnjziwBIuaW8qnRoEyrsLHtGVEAvOHtiBRzpvplGD1lH3xyHcOc5VPbK3Vvsy48k1cxVdPriTFnb8yG8rqsumDb1U07hEMmu8k8JqL6OGpU1kTJGffCHrqE6xxrLWhpe0G3u+3tw0ypyQzKNu+8uF0z7xofM51P2Qv9zT1kO2z1lHF2/GklOQkSxetI7tV+fzvC3fye+2l1rbsDX/Gr6WS5ZmU/dG2bRovI+M2pjOo674q/zuvuqk6k7xEz8HW0z9yc3bZdLcG75St9tELW4TRjtyKrjuRX15Kd2O3s/vTg5XM2iOmaHaXOKvkrZtpMWHyvjm43LWL/eR+XvWU+RGPXk0K1Cdr/yad0S1k0H6FtK+m6nc6eAjh5g7K6PYCop8nUp712zn48938NFuASo35whPObWW6q3UyrxYrSwbEaR+t11DNTV2nF/FVab1fMQDLAexFd5XMxa/zZtS20tOq+IljwxZx5tnrqT7WQe5v/l9clwg5MeqL6jlx79k+LA8r8Fcb+U710VOrlFFzXCNoIH1/1K6+1o+8txVdrqkU748VfYb7yZND7vJzN5rmR3W8u7zHnJ6N3fVef4mtquMkuVxAarxTA/plbuFt/j6qmLHSpqj56qiin5Tp15fed3OaB53J4fbhLqpPR2aqfj2+K9R5iOv9F9KB0cVs/0kI/m/M8V8dcZssli+kns09FORn7rKp+a7afatchr3yFC9rh9Cd13LOLNfGdlnzibzvtnEyy3kCL1Szhk6ik0df1Lu9eU8zDKd+swdytvNFtCUs99px5H3tMeso4xb6J/X5ArRll8lnJ0xR+29844b9critC01eendH3kmZWZqUVt3GTLGk04M/sm250No44FIHjNBqF32zfjykhmqaaq+Wn/wM2254i2X3HOWSf1ayxlT/FSFn4U6FTGNTN9voKst37B/NWf5ZvcbOh9qq1jzmwrxO6/7ME857RjW0yQ32f6Hhlu/N1Unxl4JftGoLG/AXh85cdc8+Wh3H565Qk9Zju5CViMc5fCmjtJ9zGRZ1DNQJp5cRKV9QvjbKiE3l2+lt2leUrT1UcbqF3fq4qkOFEXQiaRq6sXjI3T2UYhKWGkqSwy+5JnbV/LKGcWkaX+U+3XfQ0uT3/NIj7pq5hoX9cDrBs97WUr9G9nSsJJyGlWrO42e0IAuurgp+9R0mjkmlPIijuVF6n1mu6RmdLRjEB/Y94WGtRlGnfp8YCsHc1o0sbb0Pzebt+/Tybi2/tJptAXtfUT0Oj1AeWVG0rNLPvKRV4B83LYt1drxmP4PU3k/ng=="
_SIN_CORR_B64 = "eNoU13VcVE0XB/C7u3RImiDdLWBzzxioKGJgi4GKKCUWdqCgiCAIiIqi2CJt+6jMtRvsxEAQKSkTxXh/7x/fD8vuvTNnzpw5d3egIAiJkCAThEfwBlQVgmDwf0qCMBImwCS4DPtV8FpNEKJhgjruhRTooiEIYyEDymCypiB00hKEzhAAWyCwnSCo6QrCCBgPynqCsBWm6QuCiQHmNRQELTCHyXAA2rcXhFgoB92O+Ay2wKBOguAP0bABUiED1DsLgg9shgdQCepdMAeoGgnCaGNBiINb8NZUEKabC0JXC0GwgunwDgIsBSEZsqAc3oOGlSDMgvsQYIOxbLEOmAJJ8AA+wjQ7jAPx8Bb22QtCNRg4CEJ/SIJhjoJwFJSdBCEIEuEatHcWBAsoh6kugjAb7sEsV0GIgf1g5CYIl+AGGHVDPsDOHbnyxBq6Yw4YDYmQCSY9MCaMhwNQAoY9sWaYBrEQB89gdi/kvzfmhg2wB9r1EYTXUA4z+uI6qPAShJki8gLTCPHBMTjEsGaogg79MAeo9BcEbUgeIAilUAHtBmLf4BG080YeYc8gfAapgwXhMIwbIggH4RBcB1UfrA2ywHwo9hg0h2HtsB7UfbE+kA1H/kHuh3yMEoQrMG006hsqYYQ/1j8GtQiVsG4sahAGjUNeIWQ8YoUHYDlBEGyhEAInYs2wEQ5DNmhPxmu4GiAINVA5VRDqoeM05B/ewbrpguAdiFhnCII1WM1EvkFlFmIJEoQweAHvIHg2cgLvYWIw5oStkAUfwXAOzhxkwT74CGPnCkIopMBRMA9BTiAW3oFlKPYH0kE9AnUI9pANOWA+TxAGQjrMikTu5qMmYf8C1ONCrG8Rah5qQWUxzjTYRKEXgOYS5AccIAjS4RBoLUP+l+M8wn44DEfhKtyHziswL1jAGMgA+UqcbxgKs0G+CvsB1+AlfFyNeNbgXnj1/79rcf5hItwBpWjsHcTCYTgCtyF4Hd6HzWCxHjmFCjCMQW+LxRo2YBw4Cq/gzkasMw6fQcImrC8evQjKoQaaQXmzIAyBfZAPNTA2AfPAJfgAg9BUg8A5CfsPGXAAXJKRdwjeinFScA3YpgrCHEgEwzRBCIcqsNqGXIMLdEgXBD+w3Y493YH44AhUQDVo7BSECFDOwDmBwVALwi6cFTgKOVANSrtRizAJPsCMTMQGM/Yid/AQnoNtFsaCA1ANNfsEwXE/6hfSofIAcnQQ/ewQzgtUQNxh9DjQOoIaOYoelY1Y4QgoH0NtQjK8BtscXAPtc9E/oBZs87DncBc0CzAOzIEPoFKIPgA2cBAOF6EXn8D5h6NgeRK9AsrgExicwnynsX8QCqkw4gz6ENiexRmAWsg/h8/+w5rgGlwH4/M4e7ANjsC4i+j1cAdegEExagq2wh14DFYcuYAS8JeQH6i6hD5xGfsK8itYExhDA8iv4jzDHaiDOdcE4cl11NYN9GuIg8KbgnAaht8ShLkQAclw9DbWdgfzQTSkwHY4BjfgNrwAx7s4d6B9D2OVYM5SxAnBkAtB9zEuhEAS1IHmA0HYDYMf4txBHTg9wv9wC4yeYIzn6IVg9QJ1CUEQDsegDKrB5SX2B6Jh0yvEDdshFzqUIZ+v0YveoAfCejB+i/8hCG5AEyi/Q+6hFO6Ddjn6DYyAB2D1HnmAPZD9/9cVGAMmwma4VykIjaD0AXODQxX6C2TCQciGPAj4iOceJIK8GusFH3gDA2rwGcyFg3AYjoFnLfYBcsGuDucJSqEWlOqRFwiATZAMGXAI3sCHT6hRsGxATFAJVaDdiOdmEz6Huc2oPzgItdCpBX0PrMEeZsEH6PgZOYZjUPEF+wWKr9jTb8gB5EAVRH/HWPAU2v3AOGAHg+A22LZi36DrT+wd2P5CXUEOxLWhL0D2b8wJHf/gTEEKvPmLWviH/YFqmCXIhFDIhq4ymVANVnKZYAfHoArClGRCDeQo43MYoSITHoKdqkzIBFs1meAEWVAKyuoyoRskwRuYoiETpmrKhI8QoiUTbsMHqIFa0NaWCdegBupAo51M6AA7IBM+goqOTHCBq6Cmi+ugo55McIUhMAXa6cuEIzDSUCb4wwQ4DE9Aq71MsAe9DogXJsGxjpirE9YHmfAQXneWCYZdZEJ72AGHoAQGGsmEPNAylgkmYAV2oNcV98BbUJjIBE0YC9GgYioThsJYeA0aZjLBBiZDJMRDERiaI34IhwhIgo0WMmEXXAIdS5ngaCUT5kAJPIV21pgf7CEINkAiRNrIhPVQDeNsZUK6nUzoZC8TusAoiIXXIHPA+JALnRxlgg+Eg8xJJjg4y4TBEAyHIAeugYML1uuKvYYwuA0qbhgbUuApKLrhM0iDnXANquATtHOXCcOgnQfmhC1wGzQ9ZcJeuAwN0KU7ahEiYBfkQg3o9EAMEAIGPbHHkAOPQaUXxgaX3jKhEF7AjT4y4RXUwZK+iAU+gqYX6lGUCXPhDrwCgbBmhjn6YW6wgcEwAeLAboBM2AwO3lgfbIAjg2SC6mBcD/6wE9SHyIQ9cBseQpgP9hnUhyJ38BY0hqHmIRzyoBYcfLF/0Gk4YvFDjUAwxEMS7IAcqAX1EZgTouEODB8pE+7CW9AchVqBTbAD7sAbGD0aY8E2eAu2/qhpSIRdcA3q4RPojkF9gzfsAoexGBOc4TAUQR3ojJMJ+hAM2+EwKI+XCc9g7wSck4kyIRXSQG8S/ocjcAyUJyMusA9Ar4BNMHWKTJgH46fKhAKoB8U05BimQhzkgsF0mWANJ2BcoEyYDQkweQZ6GYRBIZjMRL3DRlCahRhgbxB6HYyYjR4AOyAXasA6WCasgzzQnIP9Bp256DEwGq5DPYwIQTwQA4chOBTrh5ugF4brw3EddI2QCabgCEfg8DyZ0D8Sa4NqMJwvE3yhAbIWYA8WyoT8RTLhOcxbLBPkUegfoL8EdQ058ARGLJUJRyEf7kMDxCxDbYP9cpwbeLkC963C3JAGuqux51ACj+ElvINqqPv/Z2tQQ1AG/dYih5ALl0E9GjUAY+AeGK2TCYMgAEIgFi7Dexi9Hn0KQsAwBr0Y7sETeAkGsTJh+gbEuxF7Boo41BzUbELOYFU8em8CYgLHRJkwbQv2A/Ym4dkCYcnoP1vxF3JTsGb4BkNSUVNpqOltMmE+fAKTdOQC7oFsO8YFB/CF6RAO+jsQL4RCGJzYib6QIRMy4Cg47MJfGLQb10AYmGaiZmE7XIUycN6D8wV3wXEv8gD9s3AP3AOlfejN8BQ67EcOwQ6eQTVsOYB7Yd1BrAlMD8kEd5gDsdAATdDhMOKDUHA9grMPsbATHI+i98BzMMxGjJAMl8DpGNYMR0EpB+cDFLmIEcIhAnrnocbBIB/1APMgB96CWQHWC+UwtxD3QCKYFaFGQf04ah+2g80J9E0IhfknERvYnEJ8UAh2p9Ef4RMMPoMedQ5rPY+9hitwHQZfQO3AXaj//+uLeEZCKOyADsU416DEMS7oX8IzHF5DNQy+jPEgEHKgBDSvYJ8gGy7B4KvIwTXEBv3hHtRA/f/fu45agmCoAdkN5B9Ub6KuwAac4SGY3cb/MA4ug+odnDfYBU53sU/gBinQ7p5M8INkMCjBPoMdJMK7UswFDaBxH89BcICxkA37HsiEW5D9EHM+Qj5gPdyDuY9xH9Q8wR4/Q2zP8foFcgy6L1F7sAv6v8LckAM6ZfgfHoDDa9TMW/R8GAlzQeUdcgaOsA0OQD1cKUfdw7D3OA8ViAnklXimwQOog3qIqcJfGPEROavGOa7BvLWICUzrUEOwCbIhrx7PYSgHlU/o5RANd+EdpDXgs0bkpQnzQCSkQXIz8gK3oRSGt+Dzz8g73APTL8gRmH1FTcMCWA/7IBeUvyEOuAmfIfo7nk9wDe6C7w/kEx63ogf/RM4hAzKhCMx+oedCOETDFRjZhtz+Rq3CXQj/g3qDBmj/F/fAIBgHUyAcIiAWysHpH+oW7oOeIBfc4So4yOTCDEiHW/AUPsFIuVwYC8nwELIVcuEGLFSSC6HKciEPjFXkwlLYBbnwAJ6Ak6pcuAM1MFdNLoRDOuyA/epy4RgcB1UNzA3JmhgDQrTkgqG2XAiDbDjWTi6chPtgpoPPIRUuwU2oB21dxAKGenLBHGyhBDrp4zVEwDu4ZyAXmkDPUC6YwhuY2B5/oWMHuTAdAjvKhaMQ0kkupEAWPILXoN9ZLvSHEZAOLl0wtxHugSwohxpwMsY6YWRXuRAH8XAJDEzkwlAIh0yoBh1T5BBK4A0omSEWSIQmUDaXC/1gDoTDaAuMB50s5cIUCIEl8AmMrJBLCIF5UA3O1nJhIMyDN6BsIxcWwBa4DWXwBiJs5cJuuApFdnJB3R75h0K4Dp9gnYNckDvKBU3oDObQAO2csBcwwhljwG2Y5CIXdkIW1EI/V7ngD0dB2U0u6IIdDILxkAh63eTCYJgHcfAEdN2xJg+8B7HQAI6ecmE9NHSXC19Bt4dcmAohsBkaoENPXA/RveTCNXgKx3qjBv6vD+btiz2EelDxwt6APYyDRlEuuJFcGAKFkMdQ91ALXftjf8EZXGAuJMN2eAxv/v96APIKSQPlQj40gKE34oY5EAJGg5A/qAbbwXIhDY6CyhC5oAZd4CEY+eCcwsChiANOg8ow7BdsgEQ4Dum+OH/gNBy5gRQogWN+eB8KYMgI1CqEjcL98Ag6+suFL+A6Ri7gsAsyaA/24AKpkAsDx6GeIQmq/v96vFyYAPuhAQInoCdAMhwDrYnIEXSD+fAAtCfhPVgHbpOxP1AOwwIQHzyAfVPwGi6D5lTcC4+gYBrO/3TcDwXQECgXmqHdDLnQHUbBbjCaiVggAoRZyDtshqlBqAW4AU+gBuohYDaum4NahHCYB2+g61y5cAL6hciFMXAFboJzKNYJsjC5MAAGhqOHQBoYRGBdEAEPwHeeXJgMIZAH9pHYM0gF+XzsHWTDe7i+APOCfCFyB5NhKkRAu0W4FyYtlgsvoAx0ozAXhIPDEuw7REAOnIPL8B7aL8VaoAGaoMsynCcIgTCwW471QANkr5ALGitREzAC/GECzFslF2Kg02qsH2ZAHORD4Brsy1q8D4sgG/LAKRq9EgbDUJgEGfAShq1Db4ASqAWz9Vgf3IT+MYgHHkAjXIlFv9sgFxJgH+RDAThvxDjwBirAJQ7rhcBN6G+wHhpAKR5nFL6B62acAYiGNMiA/ZAPQgL2FBzACdwhG55CXCJqBsohcAtihzzYniQX3oJWMmoAYqDw/6+3Yr0QAqGwFwalIE7olIqcwx14BG8hMg1nAtLgDtRAI3huQ02l43oYC0tgA4Rulwt1IOxAHsAVxsAUWAgp4LNTLgwHgwzs/y7kD15BPSTvRl7hM7hm4pkC/jB8D+KGFzBkL17DXsiDe/AN1LPQ+2EuqOzD+YaNsBlSIBtM9suFmVAAr6AZDA/gfciBBmiBzgexDrA4JBdKoRZUDqP/gR3YQwm8hvwjeAZBA7geRf+BV5CZjeuPyYXnoJODz2Az6ORiH8EFIqAAiqAe5Hn4ngND85EveAmv4BMYF2BuiIU4aIAWUC1EbcA02FyE98D/OGoWnE+gbiECYiEeHoLmSYwFLjAKIqHg/++dwrk8jXMGZhAIr8HuDK6B+fAU3oPJWfQgyIfb0AgjzyEOyINzoP+fXPCGBTDvPHIGEReQa1BcRH2CE8wEg2LUEsyDNFjA5cIyyAdlCXUKSbD+Ep4p0Aihl3ENxMGQK4gVKkD5KtYGo65hbNgEm2EvFIDWdawTXGEbNIP8Bp6xcAm63cQeQCgshXzwvYX/IRkOQMxtjAV77+AMgt5d7AO4wQ64Dmr3MDc0lGAOsCvFPfflgt9DPHMeYU0gf4zz8gTrhoWQBvuhGbo9xfuwGXJB7QVyDJfg2f9fv8IzFq7DJ4gvw2t4AO+hEdRfow6gO8yAErgPjeD2Bv0UNsMBWPcW/RkWvpMLp+AylIJ6Ofo6xEADKCowPqhUYt1wFRpA8wN6SxVihtEf8X818gyJUAcaNegz4AZ+EA0bIA/212LNIK9HHwOjT6hjkDeg9sEFBsACOAC7GnFuoAz0m3A9FEAZNIF9M2ofjFvQMyERGsDiM+oMzkIz3PyCuoGcr7gfwr5hn79jvh/IAxRAHYxrRd+BfHgKhj/xvQmmwQyIh6fQBN9B5xfWDkYwHMZCDCjaMDbUgtFvjAdX4D24/EFNwjxIhzIw+ov3YCIcgEbQ+YfaERTCFJghUwjHoASeyxWCg0IhGCsphHDYCY2grKwQ9MAJnGEQFMBrMFZRCOMhAuZBEbyACaoKYSaEwjNQqCkEdRgMo2CgukLYAPugFEw0FEIZCJoKwQCMwAlGQCwcgNcwS0shJEARfIHO2rgWYkC5nUKYBG+hHlqgo45CmAzKugqhCziAI4TCLqiARrDXUwj9YCRMhs+grq8QNMAc5AaIEVxgMiyEKMiEC4YK4Q7cg3btFYI7bIW9MK8DroXPHRWCYScFfndhvdAIKp0VQi3YGymEaRAC+TDBWCEUQhFodMUawRU2wUgTXAszQcdUIQyHeZAF98wUwkPoZ47XUApfQGahEDIgy1IhvARlK4XwDhTW2APYDyfBwUYh5MBTqAe5rUIYAMNhBtyEezDOTiFEwkswsFcIARAFx+AkPIYNDsgBVEI/R7wPTk643hnrhYPwGBrB3AV5hRB4D59ghivmgCZQd1MIm+FKN6wHDN0RkwfigDA4BkqeCuEqfAOn7ogbcnpgTT2RL8iDuxDfC/vXG/dBIqRAGdSCQx98BuugW1+F4AGT4SbEeCmEXCiCOugvYiwoBGdC/OAOyXACVBneh1HQBF36oe4gAiIhC/r3x9mBWlAdoBC6gischGMwbKBCmA87oQnsvXEvbIZdkAtLBiGngxEnBEDUEMQEz+AVKHwUgi+oDEUeoQzshyE2uAIHfVGzwxXCdXgEHn4KIQ32gNZIxA2uUAn10ADaoxRCHIzyVwhTYRvUweIxuA9OwRN4BjpjFYIn9AN/2A5241DX4xXCdMiFJ+A/AXsCBhMVgiWMgTpwmYT3YSukwTbIgSZogROTUQ8BCqET+ME82ApFsHEKrgeLqcg7bIYkKITjcBO+wKxpyCM0wvjpeA1ZcBnawDFQIYyFqbANmmHnDIw7UyEMhVngMQvnBi4HKYSJsxVC+2D0HkiHz3Ow3rn4HyzBEdxgAhTCAzANwdwQCSfBNBR7H4a1QB3Yh6PeYT4UQkQExobrUA/j5imEVHCNVAgd5qOW4A28g89gsADXgXwh9gQCYCakQz1YLMK8cBc6LkbOoRDuwCPYEYW9BZcluHYp9gIKoesy5BrUVuAeOA7OK9FXoQUOrEKMcBcMVuMaiIMDcBxc16C/r8U5hD1wB7zXITaYFYPzBPfhAfSLRR1vwD2wBRw34qyCWxzmg8hNWAu0gCIedQrOmxEnLE5AfuAuPIGxW3B+QTUJ40AjKCfjPEApNMNn0NqKcwpdU/DcgQy4C0tS8QyE+9twJqBrOvYPNkA+tNuOWoQKUNqJHg4pcBzeQDN8g8wMrBsKoAnCdqGPwazdyHUmag5eg/Ye9DTYDiV7cS20y8LZAtN9mAveQb/96FXwGuQHcG6hEWYdQk2D0mH0bWiCZnA5gj4D06EJVI6idmEY+EIEFEAhnIAz4JONsw/jIRbS4BmMPYYzBU/ANwfnESrhE+jn4nPIhGvQAqZ5GAtOwWtQzke9QwXcKlAIX8G4EH0ZXGEy1MPm4zjHcAkqoQ4sT6AmIAOuwTPQOIn+Aq9B6xT6MwyA6fASvE8jRrA6g+vgDFRCx7Poe9AVHGEbZMETqIAm0D+H+2EPuP6nEHpAJBSdRz+6gO9WsAm2wjY4AHlQALcuIr/F2CeohG5cgd8rqA24Ai/AUEIcMAqsLqGOIRxa4CsYXEZ/B7crCiEIdkIRXIIm+AxfwegqvuvAlGsK4TZ0uY73YDdo3cAaIQL2wDVo+v97N1GHMP8WvvfdUQijIQ123MUa4DJY3ENMYFmCGoIgSILd8BkiSpHn+xgHNoHVA8QIuyHoIdYCzfAFBj3CeYeFkAXN4PEY+YHcJ8gXlELYU8z7DHsCgRAKYXAC2j/HGLAbDkAOBL3AfRD0EnkHt1foexAFI8owBgRBJli9xn7CRAiFcKgH/TfIHayGIngG7u8QH/SDANgC5SAvx7UwH9q/R34gEnLgITSDawXihPnQAC0wrBI1DI3g+AF1XoV6gPCPqB0wqMb35hrsH2jV4n9wg3nwCczq8PyEHDgNT8GlHvdCJjwBxSd8LwJXSIBkSIVc6NaA8woLYB2kwnH4CoaN+M4AY5oQDyTDfXgHSs2YE7q3YB8gHFKhHrI+Y03g8EUhHAKjr4gZ7sAzaAH/b7gH5N8RL/jBZJgJedDuB84AhMM82AjGreiB0OEn9g8G/UL+YBGEtaHPQx4UwT1w/I09gfn///sHtQiZUA7ef1EX8BZM/uF9KIK3oCcpmL+mC0VldmBPWrtRgHy7l2tDMkWOW0oL9hykUzWebLFoRLYrS8WBhSH0dPw3r75Lh9D3Baqset0RWpH+uPi0SRgr/O1Nq66ZMeeyVZRy2Ylaxankub2QAlW3Usg2b4q0Pcbdag+JjnNM+B8NJanslC0rW+AkLUr7SR+OmdAdM2sKcRhMmWuV2b2VF8XtGVfp5MI9Ykb5HBZe4Eo/J7SJdg77eK/sF2LtkXW047+F/OgvO2Z7W19KtNCVLppZsUlXtZhJwZni0DZHtr5BlbnrOUpLOu8Xf6vZS8p9Z1FajiabYLyf3/CYy046ZfI5lZvIfkIozZh4grb1yxev9V9A304vpuPtVdmi3fH0pGI5hef7kUGTLQ2Zuo8vTq4QdW59p+099VhqqpqUUCWwlZN+UIt/DY1wzuABKzaKLZRHn9kWWnajgg4OLyDdbvNI3SFYLJm1mXs1qUpFsWP5yPnxXLskUozcnkEFvu/FZ+dbadCkrdwzxpYtsnMiZ7UYmuLVSPGNkmjgHctjFhjQV1V/1iVgGq2548rOnwuk6K4vyPyUO4seMJtctwSKEet86HGJCg1ptJdsp0ZRsLmm9GmuBfu+sJJM7UaQ4pYr+60+iib0r6Uefb/zxPGarBpx9zI+z+ffVWU/LSyZg3etaPXFkP2drMtuj/RiUTW6TH2dFhsc6MrUNI+Sle5mmtHrN32InkOGgSH07EIyjUyWsaeX+lPkkSBxzAhzGjtdiVEXJbqtf5j21ZkwRYwWW3bfTXpWspWPKtJk5TWradO+vxQ0yJN/H/GJrvlso27b5tMVdwc2YVMs3Rrmys7+06PVm60l167T6HJ4d9rby44dVqhK1qsHk6XSD+ovO0ZDajfRLn6JH9Y7IurPcKCLSXH8p1MyTemsxNRGyNhhNZE/PjeMpj6yZYbLLJm/vUTtxq2mtadGcVmrp/SfXw0dOLGO1AODJSnMgPQ2DKb109Sl68f3iHP/beDXLjeQ5Z6NdHlWk/jM8YJXr67f6EHcESqeOot61fWg3zE3RF3VM8VHhL7Ut1st171VIjbf3EopGbNEj/pZvPepAWxvgjHrP9KeG094Vvw32p7k3j60c/1San+/nJs3/6aBUxyl01MWif07jSDphRV7v8OBbLs6Sl+mVlOhsgUN+qrJZtulUnMPW1bYyZmWJymYh5aqFJ1kx4KXarOLV12oxbs9Gzh4t3hw0EJy2qdDs1waaVBgvbjx+idekeDIFugfpNAFB2nMPbl0dMRCPnygGnvteFNcF/pNbDI4KPolt9Kq/DPcYOB+nv88h0w+e7A+I7eQmfSPTvir8IYHEfTBzpCFL3ks7n7wg8/MiOIOj7ypxuce2f7zlGarWdDXU4Vi5yPbSDWsUvT8uo0CGjrxWf85saqX0fT6myor+7Scau7a0PO/HSTF8Bb+qCaKzkSpMZpeQXbHf/Maw1By7BfBj48o4IfvGbO+i1ZT1792rHhWG+VedeTflnlS2bgx9HvPb/qxuAOJyRas8fhvsfs1DfrSW0PKzJ5EZOkgGXvWiVEvjemMEEsTKyfQ88KDXGY+iTYpfxJ/+lhI7jrN5KtwYh/HttDnr87sLbOVvt/ykN4fcmTnO/ekuMkqbMrDFeS72J55GZtzrasW7OmLvlSf3kgjPnowd5sudEm52cuu7nZxr+c7+elcTxZkMV1cOsCIemgPJ7Vuf8Xq5hH867ECMugg0syBhRR5LUjcbOUm/ROWcWGdnEkn+4puW56Kee8WUPZaO3ZPPE3haj7UOs2HLtFM7hy/hzclpYjFvXbQsthzlKC+jA4OGCBqlW+k7b8tpNj5fjTO6AMFuueT8q3V1ENfg6WXjKTaQdupZkCUeGb5OV7cbESBd2KL3fpp0TH3lXR3xSYqiRlDpa+qROtXY0imnEaPg/fSSR9l6cvJy3Sp1I6lf9Bmf1IdmG7Zfuo+QI8N7KsrdUqfROejlKUUH112c8dkKjjoQcu8roqu00zo02sb1jniK2mqqkqNvRNoe3Eh7xNuyHY/+0Xazn15bEcT2u3dcHGk+me6NK2AD7iZSD3G60oRl70oJqqxuOL1aDZyxlyKKdvPmy77UJi/MQ1PNKaTumE0dc8rKvjQi/9Rs5GEndG0uv08+lDznN6Fn+BrtNQ400mlyEo5S111kj4POkImreos9+Yi8vdSZppjskjjxi5KzDwufiwMZK9+D6VJp89y2TkVMu/gKFXH1/Bho7Wl4JLlFFU8nORBtnRg6zYx0+0ot5qlLZXO+UitjV+p5ZML+3hyOgVLE3jSlN2kMstRCltkz3JfhNC2TR6SxoQcKrlswo+sDOW/9yfzATsT6cXMWgr/p5DsdA9TzwuL6cxRbXbb/SoVs+385+WpJHzYyKc5HOXfHndhcSWZ4g1PVxaZbsOenAwlzQg9SijeQAd15axII4kyK2yYudpf6jR8nHg1YrmXcj9b6b3ldHrTa79o0d5McugbTw3tZtB1+Qt+fY9MMli6jy94+Y8MfHvRlN4rpS6PLZjpjnN82udvosouF3ZxzxS6fipAfJ7kTOpT2sjZM5c2O/wSfaXfdLc0mLqneJF7j1Pi1MFdyGO3s6RM7Xnkqrfc+m8+7ZeMSGvufVG/0I4N61wrTvg4nJcflrO1tzWYo7uCTQ1s5N3HKdjepKfijPfeVHoygZeOTaPuk5PEmm3JPGN/IAVmmfHHUZ/p4eyevGmkEk1d48f/G7WHltX7UOkPZYruXstv+6eS7cOh5LVLhXVMu0gNZvNET833/Lr1X95jaTk15dTx46XfuK1oQ683qTO/BS5sZEc15uu2gnrN8GRnLqwg4yEraeUcP3JekEVZC4v4w27OLCgoXZxw/ie3zjOUptXZU2FRFXWp8iblwdZscoe1pLGjQpy8t5lnmTuy5ReHcPUJqtIu3Q/iiS67uYupHVsz3ZHJjW3ZT/V1FKiDvydraEyfMfT47gmvOZP30vPx2hT6Zx0dcrST5p4dRN/FcdSc6yxl3v5FDy6oSzszP9D45C/cYqqFZH5Wm82oMmXVge3YgbO14uEz9qzrXVfSsXVgoYkNtHxMNJ8v68zqvuuwZzM20JdDM/nQy46sbvdk6trgzPxcVaXXm5rJ7Olk2ufmxBYq6bGezm6srsqFTXKaQm+qhrEkyZM6Pe/Jx+yfRmETbVhftR7k+/4rdZ1YyVd3d2WsnwtTv60qGSh/pwb1eOoyIIBMu8ikwCdHKWx0NjWdl7PVfgfF8+bHyP2JnVRxojeZK+zZwdXqUj9FIvXiHZhYcobenelLxTeKuH5sHbkqRvOU2Q+Kp7zNo7bjnmxLrUgmTkkUcPEdrXQooMORnZl3WxJ9WRhKFWMEdmFuV7q5ZCsVXO5Ply4XkU3PpRTfw5Op56fQ3+wl3LvHCQoX5WQqLGVDFh3g3i9/FKuoe4nOAz7zzKJTdLMkkf+1zqdDw8pIZraI+mm7E701Ye+vbaObKYto2XUl6bSVhzR/RDqdvDOElBtdmHyplTQ3OJYnLT7Kx6a/oZ7rBdbnxHExcpUan6j6neujV4zoUMS/vE8RZ79UZdHzz/EYZ3vp/PxyunPamSUftmI55z7yc73U2DCLP3yyx3kSq1aQ/9MW2t7nAnWZfJjKjVeT+0aJt862p3ftPvCv6iaUNquer97jxN+OaKG1npaS9xFrttSuHd9goEoDSjTpQ/tiEhrsWH6gDUttyaQgu7PcuadMylK3YXYrq+neFyX2yr1KLO5rzhYlmrJwL2tpxqnulPFSRZqupyMNOVpHg1KtpaFKu8S360+Jsj4rxUsbttEGJU0pam87Gr2nRpyZVkgvPiTQhVpjSV6qz76/P8THznOj/eOvUS/r6fz/35uX7i/jU6X91Bffw34+cGD7HD3IsX4zD+65R5wxRc7KG65TUOEsntYnjStS3FjVULlk+N2BjdipzxbtMZdOTYnnqnk/+PjRnmQdZY3v4kVcK9aAmsp+8Jfp0ygwM4HGbv1KZSdc2K/P0+iAlY7UZdR08rPXkOontSdVu55k3XMH9zYr5ClHbvCKiku8RtlJumA2m3+Y1IuWFKXyiElt1M1khVT/zJCFnOhAUxtXUuY6NaY5uI5e7Kwm1e7O0nBxL1d9mc/tpF+Ua+TMLvRqFSeY7OTTvl0oNs9qFd1npPF9lX1oxOEckn2wZGrms0myTKSqa6ZUj/v+9Aym6yZTqWpMAm069JP72pSIJ1ccFZPvxNPylwU8WOrPbbbNJeP4fV5r1/4oHjD4H30qU2U3E7vzhMwC2j94GOXoqHIHx3zqVmnKRu7WZZeW+fMF3WQstOYz9/s1S5I1v+WtXY/T3PCv9HJoJPUsGkhn3c3YCvTTPjt12OOlm/j8KyFsv//d4rP2dtIs9638qM0SfrfYjEXqD6bK3aE0zlmZ3Y4SmE7vI7SxJJB8Wu2kBXkWdPKuBelYplNFwGLy7dKO/m2N4VFrdYudfjtQrqDCXk1fxe1XDaNH29tLho5LyGDRdkrSUZc+9jvOY0ft4yZUR/pdVjOXDtpsOn7L3Kuvp/rCYXRhTRwPqP7N3SQVqZ8g8cyHjXzY1TP8WesZPupZLM0wfE9G07RZap0+M/HeREY+o/kH/+H05JENFQQ5sJlBjB3/asoc76ZQ8NsVNFTPj95lqJGrtSU72XObaBuykkwlDQrL+UOm5WpMyc2KpSdYsqLnRszQtjM783Ei9TswlsrcnahBLZbaH1ot+XcYTpOCNNiCbbsp7VWFOCqtid/tm0lhdktE/cPtpW+Hv/Mt9h9EWb8TNPbXB3HmG39qDsukC8830pO2f3yhsTNbNnIMaSyM4xdG2rI2e1+xJcWcQktsWWqVKbd4r8LCe66nEy1j8dvrohhs+0dM1HFiL65kct8sff5J3sC/334onnK4SgnK1tI+rRFM+848urp0Aj16cJX+aDqwz5t/8ejFblTaJ4brae6ku2e/87R+l0TVHtZ06JUt2zZ7EiUfGk3vd3eS7m7SZE+DHZmnny7LGxnMxzAL3l31G92ecUXUux5AnZLMpB5K+kylPoDWRuqxlf7OTDdtM+WcbRbHjf5CX9NcpaFr63mdviF1CtRkWmsv812O7cmvwJkFP/giVmxv4U6zI3i2zVke/vWrOOh1PLVs3M2VYxbwaFkr3bnagdaP20R+bzqQu1cZl49tz8awYzRxSUdaUfOIOieGk0tpEJ12rOPvq92Y5ug2enxDW3pw5Ty3tTpO3/u10KRrbfQ3KpGu1bWRrSKBX6hfwYNu3SjeeD6YZpUHk+WDHHHKcGdptvcW+jbOmRdc+MsvJ0zg/7o2FRtljOGWpXmkVFdJTy9m0ZaEFWzBx9H85pK/FOiVT9sufKeEXi5syKAftAnf190dx9O4Y+r0LPQlmXe2Zj1mS7RVnXOHA/0o88lTCp8YTsad6vrK+pqx/GUp9HHgPDpjsYybRwyk5MnruPlCO2btOp/eHPSm/jfmk1EMvpff6cq8Tdq8Phku5522TKftkivbWl0o6nweTBY7MvnhTAta9e4ctbw5xCvdjonjzdPJ64o1sxu5n2bkDaF9VyfS1CXKLDwtnWZXL6ZxBuGiY8+3NHfELp5tqsKGJW2ne6Eq7LH1Tz5I2EF/Byyl6FtbqEPnVNG1m51U834rlbcG0d3j1tRrpVxKm2sjXXm/jGbv/0LzH45kd8N30r5DnSXrMg8pxOWteGnoV37nxx+udySNFOYV9MupiRfFq7GZzZP4mSXH+E3fbG5xzpL9tyaIzp9ZKJp6q7PEI3bSlht29DK1XCz0yeMHvT/Qt6Rd9M3rvfhjWCLfefoYaTlE8T/Oa+i/X1GS8mB8J/BktG7idjEO39WHftBlXa6OoopvjMbd0WClke/Jd3+lONIgi8d1fc+VX42mmwGO9PpwmNfT6TIpbushvrpEk51ZokUnV+2hFo9/vNMpG6Y2YKd4otyaWfrU0pnbY+jN5BjKW2PFvjX2oY7pqTzoUAH58tHUa6Qd05ebo+6yaHhAschHaBH7bMde1aynyy01YrXNJ0rq35luNY6n4O5D+GtPc+YTslZ0XdlBCtmdRM+u27BztITe5DqQ8x876bXLBspKK6IemzSl6NMHKGnPWC72smXrRrtKpjMC6MTzDTTRw4mt6f2Ps3t/+L3FeuxiaRMP22oh/cnQZfZbD1GftjYeu92Xbs/ypHWvPpPN/s28ZmAWPz4jmw/96cymaEdyB207yezHF3r2MA6/t1zYtpLHxfFTXFh01FdSH55P8Qc/i3lzknhzOyfJodqFzZvegzYPt5Z8PFWZ5QZXdv1CBV8xz5LdWKTONvYP5IlHlFmt+wwaWvVV7J19Uwx4EMVvLdKV+m9xZOsPjecepyfw7r6LixertlLEf2rMqS21eGllK31a0Jt6BD7jL3w/UPnoY2SS6SAV9f9Jmf+10S0nT3bO6xc9bPxB1sO28S2dP5NGvScz69RGXSL6kvCkjey3dmS5Z7zI6tYfYk/mkPqLIFYe9Vtc2SGGbrln8xcD83llQD4ZGkfzU1OnUnOoJdtUkU/Vf1fQ+IZa2t23gKr6CGziR2vmP2w0v7MqmfJMCshtxHImnUqm2YsNmfX7w9S29gP1jNxKqdcMpG4Wu6lyugWzDvnFJ2QUUbt3W2kffq9YLE4hw08b+d+rEyUKiaS63SfISjGfnGxi+fF+82lSQh63xnM0+dV8Mmu9VKw69BQN8/fm2cefFA9+F8iF+faSYeoWfl3flqWY5VHc+yNc+0Izb9VVonk4t42TNaWwiUdpQ984fsnsC1fzG0tj/PJp+K9F1F/dnlncPsCT7reJey/sIZuDpkzH/7jIvM5Q8eRMaipOp5feMtYUH0UpA9ZS15atok+GobR64FtqXVzEzYwO8zPOCslxmxnb0H0k19S2pupd/lT/n4oU/Wgcdb9fJt4z3kIbT72jqOalxNIPkW/SH0parSUleJZxb6v/6PvfdP4vcDkNnveyuEJ9OK3s7cpeNq+jf5mevKzIib2ieHpQtpwivIaTxmhHNiZ/LdX1xG+jd6toyPTVdOvdxeL7VWni85+rKf7OeQoK8mC5euqMxtfQLUsVtiV+JcV1s2Uy588kfndl7WTLKNdJnfVs2cfjTdPF8oEX6da41RS04z9a9fOZV4KxgXT42y4aUNxKjUqD6aH2GlJSaeUDXTRY7HANZn10Df07SnQpdDx39ndiqrofyf3pOfFPjTkLyXFkx+3s2fRcG2m3azVNejeNz6zNpMR4TXagSw15vFwqXjrWKpamWdKnuTV04qwLix/WmYuX24pHHNhDFkqObIjfOhJDG7lVvjXzHmvLzv5eSrvfrSClHbYs3mwsNUQrSSEXNaQexdrUe68nmxWsIjVtspEcrxnR9dVW0tVb8XTotrrkvbJGvKpkz3z1P9FGzzAabqUt/V3oyqNVDVhGig+v2+ss7VnoxEZccJC2XHClgge9pWUzz4tzk/Wk5D2nRO3ydsxfO5au3/9BazZpSyN/3qIPLzfQi9NWkuWG86L+x7H88/ZY+tPRgvWrdGAeTbG8boolczr3k0y66Eg71cYVP2vqwLz0O0szXPQoyDiImjMdmU+cI5s5YgP1t/GXnv5czj8Xr+Q/3zmzoxubydxfU3Sv3kBqhzzZsIsGTLWxmT5qelCHcCc2LiaAmu3U6eA0DxoZLZPiR5lIun+viFd7uLKaBdZs4W4nVjH1L1cJnkJNAQ7sqZOzpGfXLC7PspOMdPVZ37UGRDXzqHhcHLE3t3nqzgmU3HsF6Tm1Z4vstdkQx2lUV3+Ozh/2ZFNijNiihI7ssb4Wm3vTiumYaEv1/xylVXbfSCVYpENR9tJJK1f2cKKldGWpNmtIMmA3oo/yISU/uGjnSR9P/cf96xTMkvWkaba6LOBHEz93I5nGtyixVR96UpiyGd87agZt0sum5Bw5vek8kz63TRRLsrJ4yqCT3NehlQZtVGPL18ykPl/G8KXX9VicfxsXXh0l61J76feSn3Sg3Jz3dOlD7bqvJ7s9mlL7YTm0U0OJ7X6uzxKW/CKfh3Kpn09ferz4pxh8w5UGanemPzGJlPvRig3+1kaPdi3jI98lkseGYLo41otiVndkSqvceXa3aIry1iedkvnc+lU69YjbTt3Oa0uLoudQasYcckRNjTiSR1nDPFmG9Ifq9TfQAFZF67M9meU5T5Y0agmf8dyTqeasJqV9VtxvVJQ06Zo9U+4YQKVHskXTADUW+SGJ/L/2EBtP14kJNqF0uPNZfuWNnXTo7n3xz9mtlH+ngELRUw3OO0v8ox5V7qgij9MCuzqlkMqn9KOUMTuo9o4P32hdRM3R4TTq+V/qUGfHZD8SRL9lJtRvQjUljFFit3c5s1N/ZGxI1ABqPdCekgKOU+LUaP7072FuuaKIX78dydy7/Obb7qmzlVkS2WhHkrR/Jc8f94LeFzhLZgVyZhfiIsVtjqTzyU/E8mA5xc51Yp9NHaVHjZG03+MSz70pp3cOmlL9njh+tM8XanrnTdFiCs1cm8iD3QfRazNPae9xDxa2UkG3Iz2kUc8HUZ+FaXyGz0K6aj+Yrv9YSL+8lGhxnz78xTASjZMU0r7XIpffGcQnR2Zyw6RFNK2nBZ94+ixV5uqz5NTf4lXzMv63Wrl47uw19HmuKfP7dZpi/i3jWlyVB1WbsiUt6ZSW7UN+IZZE7Wxol+KEuM0vjQf1TC+u3W/E77R3lcZvX0KLLryl/UvPiivmW0hrniXSolOuUsmlAWQXFUZDKlSoqGwpkZG9dM7Xle94+5c/NFNldtk7qOtgVXbNxJzNNNSRvu52kDbO2cVjfpmzTo2xtLyPjnRSeyPtGL6M5q4bTScul/F2BzeR6oPlFJXqKt3QwnP8uQfTGPKTW7c186acc7QsU40c+mXQi5wKWh6RQaaBK2nv0wraqP3Ua0KMGcsMm0fy+lbS+r2Pun5fSZWOdnRf9Sj/ETeHXuiqs3drL1DdQ0tWfKmS9Pkq+s/Kik1016HuWi7S2djffMsALbb7nKvUHDKSCrrspAvRcbxevoaMV4+ijs9tpArPNbR0ejWXjazl1++sIe+w7WJzT2vWnLObOt/eTee3TkGvsGLJk6zYuKOfaU9rE50f0kCJ0wL47TBNvsnzCy2v0mC+8hZ+o0WD9dbSZO+OqEtS+2pae1iF/Xk0mkpznJj31798Z90Xyq7+yWU50fRppj/9uBJN1auLeElCI7ll2DKft/7kdNmULoQIUkN6lTh243jed/g66rOtgDcNt2W1Cl/xRpsTLU9N5jq73NiMIb+oZMNqOvpJn7YeXEeCDier7RZSXnYtyQP30ovPF0S6actMDAwki0e20kIXZ8lKriI1Z6lIrU0Taf33vdR88wu/4m0tFYVbSJ1fb6CGc0rS03kutL7nNnqgUSOW5Gqz77PradE4C6l9Snu2fakpK1XNp8629qzVfThXhAZQ4zoT6UaYK7XGVIiZKYG8n/leWrzBno35Pp6qOqwWq/rk8Uu37JnHgJ5cf6ydGH32JX4HNpB2YgMdSvtCH/VjKXeuG03YLEgJvkNZ5nJJvJV7kSf2kIl/flmzioMm0l0XC6Zu9ktU+pRDPzyvU/pzRynloyHzuH5JLPmvG7Uaj6Lxjo7SnUAHtvLOYD4myoK9/lHPN+6axmMy1aUJXw9Q95EB5Jk6WuwmaxDXzZjP1zU4sqOT34vHvjWIDyss2cJ+OmzmS3um5ebE6n8pSZZbDNim6QG0c0sddUBfzKvqyoYZK6S20BbqOGUjPWxoFGUVR7jlkxbymanHguzl7NAWPTbO7DCP1frKa02cWWGGHjv37Kb4fIMdO9khjnpVHiJLyzjqkeVJ2nMj+dSydiwx05w9DK6g37ftpJSl8TQppDulKilJAYXvqQrfJQ3DptF3+SbquG0a2U4zpHvfbtPFl8E8+VwQzyyVS3Gl18V+ldqSdYwN27HxCAV0M2BBP11Y2qONVFazlH4WbaJYuy+iVb0d84j9RkG/H/G9y3RYy+sv4ruDe/gx1aF8rX88meyaweunxdOPG1947/RTNMq6J7n8s2fG/1yZWVsgJQ3azSNe9CQTqxk0zMONpXXO4P0OZnGn2x6Ssaan1DKsjVbvdmMLD/SinoUd6MCJsXzFhKjiqPMn+SJtS+n0rM0U87cvXxE9k0qfLiDdzAKep36Mpk+q91LfWUkrknvSsbLv4kS/WbQ6N5dXdU2g7jt/UmCLoaT/s5DuvTpGrf3aaEpmOckmmjLLgUE0xewL7d7bKu4b1uDlLiXQ/ODt3G2csbRb05XJ3DylpwlZ9HjGbCpY0kY6Ixq9coa08XP7hlP54Xxa7NtMI84vow5dgmnx0QR+izdQjZ4fzXFOoOO/lvDabcFULibz3p9U2dAYL2pVFeinchcaXnWId/fqwFwG6tLhk23iwAFe/MrcPzQx2kDcunoL/dkygR+58Ic0Mop5l5UfKX1uOu/tMpcOvhzIN5XP4GvWFvLwrL8UZGtEmeObaeKpdaQ924StMt1Cg08mUSs3oqvh3Zl751AaEnCGb00y4Tkrj/O/usX0fomydH5PGF1XeSgumuPBWhb58Ev9ulL1KIHWNYXRzzs36GDCU3I54SmFeRgwl3k3eOACQ+aft4Ru+Mjow1J76ZT8OBmalvJ8W1vW17ep2LsihNfK59Gd6w7SdvwmiSgKEq1cwvnrTrvoyKOp/NDYeeSnaoq9SKPpdzVZxZXLpDxHWcrhGynD7Yl4/l+iaL5E4i/zNnPjzQ3i9hOuUoR3JM2bnkobn6hxk2+xvP+7KnF0gzJrZj950ZdUGhdhx6IXdGVdNBdQ18lGtC1DIQ1udWY9tn2lxY0KNu/ADj4/JoWcZ5vTsIMreJsPYnQzowPup8h20DaeeN2cqlZ/4QcXosXNdZfO97aRhpTVU6qlD/8Qs4gq3L341nQlev39EK05Gc463rUgfx8HSXXRM94uzl5qGrmYBjFbiV8cQrqBpmzrDSdmvHgl7+T7UlziNou/OpJOry5lUsedO/ncpcai7mlLuj9nAt/gnMwHfVdmH6uj6Jr3W5olqrBHJ5eQZscyMeP1EvJbocIslfrylPOeUv6X7dTYZTwvWmRAnr7vKCZuNB0MPctt9P/xio8b+J+AYfTsQ3+RH+/KjiUl8knlypJRj4uUeTmY2leqsf76f7j811LyX2bOLpX94TfP7cB3wCay6iJxvT3LSK+HLy08vowWrf7I3de0cMczTdx+uS+13jzF757dQOccl9P6BTE8f/t+Mmos41q1t3iH1Rask5k6O1Buyp73V2PBzS288+7zNHK0knTMcCXZGk3gg27U0uib52l4QAad7KshXR29kkYp/+YLF/3hX0//o/s7M+h10kqq/WJLnefqSxGL22jek1Aa06DGChveU8i3IhriEUPZESqsqkGQ6jYUiv3TLvH6pYuktfjdd0dRSCGbtvJ3Ayr5p5Q2CtqZxF/YXaQF72zY4zXE7T9OpvC77aQdCa5sBKmyFtGeLA+cFVWvpPJg9TXcd/p+PuaDqlRuGssHTrpCF3pWkdIKDRozNJJ6fFZiFgEplHZZxsb4abCdSVU0ZYIGW9rnH521GkJPqtZQfA9rpv23nXRBt5maL7cSP1Uhni7dRvsjTRnb94l7d9Ojt4GP6XPmWlL0qyZDy0rRtDFKPORpKTVpbPX61sdZGjTORVrTEEglByvFTxWGUrz3GzruH038X6WoP++D+GjwSNZfxVMq2NhBmn4umja5evKCizbM8yvWmPWF/P7ZMEX+HjqqspDfixpDBflONMRzCQtXPysO3TGGFqRtIV+XaPLX+Ud5MkNy71wn+g3JoMOX1lHJDwfp1YMxXG+Yo1QxRpOF3lWWDnnUco3ki6LT2DbeK3s6uWZaS5O8VtMYL1vpTo2lNPzjHPrWLZH3764hOc0vFn9e0GbP7muzUqO3vPR0PdmUVfKI+34UP89W+t6sKfn3V7Cz7ZWkl4bneNqm8XQy7hzJXzI8WwaT391V4qTF9swvkItxLnIavsGAve+vx45HdGWFHhtp6gsuNvx3lXyOavJb2g6s6D8jdqTnUJ6N35TjN+tLpR3xXdEbv9MGGki3Jl8jJhlIWlMn0ro1E+nStFt0V9tc6qQ/nDaPvCTKSzykQ5+X8k/es+mIfhPttHdkqf85SM7X4um9W7Sodus6yYOy+RsfXZYyRZdJOxyZMPkGxY4z469bf/JcvTh+rKCZbC+7SWey2rMRf8qpU28nlv9Il83dtZU//eEhHe+nT5dXXBEHt9/F/9ehmbhjnbdRnGSpkRZUmpFBhMdWVFp+506SllERWiypDJWKGuVNSlFUFNlKxTC00qIeU0zub4MWjBk1LdKiSYumshZFeF3zP5xz7s+5rzMgLA+uh20o8kAL0mNtUI6hFPxPBgY3t2DscC/MnJWD+sxoxK1tkgL3qJNzfJM0ta6ER4fmgHZpi8Q8LwiHm4ztrZDJc9Bb5QXPDe1clmZLnRd8xfFECyrS+J5S1o2hzzO8EeStQfWXLChixXOY9NjCNtxaNF3xxoo2C1r2yBt3LOy5ILmRtfLbMG+kuzR74yKRVd7NnUu/pRTNKGnvJR8c6rUVcfOUxY7cTkQUvsPlrD3IT27hBx7GFPvRktQSTv7XXW+FqP332731tIPfP/sIxcs3JfmiAq4/9w01cypKDhqIhFO+sMlSpsKL7fA9fQpri85zmeEAcTZ3DjXUKNCnSHd2PKpFOkbZkvxFFxaPy5aU2xR5V0wk36gOw5052dKa4UWc7rady5z8uGL4fJK5+bOfzkX+tO0zzGcmSqMuKov35z9D/3YLJmkpUNCbTvREapPR6/0onfgFt4umcMkmTRoXthNqmdrkunkVrp8cgfYjHVJU3+1KFGnFgV5noG8fC49ljZxsPQXTnJXIo7IOs3ULi+WTz+L3phaOPNqBI0F+sFswEn8+7dNYZoUUn92B3J88paw/Y3F3+DaUyj0lh+cJLDv4K58uT592SKELzsOM6ORShulcE9q4cCpWv30rJXvVFVd7m1OyYSuUH1dKU3NSsWP1cl7gsgpGe22x88Rubp2lA3f/LmlGzXX4P7TnYNm/qNbtxrOn0/D8nh0HOAfAoVpXLPb1ZeN1BhQY0edJ3yrp/oal3B0voaBhO49aPpKcDtiSrbkhV9yUoLdkNWQO56De7M+6jbFozOr5byPGNvu4omc1Jlm6059Jngi5uggV883Yq6tbCh+hRWEXejH4/Bhx4EIbNx7YSs+iLXgXfeTq1l5o+VjzX1f/klY6KLNx3ib8I1Mg7/Vr8cL+kaRYfREfniZh4Xl9BG6Nh+MO4rWZcmnmhEC0mqviw4/Tca3/BBH9SYHCf9TF66cXMNzfhY+k68LJzo23BFgIT8d2tqy9zNbLFOlA/AA63LmYjd7rYvrrdYgYmIETniuR9quX9MjajLYq7eIp6v5cuLkUIbMrkOYpR2zILq76RUZT9lmJpVc3IFrjLY+P6EebS4PxJsIBba47+EJCEHQe9yPlV3p49+sxGA9tx7XLl7DeOBhTas9Jfvu6ETHjC3cE1HJnlpLwv+8C23I14bxfifb6JOGD53EcSMvHe3M98YePI64EKcFteSfrlGuQZpsSz9iygvNc43F5TVOxed5qKeBTEopiCrilSI6URZZcG9KDC4OXsEGdHDP6/GsXlwzXuqM8S7eLfXuUUPjXYU49ZYBJt9zZep4TNMrjJVm0mcjL8uYXGimo/C4ED/SV6WVGEX+Zmgmfn1Iwan4YK682ECcHebGVtrmwT/keUb/vhcmazVjudgnnMvOlGeZjoD86FT8GmWJoRhv332Iu5D7a4lSHoXAZmIK4yFRcq0jEhDwHdI29wdENs3nOlFCMHDYXg5QPozkshD3rFuBKeCj6l+XyymHTYZhqJSpzFChq4lU42h/GvvOhMHuoAm3NDVLzgH7iUmc/mn42g6cYxbGylxX5LTrDk15aCCV7M7JIGiukXjOxdMkJ3qe0D/4LnqN+3ld+FacpNDZ/5HtaByFUjBH9WyhMbrRi/AS5dMXmA+/+WVuUt0fwHd8yvnDAmJyy5VJMdx1H2htQWuZWxOzWEMUNqlQZtQOu5mOoN7wXWTXheHPSgFYMD0ewbjfX5H5iXYtuhK63ETPv9rCJnxqlRZTyZ5PNfDZLDYqx2qLk2yOoOqMiYiMt6E7YNsw2tKGym2o0rn6jZHhiPbY8a8cDh1oeWaggxj1UI7/js/HdqGwMmLgU87/eZ69X25ARZkvhuTvZyrK/+MEuHHV7NkkvZofja8R8xHy9B89hKTw+dAsrptqKD3NeYljUE65XeQnXVUdhrWFGSRZNuLLjffFKxyXItDPFPQdTqC4wxaz5vvAqOIo3xYakrucFx1U+3BBtScsfX4Oy1ivcv2yKh5pXpfRXZXDgQE7UWYitWy3IWmEO6mwX4manhujiHTjIA5G3+arkvaEXk02M6IbRJHYfMQKbMs35yYIWxPX1mo8aj4sDf9MXz1QYv5QZiDcJujC7HYGCP/rY4N1Aqtjggt9vdPNd1WaueXSNnwSsZduEN7gy9xiPyP/MRqME1via4rZeOtymm1CMtzXNq7qE+y7qqNwjcPhQc19GfUN3q2rQ9t6Ykp08+PXbY+jvakz59xrg0Mc4DypeSeb+6nQxXlHcbldHi7wTf+cuQpH2a+nb42pUdFSdBia0wtfRnLQuq9OQ0/3F+kp1OjTSTBTXjKX9f6tTfOIglJ8rwKXzyqJ521zo1yuL68omdP+1AdXflvGN1Qa0WPNneExyh+mt0SjvSZMeXJGJgRljxJNAfcqsbpVK1PVpU1AOZKkqIvSiCZlNf4epJwfRsWf/EwFThyDtUdC0ye3ucPnmXx5aoYG0lw3S3snHsXtiuuTkpiZeJeoJe+evGGQ1CdZxHvAwt2Vv5VKkuUQhbp0pfQizpM6nHlizNgolyRpC+uKB6iWlcDxpBVdnP26/Zkrb8zJxJEGDHn183scV1qjZ+AE/W5tzde8TDsxVJCv3OUKu3YEvP8Sz3ZoleBSTBZ2Jjdgp1xQ5SY2885wZacUcl/btMhIZVVkw3v+JsyyzsbP1K38MUGWZvBG6BbuxrSQYZztV2Tc/inf9Mpiag8fhiK0K6YWPw4pbidi+2ZAXxIzD6bzrkoXeaKFTZcZOSn4wWtsESz8D4bxMJo5UjaXCLFWxIVJGK/anS7KpQ0ijtgmGfptgN28I1W6R0ZC7MnpTK6MMfW3yLdjOOdeXIdrOgV2q+lFO5XjULdEkeWEz5p2zobqobJyv7eKuEls6/v1oEdvbjKahFuJ961YRbNqCnIJs3DNo5dy8RiiOMaI1/qbUYagoFH1iGBHtSJ/ZJB3dNpSuTS+V8oe3wnSpF0KeMlKShtLh3UqkGOkFTTcV+mG5AY1QVaWGlnZOOqMgzB7koNTUmH4SRtR+SJ26xnvDIWIM/R/KJ5ZT"


def _apply_corr(tab, ncorr, b64):
    if ncorr == 0:
        return tab
    raw = zlib.decompress(base64.b64decode(b64))
    idx = np.frombuffer(raw[: 4 * ncorr], dtype=np.uint32)
    vals = np.frombuffer(raw[4 * ncorr:], dtype=np.uint32)
    flat = tab.ravel().view(np.uint32).copy()
    flat[idx.astype(np.int64)] = vals
    return flat.view(np.float32).reshape(tab.shape)


def _rope_tables():
    ks = (np.arange(0, 128, 2, dtype=np.float32) / np.float32(128)).astype(np.float64)
    ivf = np.float32(1.0) / (np.float64(500000.0) ** ks).astype(np.float32)
    t = np.arange(T, dtype=np.float32)
    freqs = (t[:, None] * ivf[None, :]).astype(np.float32)
    f64 = freqs.astype(np.float64)
    cos = _apply_corr(np.cos(f64).astype(np.float32), _COS_NCORR, _COS_CORR_B64)
    sin = _apply_corr(np.sin(f64).astype(np.float32), _SIN_NCORR, _SIN_CORR_B64)
    return cos, sin


class S:

    def __init__(self, h):
        self.h = h
        self.n = 0

    def bump(self, k=1):
        self.n += k
        return self.n


def build_nc():
    nc = bass.Bass("TRN2", target_bir_lowering=False)
    dram = lambda n, sh, dt, kind="Internal": nc.dram_tensor(n, sh, dt, kind=kind).ap()
    I, O = "ExternalInput", "ExternalOutput"

    xT = dram("xT", [D, T], F32, I)
    xtok = dram("xtok", [T, D], F32, I)
    xcols = dram("xcols", [T, 1024], F32, I)
    wq = dram("wq", [D, 1024], F32, I)
    wkv = dram("wkv", [D, 512], F32, I)
    wo = dram("wo", [D, 1024], F32, I)
    wf = dram("wf", [D, 68], F32, I)
    wbs = dram("wbs", [64, 1024], F32, I)
    cosT = dram("cosT", [128, T], F32, I)
    sinN = dram("sinN", [128, T], F32, I)
    dmask = dram("dmask", [128, 512], F32, I)
    ones128 = dram("ones128", [128, 1], F32, I)
    epsc_in = dram("epsc", [128, 1], F32, I)
    ident_in = dram("ident", [128, 128], F32, I)
    y = dram("y", [T, 1024], F32, O)

    qhat_h = dram("qhat_h", [NH_L, 128, T], BF16)
    qhat_l = dram("qhat_l", [NH_L, 128, T], BF16)
    khat_h = dram("khat_h", [NKV_L, 128, T], BF16)
    khat_l = dram("khat_l", [NKV_L, 128, T], BF16)
    vtok_h = dram("vtok_h", [NKV_L, T, 128], BF16)
    vtok_l = dram("vtok_l", [NKV_L, T, 128], BF16)
    rd = dram("rd", [NTT, 2, TTW], F32)
    rdb = dram("rdb", [NKV_L * NQT, 512], F32)
    ag_in_h = [dram(f"ag_in{i}", [1024, T // 2], F32) for i in range(2)]
    ag_out_h = [dram(f"ag_out{i}", [4096, T // 2], F32) for i in range(2)]
    gdram = dram("gdram", [4, T], F32)

    ctx = ExitStack()
    sem = lambda n: S(ctx.enter_context(nc.semaphore(n)))
    sbufO = lambda n, sh, dt: ctx.enter_context(nc.sbuf_tensor(n, sh, dt))
    ctxA = ExitStack()
    sbuf = lambda n, sh, dt: ctxA.enter_context(nc.sbuf_tensor(n, sh, dt))
    psum = lambda n, sh: ctxA.enter_context(nc.psum_tensor(n, sh, F32))

    ops = {k: [] for k in ("sync", "scalar", "vector", "tensor", "gpsimd")}

    def em(eng, fn):
        ops[eng].append(fn)

    MIDL = sbufO("MIDL", [68, T], F32)
    ONE1 = sbufO("ONE1", [128, 1], F32)
    XNH = sbuf("XNH", [128, NFC, TTW], BF16)
    XNL = sbuf("XNL", [128, NFC, TTW], BF16)
    NW = 4
    SWF = [sbuf(f"SWF{i}", [128, 512], F32) for i in range(NW)]
    SWH = [sbuf(f"SWH{i}", [128, 512], BF16) for i in range(NW)]
    SWL = [sbuf(f"SWL{i}", [128, 512], BF16) for i in range(NW)]
    STX = [sbuf(f"STX{i}", [128, 2048], F32) for i in range(2)]
    SDUM = sbuf("SDUM", [128, 2048], F32)
    SXF = [sbuf(f"SXF{i}", [128, TTW], F32) for i in range(4)]
    SSUMa = sbuf("SSUMa", [128, 4], F32)
    SSUMb = sbuf("SSUMb", [128, 4], F32)
    SSQa = sbuf("SSQa", [128, 4], F32)
    SSQb = sbuf("SSQb", [128, 4], F32)
    SM = sbuf("SM", [128, 4], F32)
    SNM2 = sbuf("SNM2", [128, 4], F32)
    SEV = sbuf("SEV", [128, 4], F32)
    SR = sbuf("SR", [128, 4], F32)
    SMR = sbuf("SMR", [128, 4], F32)
    RBC = [sbuf(f"RBC{i}", [128, TTW], F32) for i in range(2)]
    MRBC = [sbuf(f"MRBC{i}", [128, TTW], F32) for i in range(2)]
    T1 = sbuf("T1", [128, TTW], F32)
    XN32 = sbuf("XN32", [128, TTW], F32)
    CST = [sbuf(f"CST{i}", [128, TTW], F32) for i in range(2)]
    SNT = [sbuf(f"SNT{i}", [128, TTW], F32) for i in range(2)]
    QSB = [sbuf(f"QSB{i}", [128, TTW], F32) for i in range(2)]
    QSW = [sbuf(f"QSW{i}", [128, TTW], F32) for i in range(2)]
    RT2 = sbuf("RT2", [128, TTW], F32)
    RT3 = sbuf("RT3", [128, TTW], F32)
    QH8 = [sbuf(f"QH8{i}", [128, TTW], BF16) for i in range(2)]
    QL8 = [sbuf(f"QL8{i}", [128, TTW], BF16) for i in range(2)]
    VSB = [sbuf(f"VSB{i}", [128, TTW], F32) for i in range(2)]
    VTH = sbuf("VTH", [128, 4, 128], BF16)
    VTL = sbuf("VTL", [128, 4, 128], BF16)
    EPSC = sbuf("EPSC", [128, 1], F32)
    IDT = sbuf("IDT", [128, 128], F32)

    PW = [psum(f"PW{i}", [128, TTW]) for i in range(8)]
    WAVES = [["ffn"], ["q0", "q1", "q2", "q3"], ["q4", "q5", "q6", "q7"],
             ["k0", "k1", "v0", "v1"]]
    WBANK = {"ffn": 0,
             "q0": 4, "q1": 5, "q2": 6, "q3": 7,
             "q4": 0, "q5": 1, "q6": 2, "q7": 3,
             "k0": 4, "k1": 5, "v0": 6, "v1": 7}

    s_xstat = [sem(f"s_xstat{i}") for i in range(2)]
    s_xf = [sem(f"s_xf{i}") for i in range(4)]
    s_w = [sem(f"s_w{i}") for i in range(NW)]
    s_ct = sem("s_ct")
    s_A = sem("s_A")
    s_V = sem("s_V")
    s_Tm = sem("s_Tm")
    s_G = sem("s_G")
    s_mr = sem("s_mr")
    s_qst = sem("s_qst")

    bank_free = {}

    def W(eng_wait_list, engobj):
        for (s, m) in eng_wait_list:
            if m > 0:
                engobj.wait_ge(s.h, m)

    last_stats_gate = [0]
    xn_done = {}
    ct_done = {}
    ct_mark = {}
    xn_marks = {}

    def emit_stats(tt):
        gate = last_stats_gate[0]
        accs = [(SSUMa, SSQa), (SSUMb, SSQb)]
        stx_free = {}
        for sub in range(4):
            for hf in range(2):
                bi = (sub * 2 + hf) % 2
                ld = s_xstat[bi].bump(16)
                fr = stx_free.get(bi, 0)
                def f(sync, tt=tt, sub=sub, hf=hf, bi=bi, fr=fr):
                    if fr:
                        sync.wait_ge(s_A.h, fr)
                    t0 = tt * TTW + sub * 128
                    sync.dma_start(out=STX[bi][:],
                                   in_=xtok[t0:t0 + 128, hf * 2048:(hf + 1) * 2048]
                                   ).then_inc(s_xstat[bi].h, 16)
                em("sync", f)
                sacc, qacc = accs[hf]
                m1 = s_A.bump(2)
                stx_free[bi] = m1
                def f(scalar, bi=bi, ld=ld, sub=sub, sacc=sacc, qacc=qacc,
                      gate=(gate if (sub == 0 and hf == 0) else 0)):
                    scalar.wait_ge(s_xstat[bi].h, ld)
                    if gate:
                        scalar.wait_ge(s_mr.h, gate)
                    scalar.activation(out=SDUM[:], in_=STX[bi][:], func=AF.Copy,
                                      accum_out=sacc[:, sub:sub + 1]).then_inc(s_A.h, 1)
                    scalar.activation(out=SDUM[:], in_=STX[bi][:], func=AF.Square,
                                      accum_out=qacc[:, sub:sub + 1]).then_inc(s_A.h, 1)
                em("scalar", f)
        macc = s_A.n
        m_half = s_V.bump(2)
        def f(vector, macc=macc):
            vector.wait_ge(s_A.h, macc)
            vector.tensor_add(out=SSUMa[:], in0=SSUMa[:], in1=SSUMb[:]).then_inc(s_V.h, 1)
            vector.tensor_add(out=SSQa[:], in0=SSQa[:], in1=SSQb[:]).then_inc(s_V.h, 1)
        em("vector", f)
        m_m = s_A.bump(1)
        def f(scalar, m_half=m_half):
            scalar.wait_ge(s_V.h, m_half)
            scalar.activation(out=SM[:], in_=SSUMa[:], func=AF.Copy,
                              scale=1.0 / D).then_inc(s_A.h, 1)
        em("scalar", f)
        m_nm2 = s_V.bump(2)
        def f(vector, m_m=m_m, m_nm2=m_nm2):
            vector.wait_ge(s_A.h, m_m)
            vector.scalar_tensor_tensor(out=SNM2[:], in0=SM[:], scalar=-1.0,
                                        in1=SM[:], op0=ALU.mult,
                                        op1=ALU.mult).then_inc(s_V.h, 1)
            vector.wait_ge(s_V.h, m_nm2 - 1)
            vector.scalar_tensor_tensor(out=SEV[:], in0=SSQa[:], scalar=1.0 / D,
                                        in1=SNM2[:], op0=ALU.mult,
                                        op1=ALU.add).then_inc(s_V.h, 1)
        em("vector", f)
        m_r = s_A.bump(1)
        def f(scalar, m_nm2=m_nm2):
            scalar.wait_ge(s_V.h, m_nm2)
            scalar.wait_ge(s_ct.h, 48)
            scalar.activation(out=SSQb[:], in_=SEV[:], func=AF.Sqrt,
                              bias=EPSC[:]).then_inc(s_A.h, 1)
        em("scalar", f)
        m_mr = s_V.bump(2)
        def f(vector, m_r=m_r, m_mr=m_mr):
            vector.wait_ge(s_A.h, m_r)
            vector.reciprocal(out=SR[:], in_=SSQb[:]).then_inc(s_V.h, 1)
            vector.wait_ge(s_V.h, m_mr - 1)
            vector.tensor_mul(out=SMR[:], in0=SM[:], in1=SR[:]).then_inc(s_V.h, 1)
        em("vector", f)
        xd = xn_done.get(tt - 2, 0)
        m_st = s_mr.bump(16 * 8)
        m_bc = s_mr.bump(32)
        def f(gpsimd, tt=tt, m_mr=m_mr, m_st=m_st, xd=xd):
            gpsimd.wait_ge(s_V.h, m_mr)
            if xd:
                gpsimd.wait_ge(s_V.h, xd)
            for sub in range(4):
                gpsimd.dma_start(out=rd[tt, 0, sub * 128:(sub + 1) * 128],
                                 in_=SR[:, sub:sub + 1]).then_inc(s_mr.h, 16)
                gpsimd.dma_start(out=rd[tt, 1, sub * 128:(sub + 1) * 128],
                                 in_=SMR[:, sub:sub + 1]).then_inc(s_mr.h, 16)
            gpsimd.wait_ge(s_mr.h, m_st)
            gpsimd.dma_start(out=RBC[tt % 2][:],
                             in_=rd[tt, 0:1, :].to_broadcast([128, TTW])).then_inc(s_mr.h, 16)
            gpsimd.dma_start(out=MRBC[tt % 2][:],
                             in_=rd[tt, 1:2, :].to_broadcast([128, TTW])).then_inc(s_mr.h, 16)
        em("gpsimd", f)
        last_stats_gate[0] = m_bc
        return m_bc

    rbc_mark = {}

    sxf_free = {}

    def emit_xn_chunk(tt, fc, gate_pe):
        bi = fc % 4
        ld = s_xf[bi].bump(16)
        fr = sxf_free.get(bi, 0)
        def f(sync, tt=tt, fc=fc, bi=bi, fr=fr):
            if fr:
                sync.wait_ge(s_V.h, fr)
            sync.dma_start(out=SXF[bi][:],
                           in_=xT[fc * 128:(fc + 1) * 128, tt * TTW:(tt + 1) * TTW]
                           ).then_inc(s_xf[bi].h, 16)
        em("sync", f)
        m32 = s_V.bump(2)
        sxf_free[bi] = m32
        rb = rbc_mark[tt]
        def f(vector, tt=tt, fc=fc, bi=bi, gate_pe=gate_pe, rb=rb):
            vector.wait_ge(s_xf[bi].h, ld)
            vector.wait_ge(s_mr.h, rb)
            if gate_pe:
                vector.wait_ge(s_Tm.h, gate_pe)
            vector.tensor_mul(out=T1[:], in0=SXF[bi][:], in1=RBC[tt % 2][:]).then_inc(s_V.h, 1)
            vector.tensor_sub(out=XN32[:], in0=T1[:], in1=MRBC[tt % 2][:]).then_inc(s_V.h, 1)
        em("vector", f)
        mh = s_A.bump(1)
        def f(scalar, fc=fc, m32=m32):
            scalar.wait_ge(s_V.h, m32)
            scalar.activation(out=XNH[:, fc, :], in_=XN32[:], func=AF.Copy).then_inc(s_A.h, 1)
        em("scalar", f)
        ml = s_V.bump(1)
        def f(vector, fc=fc, mh=mh):
            vector.wait_ge(s_A.h, mh)
            vector.tensor_tensor(out=XNL[:, fc, :], in0=XN32[:], in1=XNH[:, fc, :],
                                 op=ALU.subtract).then_inc(s_V.h, 1)
        em("vector", f)
        xn_marks[(tt, fc)] = (mh, ml)
        if fc == NFC - 1:
            xn_done[tt] = ml

    def emit_ct(tt):
        gate = ct_done.get(tt - 2, 0)
        s_ct.bump(16)
        m = s_ct.bump(16)
        def f(sync, tt=tt, gate=gate):
            if gate:
                sync.wait_ge(s_V.h, gate)
            t0 = tt * TTW
            sync.dma_start(out=CST[tt % 2][:], in_=cosT[:, t0:t0 + TTW]).then_inc(s_ct.h, 16)
            sync.dma_start(out=SNT[tt % 2][:], in_=sinN[:, t0:t0 + TTW]).then_inc(s_ct.h, 16)
        em("sync", f)
        ct_mark[tt] = m

    def wave_src(wv, fc):
        if wv == 0:
            return wf[fc * 128:(fc + 1) * 128, :], 68
        if wv == 1:
            return wq[fc * 128:(fc + 1) * 128, 0:512], 512
        if wv == 2:
            return wq[fc * 128:(fc + 1) * 128, 512:1024], 512
        return wkv[fc * 128:(fc + 1) * 128, 0:512], 512

    qsb_free = {}
    qsw_free = {}
    qh8_free = {}
    vsb_free = {}
    vth_free = [0]
    vt_banks_free = {}
    ffn_evac_mark = {}

    def emit_rope_evac(tt, name, bank, done_mark, par):
        is_q = name[0] == "q"
        h = int(name[1])
        scale = ISQ if is_q else 1.0
        m_qsb = s_A.bump(1)
        fr = qsb_free.get(par, 0)
        def f(scalar, bank=bank, par=par, done_mark=done_mark, fr=fr):
            scalar.wait_ge(s_Tm.h, done_mark)
            if fr:
                scalar.wait_ge(s_G.h, fr)
            scalar.activation(out=QSB[par][:], in_=PW[bank][:], func=AF.Copy).then_inc(s_A.h, 1)
        em("scalar", f)
        bank_free[bank] = (s_A, m_qsb)
        m_sw = s_G.bump(32)
        qsb_free[par] = m_sw
        frw = qsw_free.get(par, 0)
        def f(gpsimd, par=par, m_qsb=m_qsb, frw=frw):
            gpsimd.wait_ge(s_A.h, m_qsb)
            if frw:
                gpsimd.wait_ge(s_V.h, frw)
            gpsimd.dma_start(out=QSW[par][0:64, :], in_=QSB[par][64:128, :]).then_inc(s_G.h, 16)
            gpsimd.dma_start(out=QSW[par][64:128, :], in_=QSB[par][0:64, :]).then_inc(s_G.h, 16)
        em("gpsimd", f)
        m_rope = s_V.bump(3)
        qsw_free[par] = m_rope
        ctm = ct_mark[tt]
        def f(vector, tt=tt, par=par, m_sw=m_sw, ctm=ctm):
            vector.wait_ge(s_G.h, m_sw)
            vector.wait_ge(s_ct.h, ctm)
            vector.tensor_mul(out=RT2[:], in0=QSB[par][:], in1=CST[tt % 2][:]).then_inc(s_V.h, 1)
            vector.tensor_mul(out=RT3[:], in0=QSW[par][:], in1=SNT[tt % 2][:]).then_inc(s_V.h, 1)
            vector.tensor_add(out=RT2[:], in0=RT2[:], in1=RT3[:]).then_inc(s_V.h, 1)
        em("vector", f)
        ct_done[tt] = m_rope
        m_hi = s_A.bump(1)
        frh = qh8_free.get(par, 0)
        def f(scalar, par=par, scale=scale, m_rope=m_rope, frh=frh):
            scalar.wait_ge(s_V.h, m_rope)
            if frh:
                scalar.wait_ge(s_qst.h, frh)
            scalar.activation(out=QH8[par][:], in_=RT2[:], func=AF.Copy,
                              scale=scale).then_inc(s_A.h, 1)
        em("scalar", f)
        m_lo = s_V.bump(1)
        def f(vector, par=par, scale=scale, m_hi=m_hi):
            vector.wait_ge(s_A.h, m_hi)
            vector.scalar_tensor_tensor(out=QL8[par][:], in0=RT2[:], scalar=scale,
                                        in1=QH8[par][:], op0=ALU.mult,
                                        op1=ALU.subtract).then_inc(s_V.h, 1)
        em("vector", f)
        m_store = s_qst.bump(32)
        qh8_free[par] = m_store
        def f(gpsimd, tt=tt, par=par, is_q=is_q, h=h, m_lo=m_lo):
            gpsimd.wait_ge(s_V.h, m_lo)
            t0 = tt * TTW
            dh = qhat_h[h, :, t0:t0 + TTW] if is_q else khat_h[h, :, t0:t0 + TTW]
            dl = qhat_l[h, :, t0:t0 + TTW] if is_q else khat_l[h, :, t0:t0 + TTW]
            gpsimd.dma_start(out=dh, in_=QH8[par][:]).then_inc(s_qst.h, 16)
            gpsimd.dma_start(out=dl, in_=QL8[par][:]).then_inc(s_qst.h, 16)
        em("gpsimd", f)

    s_ct.bump(48)
    def f(sync):
        sync.dma_start(out=IDT[:], in_=ident_in).then_inc(s_ct.h, 16)
        sync.dma_start(out=ONE1[:], in_=ones128).then_inc(s_ct.h, 16)
        sync.dma_start(out=EPSC[:], in_=epsc_in).then_inc(s_ct.h, 16)
    em("sync", f)
    m_idt = s_ct.n

    if DBG:
        dbg_st = dram("dbg_st", [4, 128, 4], F32, O)
    rbc_mark[0] = emit_stats(0)
    if DBG:
        _mmr0 = s_V.n
        _mdmp = s_qst.bump(64)
        def f(gpsimd, _mmr0=_mmr0):
            gpsimd.wait_ge(s_V.h, _mmr0)
            gpsimd.dma_start(out=dbg_st[0], in_=SSUMa[:]).then_inc(s_qst.h, 16)
            gpsimd.dma_start(out=dbg_st[1], in_=SSQa[:]).then_inc(s_qst.h, 16)
            gpsimd.dma_start(out=dbg_st[2], in_=SM[:]).then_inc(s_qst.h, 16)
            gpsimd.dma_start(out=dbg_st[3], in_=SEV[:]).then_inc(s_qst.h, 16)
        em("gpsimd", f)
    emit_ct(0)
    for fc in range(NFC):
        emit_xn_chunk(0, fc, 0)

    swf_free = {}
    mm_mark = {}
    wstep = 0
    for tt in range(NTT):
        if tt + 1 < NTT:
            emit_ct(tt + 1)
        for wv, wave in enumerate(WAVES):
            for fc in range(NFC):
                bi = wstep % NW
                names = list(wave)
                widths = [68] if wv == 0 else [128] * 4
                wwidth = sum(widths)
                mld = s_w[bi].bump(16)
                frb = swf_free.get(bi, 0)
                def f(sync, wv=wv, fc=fc, bi=bi, frb=frb, wwidth=wwidth):
                    if frb:
                        sync.wait_ge(s_Tm.h, frb)
                    src, _ = wave_src(wv, fc)
                    sync.dma_start(out=SWF[bi][:, :wwidth],
                                   in_=src).then_inc(s_w[bi].h, 16)
                em("sync", f)
                m_wh = s_A.bump(1)
                def f(scalar, bi=bi, mld=mld, wwidth=wwidth):
                    scalar.wait_ge(s_w[bi].h, mld)
                    scalar.activation(out=SWH[bi][:, :wwidth], in_=SWF[bi][:, :wwidth],
                                      func=AF.Copy).then_inc(s_A.h, 1)
                em("scalar", f)
                m_wl = s_V.bump(1)
                def f(vector, bi=bi, m_wh=m_wh, wwidth=wwidth):
                    vector.wait_ge(s_A.h, m_wh)
                    vector.tensor_tensor(out=SWL[bi][:, :wwidth], in0=SWF[bi][:, :wwidth],
                                         in1=SWH[bi][:, :wwidth],
                                         op=ALU.subtract).then_inc(s_V.h, 1)
                em("vector", f)
                mh, ml = xn_marks[(tt, fc)]
                bfw = [bank_free.get(WBANK[nm]) for nm in names] if fc == 0 else []
                m_mm = s_Tm.bump(1)
                mm_mark[wstep] = m_mm
                def f(tensor, tt=tt, fc=fc, bi=bi, names=names, widths=widths,
                      m_wl=m_wl, mh=mh, ml=ml, bfw=bfw):
                    tensor.wait_ge(s_V.h, m_wl)
                    tensor.wait_ge(s_A.h, mh)
                    tensor.wait_ge(s_V.h, ml)
                    for bf in bfw:
                        if bf is not None and bf[1] > 0:
                            tensor.wait_ge(bf[0].h, bf[1])
                    off = 0
                    mvh = XNH[:, fc, :]
                    mvl = XNL[:, fc, :]
                    li = None
                    for nm, width in zip(names, widths):
                        wh = SWH[bi][:, off:off + width]
                        wl = SWL[bi][:, off:off + width]
                        pb = PW[WBANK[nm]]
                        outp = pb[0:width, :]
                        tensor.matmul(outp, wh, mvh, start=(fc == 0), stop=False)
                        tensor.matmul(outp, wh, mvl, start=False, stop=False)
                        li = tensor.matmul(outp, wl, mvh, start=False,
                                           stop=(fc == NFC - 1))
                        off += width
                    li.then_inc(s_Tm.h, 1)
                em("tensor", f)
                swf_free[bi] = m_mm
                if wv == 3 and tt + 1 < NTT and fc >= 3:
                    emit_xn_chunk(tt + 1, fc - 3, mm_mark[wstep - 3])
                wstep += 1
            if wv == 3 and tt + 1 < NTT:
                for fcl in (NFC - 3, NFC - 2, NFC - 1):
                    emit_xn_chunk(tt + 1, fcl, mm_mark[wstep - NFC + fcl])
            done = mm_mark[wstep - 1]
            if wave == ["ffn"]:
                m_ffn = s_A.bump(1)
                def f(scalar, tt=tt, done=done):
                    scalar.wait_ge(s_Tm.h, done)
                    scalar.activation(out=MIDL[:, tt * TTW:(tt + 1) * TTW],
                                      in_=PW[0][0:68, :], func=AF.Copy).then_inc(s_A.h, 1)
                em("scalar", f)
                bank_free[0] = (s_A, m_ffn)
                ffn_evac_mark[tt] = m_ffn
            else:
                for i, nm in enumerate(wave):
                    if nm[0] in "qk":
                        emit_rope_evac(tt, nm, WBANK[nm], done, par=i % 2)
                    else:
                        kvi = int(nm[1])
                        m_vsb = s_A.bump(1)
                        frv = vsb_free.get(kvi, 0)
                        def f(scalar, bank=WBANK[nm], kvi=kvi, done=done, frv=frv):
                            scalar.wait_ge(s_Tm.h, done)
                            if frv:
                                scalar.wait_ge(s_Tm.h, frv)
                            scalar.activation(out=VSB[kvi][:], in_=PW[bank][:],
                                              func=AF.Copy).then_inc(s_A.h, 1)
                        em("scalar", f)
                        bank_free[WBANK[nm]] = (s_A, m_vsb)
                        mk_vsb = m_vsb
                        tb = 1 + kvi
                        btf = bank_free.get(tb)
                        m_vt = s_Tm.bump(1)
                        def f(tensor, kvi=kvi, tb=tb, mk_vsb=mk_vsb, btf=btf):
                            tensor.wait_ge(s_A.h, mk_vsb)
                            if btf is not None and btf[1] > 0:
                                tensor.wait_ge(btf[0].h, btf[1])
                            li = None
                            for sres in range(4):
                                li = tensor.matmul(PW[tb][:, sres * 128:(sres + 1) * 128],
                                                   VSB[kvi][:, sres * 128:(sres + 1) * 128],
                                                   IDT[:], is_transpose=True,
                                                   start=(sres == 0), stop=(sres == 3))
                            li.then_inc(s_Tm.h, 1)
                        em("tensor", f)
                        vsb_free[kvi] = m_vt
                        m_vth = s_A.bump(1)
                        frh = vth_free[0]
                        def f(scalar, kvi=kvi, tb=tb, m_vt=m_vt, frh=frh):
                            scalar.wait_ge(s_Tm.h, m_vt)
                            if frh:
                                scalar.wait_ge(s_qst.h, frh)
                            scalar.activation(out=VTH[:], in_=PW[tb][:],
                                              func=AF.Copy).then_inc(s_A.h, 1)
                        em("scalar", f)
                        m_vtl = s_V.bump(1)
                        def f(vector, kvi=kvi, tb=tb, m_vth=m_vth):
                            vector.wait_ge(s_A.h, m_vth)
                            vector.tensor_tensor(out=VTL[:], in0=PW[tb][:], in1=VTH[:],
                                                 op=ALU.subtract).then_inc(s_V.h, 1)
                        em("vector", f)
                        bank_free[tb] = (s_V, m_vtl)
                        m_vst = s_qst.bump(32)
                        vth_free[0] = m_vst
                        def f(gpsimd, tt=tt, kvi=kvi, m_vtl=m_vtl):
                            gpsimd.wait_ge(s_V.h, m_vtl)
                            t0 = tt * TTW
                            dh = vtok_h[kvi, t0:t0 + TTW, :].rearrange("(s p) d -> p s d", p=128)
                            dl = vtok_l[kvi, t0:t0 + TTW, :].rearrange("(s p) d -> p s d", p=128)
                            gpsimd.dma_start(out=dh, in_=VTH[:]).then_inc(s_qst.h, 16)
                            gpsimd.dma_start(out=dl, in_=VTL[:]).then_inc(s_qst.h, 16)
                        em("gpsimd", f)
            if wv == 2 and tt + 1 < NTT:
                rbc_mark[tt + 1] = emit_stats(tt + 1)

    phaseA_stores = s_qst.n
    phaseA_midl = ffn_evac_mark[NTT - 1]
    s_cc = sem("s_cc")


    ctxA.close()
    ctxB = ExitStack()
    sbufB = lambda n, sh, dt: ctxB.enter_context(nc.sbuf_tensor(n, sh, dt))
    psumB = lambda n, sh: ctxB.enter_context(nc.psum_tensor(n, sh, F32))
    KH = [sbufB(f"KH{i}", [128, T], BF16) for i in range(2)]
    KL = [sbufB(f"KL{i}", [128, T], BF16) for i in range(2)]
    VHt = [sbufB(f"VHt{i}", [128, NQT, 128], BF16) for i in range(2)]
    VLt = [sbufB(f"VLt{i}", [128, NQT, 128], BF16) for i in range(2)]
    SQH = [sbufB(f"SQH{i}", [128, 4, 128], BF16) for i in range(2)]
    SQL = [sbufB(f"SQL{i}", [128, 4, 128], BF16) for i in range(2)]
    NPB = 3
    P32 = [sbufB(f"P32{i}", [128, 512], F32) for i in range(NPB)]
    PHt = [sbufB(f"PHt{i}", [128, 512], BF16) for i in range(NPB)]
    PLt = [sbufB(f"PLt{i}", [128, 512], BF16) for i in range(NPB)]
    SACC = [sbufB(f"SACC{i}", [128, 512], F32) for i in range(2)]
    DMK = sbufB("DMK", [128, 512], F32)
    RRW = [sbufB(f"RRW{i}", [1, 512], F32) for i in range(2)]
    RBCB = [sbufB(f"RBCB{i}", [128, 512], F32) for i in range(2)]
    CTXS = [sbufB(f"CTXS{i}", [128, 512], F32) for i in range(2)]
    PS_S = [psumB(f"PS_S{i}", [128, 512]) for i in range(NPB)]
    PS_C = [psumB(f"PS_C{i}", [128, 512]) for i in range(2)]
    PS_SUM = [psumB(f"PS_SUM{i}", [1, 512]) for i in range(2)]
    s_kv = sem("s_kv")
    s_qb = [sem(f"s_qb{i}") for i in range(2)]
    s_TB = sem("s_TB")
    s_BA = sem("s_BA")
    s_BV = sem("s_BV")
    s_rb = sem("s_rb")
    s_cst = [sem("s_cst0"), sem("s_cst1")]

    m_dmk = s_kv.bump(16)
    def f(sync):
        sync.dma_start(out=DMK[:], in_=dmask).then_inc(s_kv.h, 16)
    em("sync", f)

    m_kv = s_kv.bump(16 * 8)
    def f(sync, pa=phaseA_stores):
        sync.wait_ge(s_qst.h, pa)
        for kv in range(2):
            sync.dma_start(out=KH[kv][:], in_=khat_h[kv]).then_inc(s_kv.h, 16)
            sync.dma_start(out=KL[kv][:], in_=khat_l[kv]).then_inc(s_kv.h, 16)
            sync.dma_start(out=VHt[kv][:],
                           in_=vtok_h[kv].rearrange("(c p) d -> p c d", p=128)
                           ).then_inc(s_kv.h, 16)
            sync.dma_start(out=VLt[kv][:],
                           in_=vtok_l[kv].rearrange("(c p) d -> p c d", p=128)
                           ).then_inc(s_kv.h, 16)
    em("sync", f)

    last_exp_bank = {}
    last_pread_bank = {}
    last_ctx_bank = {}
    scores_last = {}
    sum_mark = {}
    recip_mark = {}
    evac_mark = {}
    cst_mark = {}
    rb_mark = {}
    gchunk = [0]

    for t in range(NQT):
        for kv in range(NKV_L):
            li = 2 * t + kv
            par = li % 2
            mq = s_qb[par].bump(32)
            qgate = scores_last.get(li - 2, 0)
            def f(sync, kv=kv, t=t, par=par, qgate=qgate, m_kv=m_kv):
                sync.wait_ge(s_kv.h, m_kv)
                if qgate:
                    sync.wait_ge(s_TB.h, qgate)
                srch = qhat_h[4 * kv:4 * kv + 4, :, 128 * t:128 * (t + 1)].transpose([1, 0, 2])
                srcl = qhat_l[4 * kv:4 * kv + 4, :, 128 * t:128 * (t + 1)].transpose([1, 0, 2])
                sync.dma_start(out=SQH[par][:], in_=srch).then_inc(s_qb[par].h, 16)
                sync.dma_start(out=SQL[par][:], in_=srcl).then_inc(s_qb[par].h, 16)
            em("sync", f)

            def emit_ctx(j, sb, m_pl, t=t, kv=kv, par=par, li=li):
                m_cx = s_TB.bump(1)
                ceg = evac_mark.get(li - 2, 0)
                def f(tensor, j=j, t=t, kv=kv, sb=sb, par=par, m_pl=m_pl, ceg=ceg):
                    tensor.wait_ge(s_BV.h, m_pl)
                    if j == 0 and ceg:
                        tensor.wait_ge(s_BV.h, ceg)
                    vh = VHt[kv][:, j, :]
                    vl = VLt[kv][:, j, :]
                    tensor.matmul(PS_C[par][:], vh, PHt[sb][:], start=(j == 0), stop=False)
                    tensor.matmul(PS_C[par][:], vh, PLt[sb][:], start=False, stop=False)
                    tensor.matmul(PS_C[par][:], vl, PHt[sb][:], start=False,
                                  stop=(j == t)).then_inc(s_TB.h, 1)
                em("tensor", f)
                last_ctx_bank[sb] = m_cx
                return m_cx

            pend_cx = []
            for j in range(t + 1):
                sb = gchunk[0] % NPB
                gchunk[0] += 1
                m_sc = s_TB.bump(1)
                eg = last_exp_bank.get(sb, 0)
                def f(tensor, j=j, kv=kv, sb=sb, par=par, mq=mq, eg=eg):
                    tensor.wait_ge(s_qb[par].h, mq)
                    if eg:
                        tensor.wait_ge(s_BA.h, eg)
                    kh = KH[kv][:, 128 * j:128 * (j + 1)]
                    kl = KL[kv][:, 128 * j:128 * (j + 1)]
                    tensor.matmul(PS_S[sb][:], kh, SQH[par][:], start=True, stop=False)
                    tensor.matmul(PS_S[sb][:], kh, SQL[par][:], start=False, stop=False)
                    tensor.matmul(PS_S[sb][:], kl, SQH[par][:], start=False,
                                  stop=True).then_inc(s_TB.h, 1)
                em("tensor", f)
                m_exp = s_BA.bump(1)
                pg = last_pread_bank.get(sb, 0)
                def f(scalar, sb=sb, m_sc=m_sc, pg=pg):
                    scalar.wait_ge(s_TB.h, m_sc)
                    if pg:
                        scalar.wait_ge(s_BV.h, pg)
                    scalar.activation(out=P32[sb][:], in_=PS_S[sb][:],
                                      func=AF.Exp).then_inc(s_BA.h, 1)
                em("scalar", f)
                last_exp_bank[sb] = m_exp
                m_msk = 0
                if j == t:
                    m_msk = s_BV.bump(1)
                    def f(vector, sb=sb, m_exp=m_exp, m_dmk=m_dmk):
                        vector.wait_ge(s_BA.h, m_exp)
                        vector.wait_ge(s_kv.h, m_dmk)
                        vector.tensor_mul(out=P32[sb][:], in0=P32[sb][:],
                                          in1=DMK[:]).then_inc(s_BV.h, 1)
                    em("vector", f)
                m_ph = s_BA.bump(1)
                cg = last_ctx_bank.get(sb, 0)
                def f(scalar, sb=sb, m_msk=m_msk, cg=cg):
                    if m_msk:
                        scalar.wait_ge(s_BV.h, m_msk)
                    if cg:
                        scalar.wait_ge(s_TB.h, cg)
                    scalar.activation(out=PHt[sb][:], in_=P32[sb][:],
                                      func=AF.Copy).then_inc(s_BA.h, 1)
                em("scalar", f)
                m_pl = s_BV.bump(2)
                sg = sum_mark.get(li - 2, 0)
                def f(vector, sb=sb, j=j, par=par, m_ph=m_ph, sg=sg):
                    vector.wait_ge(s_BA.h, m_ph)
                    vector.scalar_tensor_tensor(out=PLt[sb][:], in0=P32[sb][:],
                                                scalar=1.0, in1=PHt[sb][:],
                                                op0=ALU.mult,
                                                op1=ALU.subtract).then_inc(s_BV.h, 1)
                    if j == 0:
                        if sg:
                            vector.wait_ge(s_TB.h, sg)
                        vector.tensor_copy(out=SACC[par][:],
                                           in_=P32[sb][:]).then_inc(s_BV.h, 1)
                    else:
                        vector.tensor_add(out=SACC[par][:], in0=SACC[par][:],
                                          in1=P32[sb][:]).then_inc(s_BV.h, 1)
                em("vector", f)
                last_pread_bank[sb] = m_pl
                if len(pend_cx) >= 2:
                    emit_ctx(*pend_cx.pop(0))
                pend_cx.append((j, sb, m_pl))
                if j == t:
                    scores_last[li] = m_sc
                    final_pl = m_pl
            while pend_cx:
                final_cx = emit_ctx(*pend_cx.pop(0))
            m_sum = s_TB.bump(1)
            rg = recip_mark.get(li - 2, 0)
            def f(tensor, par=par, final_pl=final_pl, rg=rg, m_idt=m_idt):
                tensor.wait_ge(s_BV.h, final_pl)
                tensor.wait_ge(s_ct.h, m_idt)
                if rg:
                    tensor.wait_ge(s_BV.h, rg)
                tensor.matmul(PS_SUM[par][:], ONE1[:], SACC[par][:], start=True,
                              stop=True).then_inc(s_TB.h, 1)
            em("tensor", f)
            sum_mark[li] = m_sum
            m_rr = s_BV.bump(1)
            rbg = rb_mark.get(li - 2, 0)
            def f(vector, par=par, m_sum=m_sum, rbg=rbg):
                vector.wait_ge(s_TB.h, m_sum)
                if rbg:
                    vector.wait_ge(s_rb.h, rbg)
                vector.reciprocal(out=RRW[par][:], in_=PS_SUM[par][:]).then_inc(s_BV.h, 1)
            em("vector", f)
            recip_mark[li] = m_rr
            m_rb1 = s_rb.bump(16)
            m_rb2 = s_rb.bump(16)
            evg = evac_mark.get(li - 2, 0)
            def f(gpsimd, kv=kv, t=t, par=par, m_rr=m_rr, m_rb1=m_rb1, evg=evg):
                gpsimd.wait_ge(s_BV.h, m_rr)
                slot = kv * NQT + t
                gpsimd.dma_start(out=rdb[slot:slot + 1, :],
                                 in_=RRW[par][:]).then_inc(s_rb.h, 16)
                gpsimd.wait_ge(s_rb.h, m_rb1)
                if evg:
                    gpsimd.wait_ge(s_BV.h, evg)
                gpsimd.dma_start(out=RBCB[par][:],
                                 in_=rdb[slot:slot + 1, :].to_broadcast([128, 512])
                                 ).then_inc(s_rb.h, 16)
            em("gpsimd", f)
            rb_mark[li] = m_rb2
            m_ev = s_BV.bump(1)
            csg = cst_mark.get(li - 2, 0)
            half = t // (NQT // 2)
            def f(vector, par=par, m_rb2=m_rb2, final_cx=final_cx, csg=csg, half=half):
                vector.wait_ge(s_rb.h, m_rb2)
                vector.wait_ge(s_TB.h, final_cx)
                if csg:
                    vector.wait_ge(s_cst[csg[0]].h, csg[1])
                vector.tensor_mul(out=CTXS[par][:], in0=PS_C[par][:],
                                  in1=RBCB[par][:]).then_inc(s_BV.h, 1)
            em("vector", f)
            evac_mark[li] = m_ev
            m_cst = s_cst[half].bump(16)
            def f(gpsimd, kv=kv, t=t, par=par, m_ev=m_ev, half=half):
                gpsimd.wait_ge(s_BV.h, m_ev)
                agt = ag_in_h[half]
                tloc = t - half * (NQT // 2)
                dst = agt[512 * kv:512 * (kv + 1), 128 * tloc:128 * (tloc + 1)]
                dst = dst.rearrange("(j d) q -> d j q", j=4)
                src = CTXS[par][:].rearrange("d (j q) -> d j q", j=4)
                gpsimd.dma_start(out=dst, in_=src).then_inc(s_cst[half].h, 16)
            em("gpsimd", f)
            cst_mark[li] = (half, m_cst)
        if t == NQT // 2 - 1 or t == NQT - 1:
            half = t // (NQT // 2)
            tot = s_cst[half].n
            s_cc.bump(1)
            def f(gpsimd, half=half, tot=tot):
                gpsimd.wait_ge(s_cst[half].h, tot)
                gpsimd.collective_compute("AllGather", ALU.bypass,
                                          replica_groups=[[0, 1, 2, 3], [4, 5, 6, 7]],
                                          ins=[ag_in_h[half].opt()],
                                          outs=[ag_out_h[half].opt()]
                                          ).then_inc(s_cc.h, 1)
            em("gpsimd", f)

    phaseB_dve = s_BV.n
    phaseB_cst = s_cst[1].n
    ctxB.close()
    ctxC = ExitStack()
    sbufC = lambda n, sh, dt: ctxC.enter_context(nc.sbuf_tensor(n, sh, dt))
    psumC = lambda n, sh: ctxC.enter_context(nc.psum_tensor(n, sh, F32))
    CXH = sbufC("CXH", [128, NFC, 512], BF16)
    CXL = sbufC("CXL", [128, NFC, 512], BF16)
    CXF = [sbufC(f"CXF{i}", [128, 512], F32) for i in range(4)]
    WOF = [sbufC(f"WOF{i}", [128, 512], F32) for i in range(4)]
    WOH = [sbufC(f"WOH{i}", [128, 512], BF16) for i in range(4)]
    WOL = [sbufC(f"WOL{i}", [128, 512], BF16) for i in range(4)]
    LR = [sbufC(f"LR{e}", [1, 512], F32) for e in range(4)]
    RK = [sbufC(f"RK{e}", [1, 512], F32) for e in range(4)]
    CMPt = sbufC("CMPt", [1, 512], F32)
    EEX = [sbufC(f"EEX{e}", [1, 512], F32) for e in range(4)]
    SUMG = sbufC("SUMG", [1, 512], F32)
    RGI = sbufC("RGI", [1, 512], F32)
    GEo = sbufC("GEo", [1, 512], F32)
    GBC = sbufC("GBC", [64, 512], F32)
    MIDG = sbufC("MIDG", [64, 512], F32)
    MGH = sbufC("MGH", [64, 512], BF16)
    MGL = sbufC("MGL", [64, 512], BF16)
    WBF = sbufC("WBF", [64, 1024], F32)
    WBH = sbufC("WBH", [64, 1024], BF16)
    WBL = sbufC("WBL", [64, 1024], BF16)
    XC = [sbufC(f"XC{i}", [128, 512], F32) for i in range(4)]
    YS = [sbufC(f"YS{i}", [128, 512], F32) for i in range(4)]
    PC = [psumC(f"PC{i}", [128, 512]) for i in range(8)]
    s_cx = [sem(f"s_cx{i}") for i in range(4)]
    s_wo = [sem(f"s_wo{i}") for i in range(4)]
    s_xc = [sem(f"s_xc{i}") for i in range(4)]
    s_CA = sem("s_CA")
    s_CV = sem("s_CV")
    s_TC = sem("s_TC")
    s_yst = sem("s_yst")
    s_gd = sem("s_gd")

    s_wb = sem("s_wb")
    m_wb_ld = s_wb.bump(16)
    def f(sync, m=m_wb_ld, pbc=phaseB_cst):
        sync.wait_ge(s_cst[1].h, pbc)
        sync.dma_start(out=WBF[:], in_=wbs).then_inc(s_wb.h, 16)
    em("sync", f)
    m_wbh = s_CA.bump(1)
    def f(scalar, m=m_wb_ld):
        scalar.wait_ge(s_wb.h, m)
        scalar.activation(out=WBH[:], in_=WBF[:], func=AF.Copy).then_inc(s_CA.h, 1)
    em("scalar", f)
    m_wbl = s_CV.bump(1)
    def f(vector, m=m_wbh):
        vector.wait_ge(s_CA.h, m)
        vector.tensor_tensor(out=WBL[:], in0=WBF[:], in1=WBH[:],
                             op=ALU.subtract).then_inc(s_CV.h, 1)
    em("vector", f)

    cx_marks = {}
    pe_fc_mark = {}
    pe_stop = {}
    ys_mark = {}
    yst_mark = {}
    xc_last = {}
    ys_last = {}
    midg_done = {}
    GE = [sbufC(f"GE{e}", [1, 512], F32) for e in range(4)]

    def emit_gates(cb):
        c0 = cb * 512
        g_ld = s_gd.bump(16 * 4)
        mg_prev = midg_done.get(cb - 1, 0)
        def f(gpsimd, c0=c0, pam=phaseA_midl, mg_prev=mg_prev, pbc=phaseB_cst):
            gpsimd.wait_ge(s_A.h, pam)
            gpsimd.wait_ge(s_cst[1].h, pbc)
            if mg_prev:
                gpsimd.wait_ge(s_CV.h, mg_prev)
            for e in range(4):
                gpsimd.dma_start(out=LR[e][:],
                                 in_=MIDL[64 + e:65 + e, c0:c0 + 512]).then_inc(s_gd.h, 16)
        em("gpsimd", f)
        m_exp4 = s_CA.bump(4)
        def f(scalar, g_ld=g_ld):
            scalar.wait_ge(s_gd.h, g_ld)
            for e in range(4):
                scalar.activation(out=EEX[e][:], in_=LR[e][:],
                                  func=AF.Exp).then_inc(s_CA.h, 1)
        em("scalar", f)
        m_ge = s_CV.bump(1)
        def f(vector, g_ld=g_ld, m_exp4=m_exp4):
            vector.wait_ge(s_gd.h, g_ld)
            vector.wait_ge(s_CA.h, m_exp4)
            for e in range(4):
                vector.memset(RK[e][:], 0.0)
            for e in range(4):
                for ep in range(4):
                    if ep == e:
                        continue
                    op = ALU.is_ge if ep < e else ALU.is_gt
                    vector.tensor_tensor(out=CMPt[:], in0=LR[ep][:], in1=LR[e][:], op=op)
                    vector.tensor_add(out=RK[e][:], in0=RK[e][:], in1=CMPt[:])
            for e in range(4):
                vector.tensor_scalar(out=RK[e][:], in0=RK[e][:], scalar1=2.0,
                                     scalar2=None, op0=ALU.is_lt)
                vector.tensor_mul(out=RK[e][:], in0=RK[e][:], in1=EEX[e][:])
            vector.tensor_add(out=SUMG[:], in0=RK[0][:], in1=RK[1][:])
            vector.tensor_add(out=SUMG[:], in0=SUMG[:], in1=RK[2][:])
            vector.tensor_add(out=SUMG[:], in0=SUMG[:], in1=RK[3][:])
            vector.reciprocal(out=RGI[:], in_=SUMG[:])
            for e in range(4):
                vector.tensor_mul(out=GE[e][:], in0=RK[e][:], in1=RGI[:])
            vector.engine_nop().then_inc(s_CV.h, 1)
        em("vector", f)
        g_st = s_gd.bump(16 * 4)
        g_bc = s_gd.bump(16 * 4)
        def f(gpsimd, cb=cb, c0=c0, m_ge=m_ge, g_st=g_st):
            gpsimd.wait_ge(s_CV.h, m_ge)
            for e in range(4):
                gpsimd.dma_start(out=gdram[e, c0:c0 + 512],
                                 in_=GE[e][:]).then_inc(s_gd.h, 16)
            gpsimd.wait_ge(s_gd.h, g_st)
            for e in range(4):
                gpsimd.dma_start(out=GBC[16 * e:16 * (e + 1), :],
                                 in_=gdram[e:e + 1, c0:c0 + 512].to_broadcast([16, 512])
                                 ).then_inc(s_gd.h, 16)
        em("gpsimd", f)
        m_midg = s_CV.bump(1)
        def f(vector, c0=c0, g_bc=g_bc):
            vector.wait_ge(s_gd.h, g_bc)
            vector.tensor_mul(out=MIDG[:], in0=MIDL[0:64, c0:c0 + 512],
                              in1=GBC[:]).then_inc(s_CV.h, 1)
        em("vector", f)
        m_mgh = s_CA.bump(1)
        mgh_gate = pe_stop.get((cb - 1, 1), 0)
        def f(scalar, m_midg=m_midg, mgh_gate=mgh_gate):
            scalar.wait_ge(s_CV.h, m_midg)
            if mgh_gate:
                scalar.wait_ge(s_TC.h, mgh_gate)
            scalar.activation(out=MGH[:], in_=MIDG[:], func=AF.Copy,
                              scale=LORA).then_inc(s_CA.h, 1)
        em("scalar", f)
        m_mgl = s_CV.bump(1)
        def f(vector, m_mgh=m_mgh):
            vector.wait_ge(s_CA.h, m_mgh)
            vector.scalar_tensor_tensor(out=MGL[:], in0=MIDG[:], scalar=LORA,
                                        in1=MGH[:], op0=ALU.mult,
                                        op1=ALU.subtract).then_inc(s_CV.h, 1)
        em("vector", f)
        midg_done[cb] = m_mgl
        return m_mgh, m_mgl

    cxf_free = {}
    def emit_cx_chunk(cb, fc):
        bi = fc % 4
        m_ld = s_cx[bi].bump(16)
        gate = pe_fc_mark.get((cb - 1, 1, fc), 0)
        stg = cxf_free.get(bi, 0)
        half = cb // 4
        cbl = cb - half * 4
        def f(sync, cb=cb, fc=fc, bi=bi, gate=gate, stg=stg, half=half, cbl=cbl,
              pbc=(phaseB_cst if (cb == 0 and fc < 4) else 0)):
            if pbc:
                sync.wait_ge(s_cst[1].h, pbc)
            sync.wait_ge(s_cc.h, half + 1)
            if gate:
                sync.wait_ge(s_TC.h, gate)
            if stg:
                sync.wait_ge(s_CV.h, stg)
            sync.dma_start(out=CXF[bi][:],
                           in_=ag_out_h[half][128 * fc:128 * (fc + 1),
                                              512 * cbl:512 * (cbl + 1)]
                           ).then_inc(s_cx[bi].h, 16)
        em("sync", f)
        m_hi = s_CA.bump(1)
        def f(scalar, fc=fc, bi=bi, m_ld=m_ld):
            scalar.wait_ge(s_cx[bi].h, m_ld)
            scalar.activation(out=CXH[:, fc, :], in_=CXF[bi][:],
                              func=AF.Copy).then_inc(s_CA.h, 1)
        em("scalar", f)
        m_lo = s_CV.bump(1)
        def f(vector, fc=fc, bi=bi, m_hi=m_hi):
            vector.wait_ge(s_CA.h, m_hi)
            vector.tensor_tensor(out=CXL[:, fc, :], in0=CXF[bi][:],
                                 in1=CXH[:, fc, :], op=ALU.subtract).then_inc(s_CV.h, 1)
        em("vector", f)
        cx_marks[(cb, fc)] = (m_hi, m_lo)
        cxf_free[bi] = m_lo

    wo_free = {}
    for cb in range(8):
        mgates = emit_gates(cb)
        for fc in range(NFC):
            emit_cx_chunk(cb, fc)
        for oc2 in range(2):
            gpar = (cb * 2 + oc2) % 2
            for fc in range(NFC):
                bi = fc % 4
                m_wld = s_wo[bi].bump(16)
                frw = wo_free.get(bi, 0)
                def f(sync, cb=cb, oc2=oc2, fc=fc, bi=bi, frw=frw,
                      pbc=(phaseB_cst if (cb == 0 and oc2 == 0 and fc < 4) else 0)):
                    if pbc:
                        sync.wait_ge(s_cst[1].h, pbc)
                    if frw:
                        sync.wait_ge(s_TC.h, frw)
                    sync.dma_start(out=WOF[bi][:],
                                   in_=wo[128 * fc:128 * (fc + 1),
                                          512 * oc2:512 * (oc2 + 1)]
                                   ).then_inc(s_wo[bi].h, 16)
                em("sync", f)
                m_woh = s_CA.bump(1)
                def f(scalar, bi=bi, m_wld=m_wld):
                    scalar.wait_ge(s_wo[bi].h, m_wld)
                    scalar.activation(out=WOH[bi][:], in_=WOF[bi][:],
                                      func=AF.Copy).then_inc(s_CA.h, 1)
                em("scalar", f)
                m_wol = s_CV.bump(1)
                def f(vector, bi=bi, m_woh=m_woh):
                    vector.wait_ge(s_CA.h, m_woh)
                    vector.tensor_tensor(out=WOL[bi][:], in0=WOF[bi][:],
                                         in1=WOH[bi][:],
                                         op=ALU.subtract).then_inc(s_CV.h, 1)
                em("vector", f)
                mh, ml = cx_marks[(cb, fc)]
                m_pe = s_TC.bump(1)
                bankg = []
                if fc == 0:
                    for ct in range(4):
                        g = ys_mark.get((cb - 1, oc2, ct), 0)
                        if g:
                            bankg.append(g)
                def f(tensor, cb=cb, oc2=oc2, fc=fc, bi=bi, gpar=gpar,
                      m_wol=m_wol, mh=mh, ml=ml, bankg=tuple(bankg),
                      pbd=(phaseB_dve if (cb == 0 and oc2 == 0 and fc == 0) else 0)):
                    if pbd:
                        tensor.wait_ge(s_BV.h, pbd)
                    tensor.wait_ge(s_CV.h, m_wol)
                    tensor.wait_ge(s_CA.h, mh)
                    tensor.wait_ge(s_CV.h, ml)
                    for g in bankg:
                        tensor.wait_ge(s_CV.h, g)
                    li = None
                    for ct in range(4):
                        pb = PC[4 * gpar + ct]
                        sh = CXH[:, fc, 128 * ct:128 * (ct + 1)]
                        sl = CXL[:, fc, 128 * ct:128 * (ct + 1)]
                        tensor.matmul(pb[:], sh, WOH[bi][:], start=(fc == 0), stop=False)
                        tensor.matmul(pb[:], sh, WOL[bi][:], start=False, stop=False)
                        li = tensor.matmul(pb[:], sl, WOH[bi][:], start=False, stop=False)
                    li.then_inc(s_TC.h, 1)
                em("tensor", f)
                pe_fc_mark[(cb, oc2, fc)] = m_pe
                wo_free[bi] = m_pe
            m_ffn = s_TC.bump(1)
            def f(tensor, cb=cb, oc2=oc2, gpar=gpar, mgates=mgates,
                  m_wbh=m_wbh, m_wbl=m_wbl):
                tensor.wait_ge(s_CA.h, mgates[0])
                tensor.wait_ge(s_CV.h, mgates[1])
                tensor.wait_ge(s_CA.h, m_wbh)
                tensor.wait_ge(s_CV.h, m_wbl)
                li = None
                for ct in range(4):
                    pb = PC[4 * gpar + ct]
                    mh_ = MGH[:, 128 * ct:128 * (ct + 1)]
                    ml_ = MGL[:, 128 * ct:128 * (ct + 1)]
                    wbh_ = WBH[:, 512 * oc2:512 * (oc2 + 1)]
                    wbl_ = WBL[:, 512 * oc2:512 * (oc2 + 1)]
                    tensor.matmul(pb[:], mh_, wbh_, start=False, stop=False)
                    tensor.matmul(pb[:], mh_, wbl_, start=False, stop=False)
                    li = tensor.matmul(pb[:], ml_, wbh_, start=False, stop=True)
                li.then_inc(s_TC.h, 1)
            em("tensor", f)
            pe_stop[(cb, oc2)] = m_ffn
            for ct in range(4):
                xbi = ct
                m_xld = s_xc[xbi].bump(16)
                yg = xc_last.get(ct, 0)
                def f(sync, cb=cb, oc2=oc2, ct=ct, xbi=xbi, yg=yg,
                      pbc=(phaseB_cst if (cb == 0 and oc2 == 0) else 0)):
                    if pbc:
                        sync.wait_ge(s_cst[1].h, pbc)
                    if yg:
                        sync.wait_ge(s_CV.h, yg)
                    t0 = 512 * cb + 128 * ct
                    sync.dma_start(out=XC[xbi][:],
                                   in_=xcols[t0:t0 + 128,
                                             512 * oc2:512 * (oc2 + 1)]
                                   ).then_inc(s_xc[xbi].h, 16)
                em("sync", f)
                m_ys = s_CV.bump(1)
                ysg = ys_last.get(ct, 0)
                def f(vector, ct=ct, gpar=gpar, xbi=xbi, m_xld=m_xld,
                      m_ffn=m_ffn, ysg=ysg):
                    vector.wait_ge(s_TC.h, m_ffn)
                    vector.wait_ge(s_xc[xbi].h, m_xld)
                    if ysg:
                        vector.wait_ge(s_yst.h, ysg)
                    vector.tensor_add(out=YS[ct][:], in0=PC[4 * gpar + ct][:],
                                      in1=XC[xbi][:]).then_inc(s_CV.h, 1)
                em("vector", f)
                ys_mark[(cb, oc2, ct)] = m_ys
                xc_last[ct] = m_ys
                m_yst = s_yst.bump(16)
                def f(gpsimd, cb=cb, oc2=oc2, ct=ct, m_ys=m_ys):
                    gpsimd.wait_ge(s_CV.h, m_ys)
                    t0 = 512 * cb + 128 * ct
                    gpsimd.dma_start(out=y[t0:t0 + 128, 512 * oc2:512 * (oc2 + 1)],
                                     in_=YS[ct][:]).then_inc(s_yst.h, 16)
                em("gpsimd", f)
                yst_mark[(cb, oc2, ct)] = m_yst
                ys_last[ct] = m_yst

    y_total = s_yst.n
    def f(gpsimd, tot=y_total):
        gpsimd.wait_ge(s_yst.h, tot)
    em("gpsimd", f)

    if DBG:
        dbg_outs = {}
        for nm, src, sh, dt in [("dbg_qh", qhat_h, [NH_L, 128, T], BF16),
                                ("dbg_ql", qhat_l, [NH_L, 128, T], BF16),
                                ("dbg_kh", khat_h, [NKV_L, 128, T], BF16),
                                ("dbg_kl", khat_l, [NKV_L, 128, T], BF16),
                                ("dbg_vh", vtok_h, [NKV_L, T, 128], BF16),
                                ("dbg_vl", vtok_l, [NKV_L, T, 128], BF16)]:
            dbg_outs[nm] = (dram(nm, sh, dt, O), src)
        dbg_midl = dram("dbg_midl", [68, T], F32, O)
        dbg_rd = dram("dbg_rd", [NTT, 2, TTW], F32, O)
        n_dbg = s_qst.bump(16 * (len(dbg_outs) + 2))
        def f(gpsimd, tot=phaseA_stores, midl_m=phaseA_midl, n_dbg=n_dbg):
            gpsimd.wait_ge(s_qst.h, tot)
            gpsimd.wait_ge(s_A.h, midl_m)
            for nm, (dst, src) in dbg_outs.items():
                gpsimd.dma_start(out=dst, in_=src).then_inc(s_qst.h, 16)
            gpsimd.dma_start(out=dbg_midl, in_=MIDL[:]).then_inc(s_qst.h, 16)
            gpsimd.dma_start(out=dbg_rd, in_=rd).then_inc(s_qst.h, 16)
            gpsimd.wait_ge(s_qst.h, n_dbg)
        em("gpsimd", f)

    with ctx:
        with nc.Block() as block:
            @block.sync
            def _(sync):
                for f in ops["sync"]:
                    f(sync)

            @block.scalar
            def _(scalar):
                for f in ops["scalar"]:
                    f(scalar)

            @block.vector
            def _(vector):
                for f in ops["vector"]:
                    f(vector)

            @block.tensor
            def _(tensor):
                for f in ops["tensor"]:
                    f(tensor)

            @block.gpsimd
            def _(gpsimd):
                for f in ops["gpsimd"]:
                    f(gpsimd)
        ctxC.close()
    return nc


def _prep_inputs(inputs):
    x = np.asarray(inputs["L0_attn_x"], dtype=np.float32)
    Wq = np.asarray(inputs["L0_attn_Wq"], dtype=np.float32)
    Wk = np.asarray(inputs["L0_attn_Wk"], dtype=np.float32)
    Wv = np.asarray(inputs["L0_attn_Wv"], dtype=np.float32)
    Wo = np.asarray(inputs["L0_attn_Wo"], dtype=np.float32)
    rw = np.asarray(inputs["L0_ffn_router_w"], dtype=np.float32)
    A = np.asarray(inputs["L0_ffn_A"], dtype=np.float32)
    Bm = np.asarray(inputs["L0_ffn_B"], dtype=np.float32)

    cos, sin = _rope_tables()
    cosT2 = np.ascontiguousarray(np.concatenate([cos.T, cos.T], axis=0))
    sinN2 = np.ascontiguousarray(np.concatenate([-sin.T, sin.T], axis=0))
    kk = np.arange(128)[:, None]
    qq = np.arange(128)[None, :]
    blk = (kk <= qq).astype(np.float32)
    dm = np.ascontiguousarray(np.tile(blk, (1, 4)))
    ones = np.ones((128, 1), dtype=np.float32)
    ident = np.eye(128, dtype=np.float32)
    wf_ = np.concatenate([A.reshape(64, D).T, rw.T], axis=1).astype(np.float32)
    wf_ = np.ascontiguousarray(wf_)
    bstack = np.ascontiguousarray(Bm.transpose(0, 2, 1).reshape(64, D))

    in_maps = []
    for c in range(8):
        b, g = c // 4, c % 4
        xb = x[b]
        in_maps.append({
            "xT": np.ascontiguousarray(xb.T),
            "xtok": np.ascontiguousarray(xb),
            "xcols": np.ascontiguousarray(xb[:, 1024 * g:1024 * (g + 1)]),
            "wq": np.ascontiguousarray(Wq[:, 1024 * g:1024 * (g + 1)]),
            "wkv": np.ascontiguousarray(
                np.concatenate([Wk[:, 256 * g:256 * (g + 1)],
                                Wv[:, 256 * g:256 * (g + 1)]], axis=1)),
            "wo": np.ascontiguousarray(Wo[:, 1024 * g:1024 * (g + 1)]),
            "wf": wf_,
            "wbs": np.ascontiguousarray(bstack[:, 1024 * g:1024 * (g + 1)]),
            "cosT": cosT2,
            "sinN": sinN2,
            "dmask": dm,
            "ones128": ones,
            "epsc": np.full((128, 1), EPS, dtype=np.float32),
            "ident": ident,
        })
    return in_maps


def kernel(**inputs):
    in_maps = _prep_inputs(inputs)
    nc = build_nc()
    res = run_bass_kernel_spmd(nc, in_maps, core_ids=list(range(8)))
    out = np.empty((B, T, D), dtype=np.float32)
    for c in range(8):
        b, g = c // 4, c % 4
        out[b, :, 1024 * g:1024 * (g + 1)] = np.asarray(res.results[c]["y"])
    globals()["_last_res"] = res
    return out


